# version 07
# baseline (speedup 1.0000x reference)
import numpy as np
import ml_dtypes
import contextlib
import concourse.bass as bass
import concourse.mybir as mybir
from concourse.bass import IndirectOffsetOnAxis
from concourse.bass_utils import run_bass_kernel_spmd

F32 = mybir.dt.float32
BF16 = mybir.dt.bfloat16
I32 = mybir.dt.int32
AF = mybir.ActivationFunctionType
ALU = mybir.AluOpType
AXX = mybir.AxisListType.X

NCORE = 8
NTOK = 4096
NT = 32
D = 1024
DC = 8
S = 64
E = 8
H = 4096
HC = 32
Q = 1280
RB = 10
HQ = 640
CAP = 8602
EPS = 1e-5

ENGINES = ("sync", "scalar", "vector", "tensor", "gpsimd")

DEBUG = False
DEBUG2 = False


class TSem:
    __slots__ = ("h", "v", "name")

    def __init__(self, h, name):
        self.h = h
        self.v = 0
        self.name = name

    def bump(self, n):
        self.v += n
        return (self.h, self.v)

    def tok(self):
        return (self.h, self.v)


def build_nc():
    nc = bass.Bass("TRN2", target_bir_lowering=False)

    x_in = nc.dram_tensor("x", [NTOK, D], F32, kind="ExternalInput").ap()
    w1_in = nc.dram_tensor("w1", [E, D, H], F32, kind="ExternalInput").ap()
    w2_in = nc.dram_tensor("w2", [E, H, D], F32, kind="ExternalInput").ap()
    wg_in = nc.dram_tensor("wg", [D, E], F32, kind="ExternalInput").ap()
    bwt_in = nc.dram_tensor("bwt", [D, S], F32, kind="ExternalInput").ap()
    cwt_in = nc.dram_tensor("cwt", [S, D], F32, kind="ExternalInput").ap()
    gwt_in = nc.dram_tensor("gwt", [D, D], F32, kind="ExternalInput").ap()
    gateb_in = nc.dram_tensor("gateb", [1, D], F32, kind="ExternalInput").ap()
    ldec_in = nc.dram_tensor("ldec", [1, D], F32, kind="ExternalInput").ap()
    b1t_in = nc.dram_tensor("b1t", [128, E * HC], F32, kind="ExternalInput").ap()
    b2_in = nc.dram_tensor("b2", [1, E * D], F32, kind="ExternalInput").ap()
    trikf_in = nc.dram_tensor("trikf", [128, 128], F32, kind="ExternalInput").ap()
    trikb_in = nc.dram_tensor("trikb", [128, 128], BF16, kind="ExternalInput").ap()
    o128f_in = nc.dram_tensor("o128f", [128, 1], F32, kind="ExternalInput").ap()
    o128b_in = nc.dram_tensor("o128b", [128, 1], BF16, kind="ExternalInput").ap()
    o1x128f_in = nc.dram_tensor("o1x128f", [1, 128], F32, kind="ExternalInput").ap()
    o1x128b_in = nc.dram_tensor("o1x128b", [1, 128], BF16, kind="ExternalInput").ap()
    eqrow_in = nc.dram_tensor("eqrow", [128, E], F32, kind="ExternalInput").ap()
    e10row_in = nc.dram_tensor("e10row", [128, E], F32, kind="ExternalInput").ap()
    tokid_in = nc.dram_tensor("tokid", [128, NT], I32, kind="ExternalInput").ap()
    iotag_in = nc.dram_tensor("iotag", [1, 128], F32, kind="ExternalInput").ap()
    rmask_in = nc.dram_tensor("rmask", [8, 1], F32, kind="ExternalInput").ap()
    cmask_in = nc.dram_tensor("cmask", [8, 1], F32, kind="ExternalInput").ap()
    out_io = nc.dram_tensor("out", [NTOK, D], F32, kind="ExternalOutput").ap()

    dbg = {}
    if DEBUG2:
        dbg["pyd2"] = nc.dram_tensor("dbg_pyd2", [E * Q, D], BF16, kind="ExternalOutput").ap()
        dbg["xnd2"] = nc.dram_tensor("dbg_xnd2", [NTOK, D], BF16, kind="ExternalOutput").ap()
        dbg["tidd2"] = nc.dram_tensor("dbg_tidd2", [1, 128 * E * RB], I32, kind="ExternalOutput").ap()
        dbg["carrbd2"] = nc.dram_tensor("dbg_carrbd2", [NT, D], BF16, kind="ExternalOutput").ap()
        dbg["packT2"] = nc.dram_tensor("dbg_packT2", [E * 128, DC * Q], BF16, kind="ExternalOutput").ap()
        dbg["hT2"] = nc.dram_tensor("dbg_hT2", [E * 2 * 128, HC * HQ], BF16, kind="ExternalOutput").ap()
    if DEBUG:
        dbg["logits"] = nc.dram_tensor("dbg_logits", [128, NT * E], F32, kind="ExternalOutput").ap()
        dbg["carry"] = nc.dram_tensor("dbg_carry", [1, D], F32, kind="ExternalOutput").ap()
        dbg["rank0"] = nc.dram_tensor("dbg_rank0", [128, NT], F32, kind="ExternalOutput").ap()
        dbg["rank1"] = nc.dram_tensor("dbg_rank1", [128, NT], F32, kind="ExternalOutput").ap()
        dbg["gate0"] = nc.dram_tensor("dbg_gate0", [128, NT], F32, kind="ExternalOutput").ap()
        dbg["gate1"] = nc.dram_tensor("dbg_gate1", [128, NT], F32, kind="ExternalOutput").ap()
        dbg["pdest0"] = nc.dram_tensor("dbg_pdest0", [128, NT], I32, kind="ExternalOutput").ap()
        dbg["pdest1"] = nc.dram_tensor("dbg_pdest1", [128, NT], I32, kind="ExternalOutput").ap()
        dbg["tidT"] = nc.dram_tensor("dbg_tidT", [128, E * RB], I32, kind="ExternalOutput").ap()
        dbg["gateT"] = nc.dram_tensor("dbg_gateT", [128, E * RB], F32, kind="ExternalOutput").ap()
        dbg["s"] = nc.dram_tensor("dbg_s", [NTOK, D], F32, kind="ExternalOutput").ap()
        dbg["g"] = nc.dram_tensor("dbg_g", [NTOK, D], F32, kind="ExternalOutput").ap()

    xnd = nc.dram_tensor("xnd", [NTOK, D], BF16).ap()
    packd = nc.dram_tensor("packd", [Q, D], BF16).ap()
    pyd = nc.dram_tensor("pyd", [E * Q, D], BF16).ap()
    tidd = nc.dram_tensor("tidd", [1, 128 * E * RB], I32).ap()
    gated = nc.dram_tensor("gated", [1, 128 * E * RB], F32).ap()
    tidd2d = tidd.rearrange("o (p c) -> (o p) c", p=128)
    gated2d = gated.rearrange("o (p c) -> (o p) c", p=128)
    carrd = nc.dram_tensor("carrd", [NT, D], F32).ap()
    carrbd = nc.dram_tensor("carrbd", [NT, D], BF16).ap()
    ccin = nc.dram_tensor("ccin", [1, 1040], F32).ap()
    ccout = nc.dram_tensor("ccout", [8, 1040], F32).ap()

    prog = {k: [] for k in ENGINES}

    def emit(eng, fn):
        prog[eng].append(fn)

    stack = contextlib.ExitStack()

    def sem(name):
        return TSem(stack.enter_context(nc.semaphore(name)), name)

    qs = sem("qs")
    qc = sem("qc")
    qt = sem("qt")
    qgi = sem("qgi")
    qw1 = [sem("qw1a"), sem("qw1b")]
    qw2 = sem("qw2")
    qb2 = [sem("qb2a"), sem("qb2b")]
    qgw = sem("qgw")
    qg = sem("qg")
    vs = sem("vs")
    ss = sem("ss")
    ps = sem("ps")
    cc = sem("cc")
    ge = sem("ge")
    last_v = [None]
    last_s = [None]

    def dma(queue, qsem, out, in_, waits=(), **kw):
        tok = qsem.bump(16)
        wl = tuple(t for t in waits if t is not None)

        def f(e, out=out, in_=in_, wl=wl, kw=kw, h=qsem.h):
            for (sh, v) in wl:
                e.wait_ge(sh, v)
            e.dma_start(out=out, in_=in_, **kw).then_inc(h, 16)
        emit(queue, f)
        return tok

    def idma(out, out_off, in_, in_off, waits=()):
        tok = qgi.bump(16)
        wl = tuple(t for t in waits if t is not None)

        def f(e, out=out, oo=out_off, in_=in_, io=in_off, wl=wl, h=qgi.h):
            for (sh, v) in wl:
                e.wait_ge(sh, v)
            e.indirect_dma_start(out=out, out_offset=oo, in_=in_, in_offset=io).then_inc(h, 16)
        emit("gpsimd", f)
        return tok

    def op(eng, build, waits=()):
        esem = vs if eng == "vector" else ss
        last = last_v if eng == "vector" else last_s
        wl = [t for t in waits if t is not None]
        if last[0] is not None:
            wl.append(last[0])
        tok = esem.bump(1)
        wl = tuple(wl)

        def f(e, build=build, wl=wl, h=esem.h):
            for (sh, v) in wl:
                e.wait_ge(sh, v)
            build(e).then_inc(h, 1)
        emit(eng, f)
        last[0] = tok
        return tok

    def mmg(mms, waits=()):
        tok = ps.bump(1)
        wl = tuple(t for t in waits if t is not None)

        def f(e, mms=mms, wl=wl, h=ps.h):
            for (sh, v) in wl:
                e.wait_ge(sh, v)
            for (o, l, r, st, sp) in mms:
                ins = e.matmul(o, l, r, start=st, stop=sp)
            ins.then_inc(h, 1)
        emit("tensor", f)
        return tok

    logits_all = stack.enter_context(nc.sbuf_tensor("logits_all", [128, NT * E], F32))
    bu_sb = stack.enter_context(nc.sbuf_tensor("bu_sb", [64, NTOK], BF16))
    cwt_sb = stack.enter_context(nc.sbuf_tensor("cwt_sb", [64, D], BF16))
    ldlog_sb = stack.enter_context(nc.sbuf_tensor("ldlog_sb", [1, D], F32))
    trikf_sb = stack.enter_context(nc.sbuf_tensor("trikf_sb", [128, 128], F32))
    trikb_sb = stack.enter_context(nc.sbuf_tensor("trikb_sb", [128, 128], BF16))
    o128f_sb = stack.enter_context(nc.sbuf_tensor("o128f_sb", [128, 1], F32))
    o128b_sb = stack.enter_context(nc.sbuf_tensor("o128b_sb", [128, 1], BF16))
    o1x128f_sb = stack.enter_context(nc.sbuf_tensor("o1x128f_sb", [1, 128], F32))
    o1x128b_sb = stack.enter_context(nc.sbuf_tensor("o1x128b_sb", [1, 128], BF16))
    iotag_sb = stack.enter_context(nc.sbuf_tensor("iotag_sb", [1, 128], F32))
    pdest0_sb = stack.enter_context(nc.sbuf_tensor("pdest0_sb", [128, NT], I32))
    pdest1_sb = stack.enter_context(nc.sbuf_tensor("pdest1_sb", [128, NT], I32))
    crow_sb = stack.enter_context(nc.sbuf_tensor("crow_sb", [1, E], F32))
    tidt_sb = stack.enter_context(nc.sbuf_tensor("tidt_sb", [128, E * RB], I32))
    gatet_sb = stack.enter_context(nc.sbuf_tensor("gatet_sb", [128, E * RB], F32))
    stat_sb = stack.enter_context(nc.sbuf_tensor("stat_sb", [128, 16], F32))

    PSA = stack.enter_context(nc.psum_tensor("PSA", [128, 1024], F32))
    PSB = stack.enter_context(nc.psum_tensor("PSB", [128, 1024], F32))
    PSC = stack.enter_context(nc.psum_tensor("PSC", [128, 1024], F32))
    PSD = stack.enter_context(nc.psum_tensor("PSD", [128, 1024], F32))

    t_cwt = dma("gpsimd", qg, cwt_sb[:, :], cwt_in[:, :])
    t_ldec_sb = stack.enter_context(nc.sbuf_tensor("ldec_sb", [1, D], F32))
    ldec_sb = t_ldec_sb
    t_ldec = dma("sync", qs, ldec_sb[:, :], ldec_in[:, :])
    t_trikf = dma("sync", qs, trikf_sb[:, :], trikf_in[:, :])
    t_trikb = dma("sync", qs, trikb_sb[:, :], trikb_in[:, :])
    t_o128f = dma("sync", qs, o128f_sb[:, :], o128f_in[:, :])
    t_o128b = dma("sync", qs, o128b_sb[:, :], o128b_in[:, :])
    t_o1f = dma("sync", qs, o1x128f_sb[:, :], o1x128f_in[:, :])
    t_o1b = dma("sync", qs, o1x128b_sb[:, :], o1x128b_in[:, :])
    t_iog = dma("sync", qs, iotag_sb[:, :], iotag_in[:, :])

    t_sg0 = op("scalar", lambda e: e.activation(out=ldlog_sb[:, :], in_=ldec_sb[:, :], func=AF.Sigmoid),
               waits=[t_ldec])
    t_ldlog = op("scalar", lambda e: e.activation(out=ldlog_sb[:, :], in_=ldlog_sb[:, :], func=AF.Ln),
                 waits=[t_sg0])
    t_z3 = op("vector", lambda e: e.memset(carry_local[:, :], 0.0))

    a_ctx = contextlib.ExitStack()
    xnT = a_ctx.enter_context(nc.sbuf_tensor("xnT", [128, DC * NTOK], BF16))

    def xnT_tile(dc, i):
        return xnT[:, dc * NTOK + i * 128: dc * NTOK + (i + 1) * 128]

    if True:
        _a = a_ctx.enter_context
        bwt_sb = _a(nc.sbuf_tensor("bwt_sb", [128, DC * S], BF16))
        wg_sb = _a(nc.sbuf_tensor("wg_sb", [128, DC * E], BF16))
        xt0 = _a(nc.sbuf_tensor("xt0", [128, D], F32))
        xt1 = _a(nc.sbuf_tensor("xt1", [128, D], F32))
        xnb0 = _a(nc.sbuf_tensor("xnb0", [128, D], BF16))
        xnb1 = _a(nc.sbuf_tensor("xnb1", [128, D], BF16))
        pw_a = _a(nc.sbuf_tensor("pw_a", [128, D], F32))
        ld128_sb = _a(nc.sbuf_tensor("ld128_sb", [1, D], F32))
        d128_sb = _a(nc.sbuf_tensor("d128_sb", [128, D], F32))
        c_a = _a(nc.sbuf_tensor("c_a", [128, D], BF16))
        oh1_all = _a(nc.sbuf_tensor("oh1_all", [128, NT * E], F32))
        oh2_all = _a(nc.sbuf_tensor("oh2_all", [128, NT * E], F32))
        rank0_all = _a(nc.sbuf_tensor("rank0_all", [128, NT], F32))
        rank1_all = _a(nc.sbuf_tensor("rank1_all", [128, NT], F32))
        d12_all = _a(nc.sbuf_tensor("d12_all", [128, NT], F32))
        g1_all = _a(nc.sbuf_tensor("g1_all", [128, NT], F32))
        g2_all = _a(nc.sbuf_tensor("g2_all", [128, NT], F32))
        gate0_all = _a(nc.sbuf_tensor("gate0_all", [128, NT], F32))
        gate1_all = _a(nc.sbuf_tensor("gate1_all", [128, NT], F32))
        d20_all = _a(nc.sbuf_tensor("d20_all", [128, NT], I32))
        d21_all = _a(nc.sbuf_tensor("d21_all", [128, NT], I32))
        cnt_all = _a(nc.sbuf_tensor("cnt_all", [128, NT * E], F32))
        tmp256 = _a(nc.sbuf_tensor("tmp256", [128, NT * E], F32))
        t32a = _a(nc.sbuf_tensor("t32a", [128, NT], F32))
        t32b = _a(nc.sbuf_tensor("t32b", [128, NT], F32))
        t32c = _a(nc.sbuf_tensor("t32c", [128, NT], F32))
        i32a = _a(nc.sbuf_tensor("i32a", [128, NT], I32))
        i32b = _a(nc.sbuf_tensor("i32b", [128, NT], I32))
        i32c = _a(nc.sbuf_tensor("i32c", [128, NT], I32))
        srt_sb = _a(nc.sbuf_tensor("srt_sb", [128, E], F32))
        s01_sb = _a(nc.sbuf_tensor("s01_sb", [128, E], F32))
        eqrow_sb = _a(nc.sbuf_tensor("eqrow_sb", [128, E], F32))
        e10row_sb = _a(nc.sbuf_tensor("e10row_sb", [128, E], F32))
        tokid_sb = _a(nc.sbuf_tensor("tokid_sb", [128, NT], I32))
        rmask_sb = _a(nc.sbuf_tensor("rmask_sb", [8, 1], F32))
        cmask_sb = _a(nc.sbuf_tensor("cmask_sb", [8, 1], F32))
        offsrow_sb = _a(nc.sbuf_tensor("offsrow_sb", [1, E], F32))
        offsb_sb = _a(nc.sbuf_tensor("offsb_sb", [128, E], F32))
        pack_sb = _a(nc.sbuf_tensor("pack_sb", [1, 1040], F32))
        gath_sb = _a(nc.sbuf_tensor("gath_sb", [8, 1040], F32))
        carry_local = _a(nc.sbuf_tensor("carry_local", [1, D], F32))
        carry0_sb = _a(nc.sbuf_tensor("carry0_sb", [1, D], F32))
        colrow0 = _a(nc.sbuf_tensor("colrow0", [1, D], F32))
        colrow1 = _a(nc.sbuf_tensor("colrow1", [1, D], F32))
        carr32_sb = _a(nc.sbuf_tensor("carr32_sb", [32, D], F32))
        carrb32_sb = _a(nc.sbuf_tensor("carrb32_sb", [32, D], BF16))
        zeroi_sb = _a(nc.sbuf_tensor("zeroi_sb", [128, E * RB], I32))
        zerof_sb = _a(nc.sbuf_tensor("zerof_sb", [128, E * RB], F32))

        t_bwt = dma("gpsimd", qg, bwt_sb[:, :].rearrange("p (dc s) -> p dc s", dc=DC),
                     bwt_in.rearrange("(dc p) s -> p dc s", p=128))
        t_wg = dma("gpsimd", qg, wg_sb[:, :].rearrange("p (dc e) -> p dc e", dc=DC),
                    wg_in.rearrange("(dc p) e -> p dc e", p=128))
        t_cwt = t_bwt = t_wg = qg.tok()
        t_eq = dma("sync", qs, eqrow_sb[:, :], eqrow_in[:, :])
        t_e10 = dma("sync", qs, e10row_sb[:, :], e10row_in[:, :])
        t_tok = dma("sync", qs, tokid_sb[:, :], tokid_in[:, :])
        t_rm = dma("sync", qs, rmask_sb[:, :], rmask_in[:, :])
        t_cm = dma("sync", qs, cmask_sb[:, :], cmask_in[:, :])
        t_z1 = op("vector", lambda e: e.memset(zeroi_sb[:, :], 0))
        t_z2 = op("vector", lambda e: e.memset(zerof_sb[:, :], 0.0))
        t_tid0 = dma("sync", qs, tidd2d, zeroi_sb[:, :], waits=[t_z1])
        t_gat0 = dma("sync", qs, gated2d, zerof_sb[:, :], waits=[t_z2])

        xts = [xt0, xt1]
        xnbs = [xnb0, xnb1]
        xr = x_in.rearrange("(i p) d -> i p d", p=128)
        xndr = xnd.rearrange("(i p) d -> i p d", p=128)

        t_xload = [None] * NT
        t_xm = [None] * NT
        t_xnb = [None] * NT
        t_xnw = [None] * NT

        t_xload[0] = dma("sync", qs, xts[0][:, :], xr[0])
        if NT > 1:
            t_xload[1] = dma("sync", qs, xts[1][:, :], xr[1])
        for i in range(NT):
            b = i % 2
            t1 = op("vector", lambda e, b=b: e.bn_stats(stat_sb[:, 0:6], xts[b][:, 0:512]),
                    waits=[t_xload[i]])
            t2 = op("vector", lambda e, b=b: e.bn_stats(stat_sb[:, 6:12], xts[b][:, 512:1024]))
            t3 = op("vector", lambda e: e.bn_aggr(stat_sb[:, 12:14], stat_sb[:, 0:12]))
            t3b = op("vector", lambda e: e.tensor_scalar(stat_sb[:, 13:14], stat_sb[:, 13:14], float(EPS), None, ALU.add))
            t6 = op("scalar", lambda e: e.activation(out=stat_sb[:, 14:15], in_=stat_sb[:, 13:14], func=AF.Sqrt),
                    waits=[t3b])
            t7 = op("vector", lambda e: e.reciprocal(stat_sb[:, 14:15], stat_sb[:, 14:15]),
                    waits=[t6])
            t_xnb[i] = op("vector", lambda e, b=b: e.tensor_scalar(xnbs[b][:, :], xts[b][:, :],
                                                                   stat_sb[:, 12:13], stat_sb[:, 14:15],
                                                                   ALU.subtract, ALU.mult),
                          waits=[t_xnw[i - 2]] if i >= 2 else [])
            t_xm[i] = t_xnb[i]
            t_xnw[i] = dma("scalar", qc, xndr[i], xnbs[b][:, :], waits=[t_xnb[i]])
            if i + 2 < NT:
                t_xload[i + 2] = dma("sync", qs, xts[b][:, :], xr[i + 2], waits=[t_xnb[i]])

        t_xnall = qc.tok()
        t_T = []
        for dc in range(DC):
            t_T.append(dma("sync", qt, xnT[:, dc * NTOK:(dc + 1) * NTOK],
                           xnd[:, dc * 128:(dc + 1) * 128], waits=[t_xnall], transpose=True))

        t_logev = [None] * NT
        t_buev = [None] * (NT // 4)
        for g in range(NT // 4):
            for i in range(4 * g, 4 * g + 4):
                pslot = PSC[:, 0:8] if i % 2 == 0 else PSC[:, 512:520]
                w = (list(t_T) + [t_wg]) if i == 0 else []
                if i >= 2:
                    w = w + [t_logev[i - 2]]
                tpe = mmg([(pslot, xnT_tile(dc, i), wg_sb[:, dc * E:(dc + 1) * E],
                            dc == 0, dc == DC - 1) for dc in range(DC)], waits=w)
                t_logev[i] = op("vector", lambda e, i=i, pslot=pslot:
                                e.tensor_copy(out=logits_all[:, i * E:(i + 1) * E], in_=pslot),
                                waits=[tpe])
            pslot = PSD[0:64, 0:512] if g % 2 == 0 else PSD[0:64, 512:1024]
            w = (list(t_T) + [t_bwt]) if g == 0 else []
            if g >= 2:
                w = w + [t_buev[g - 2]]
            tpe = mmg([(pslot, bwt_sb[:, dc * S:(dc + 1) * S],
                        xnT[:, dc * NTOK + g * 512: dc * NTOK + (g + 1) * 512],
                        dc == 0, dc == DC - 1) for dc in range(DC)], waits=w)
            t_buev[g] = op("vector", lambda e, g=g, pslot=pslot:
                           e.tensor_copy(out=bu_sb[:, g * 512:(g + 1) * 512], in_=pslot),
                           waits=[tpe])

        t_l128 = op("vector", lambda e: e.tensor_scalar(ld128_sb[:, :], ldlog_sb[:, :], 128.0, None, ALU.mult),
                    waits=[t_ldlog])
        t_d128mm = mmg([(PSB[:, 0:512], o1x128f_sb[:, :], ld128_sb[:, 0:512], True, True),
                        (PSB[:, 512:1024], o1x128f_sb[:, :], ld128_sb[:, 512:1024], True, True)],
                       waits=[t_l128, t_o1f])
        t_d128 = op("scalar", lambda e: e.activation(out=d128_sb[:, :], in_=PSB[:, :], func=AF.Exp),
                    waits=[t_d128mm])
        t_pb = [None] * NT
        t_c = [None] * NT
        t_cs = [None] * NT
        t_cadd = [None] * NT
        t_pwv = [None] * NT
        for i in range(NT):
            pbt = PSA if i % 2 == 0 else PSC
            pwt = PSB if i % 2 == 0 else PSD
            w = [t_buev[i // 4], t_cwt]
            if i >= 2:
                w.append(t_c[i - 2])
            if i in (0, 1):
                w.append(t_logev[NT - 1])
            t_pb[i] = mmg(
                [(pbt[:, 0:512], bu_sb[0:64, i * 128:(i + 1) * 128], cwt_sb[:, 0:512], True, True),
                 (pbt[:, 512:1024], bu_sb[0:64, i * 128:(i + 1) * 128], cwt_sb[:, 512:1024], True, True)],
                waits=w)
            if i == 0:
                t_pw = mmg(
                    [(pwt[:, 0:512], iotag_sb[:, 0:128], ldlog_sb[:, 0:512], True, True),
                     (pwt[:, 512:1024], iotag_sb[:, 0:128], ldlog_sb[:, 512:1024], True, True)],
                    waits=[t_iog, t_ldlog, t_d128])
                t_pwv[i] = op("scalar", lambda e, p=pwt: e.activation(out=pw_a[:, :], in_=p[:, :], func=AF.Exp),
                              waits=[t_pw])
            else:
                t_pwv[i] = op("vector", lambda e: e.tensor_mul(pw_a[:, :], pw_a[:, :], d128_sb[:, :]),
                              waits=[t_c[i - 1], t_d128])
            t_c[i] = op("vector", lambda e, p=pbt: e.tensor_mul(c_a[:, :], p[:, :], pw_a[:, :]),
                        waits=[t_pwv[i], t_pb[i]] + ([t_cs[i - 1]] if i >= 1 else []))
            w = [t_c[i], t_o128b]
            if i == 0:
                w.append(t_pwv[0])
            if i == 1:
                w.append(t_buev[NT // 4 - 1])
            if i >= 2:
                w.append(t_cadd[i - 2])
            t_cs[i] = mmg(
                [(pwt[0:1, 0:512], o128b_sb[:, :], c_a[:, 0:512], True, True),
                 (pwt[0:1, 512:1024], o128b_sb[:, :], c_a[:, 512:1024], True, True)],
                waits=w)
            cr = [colrow0, colrow1][i % 2]
            tcp = op("scalar", lambda e, p=pwt, cr=cr: e.copy(out=cr[:, :], in_=p[0:1, 0:1024]),
                     waits=[t_cs[i]] + ([t_cadd[i - 2]] if i >= 2 else []))
            t_cadd[i] = dma("scalar", qc, carrd[i:i + 1, :], cr[:, :], waits=[tcp])
        t_c32 = dma("sync", qs, carr32_sb[:, :], carrd[:, :], waits=[t_cadd[NT - 1], t_cadd[NT - 2]])
        t_cl = mmg([(PSB[0:1, 0:512], o128f_sb[0:32, 0:1], carr32_sb[:, 0:512], True, True),
                    (PSB[0:1, 512:1024], o128f_sb[0:32, 0:1], carr32_sb[:, 512:1024], True, True)],
                   waits=[t_c32, t_o128f, t_cs[NT - 1], t_cs[NT - 2]])
        t_carry_done = op("vector", lambda e: e.tensor_copy(out=carry_local[:, :], in_=PSB[0:1, 0:1024]),
                          waits=[t_cl, t_z3])

        t_crow = [None] * NT
        t_crowev = [None] * NT
        t_cntev = [None] * NT
        t_oh1 = [None] * NT
        t_oh2 = [None] * NT
        t_d12 = [None] * NT
        for i in range(NT):
            lg = logits_all[:, i * E:(i + 1) * E]
            tsrt = op("vector", lambda e, lg=lg: e.max(srt_sb[:, :], lg), waits=[t_logev[i]])
            t_oh1[i] = op("vector", lambda e, lg=lg, i=i:
                          e.tensor_scalar(oh1_all[:, i * E:(i + 1) * E], lg, srt_sb[:, 0:1], None, ALU.is_equal))
            t_oh2[i] = op("vector", lambda e, lg=lg, i=i:
                          e.tensor_scalar(oh2_all[:, i * E:(i + 1) * E], lg, srt_sb[:, 1:2], None, ALU.is_equal))
            t_d12[i] = op("vector", lambda e, i=i:
                          e.tensor_scalar(d12_all[:, i:i + 1], srt_sb[:, 0:1], srt_sb[:, 1:2], None, ALU.subtract))
            ts01 = op("vector", lambda e, i=i:
                      e.tensor_add(s01_sb[:, :], oh1_all[:, i * E:(i + 1) * E], oh2_all[:, i * E:(i + 1) * E]))
            w = [ts01, t_trikf, t_o1f, t_cadd[NT - 1]]
            if i >= 1:
                w += [t_crowev[i - 1], t_cntev[i - 1]]
            if i == 0:
                w.append(t_cadd[NT - 1])
            mlist = [(PSA[:, 0:8], trikf_sb[:, :], s01_sb[:, :], True, i == 0)]
            if i >= 1:
                mlist.append((PSA[:, 0:8], o1x128f_sb[:, :], crow_sb[:, :], False, True))
            t_cnt = mmg(mlist, waits=w)
            t_cntev[i] = op("vector", lambda e, i=i: e.tensor_copy(out=cnt_all[:, i * E:(i + 1) * E], in_=PSA[:, 0:8]),
                            waits=[t_cnt])
            t_crow[i] = mmg([(PSB[0:1, 0:8], o128f_sb[:, :], s01_sb[:, :], i == 0, True)],
                            waits=[ts01, t_o128f] + ([t_cadd[NT - 1]] if i == 0 else [t_crowev[i - 1]]))
            t_crowev[i] = op("scalar", lambda e: e.copy(out=crow_sb[:, :], in_=PSB[0:1, 0:8]),
                             waits=[t_crow[i]])

        oh1_3 = oh1_all[:, :].rearrange("p (t e) -> p t e", e=E)
        oh2_3 = oh2_all[:, :].rearrange("p (t e) -> p t e", e=E)
        cnt_3 = cnt_all[:, :].rearrange("p (t e) -> p t e", e=E)
        tmp_3 = tmp256[:, :].rearrange("p (t e) -> p t e", e=E)
        op("vector", lambda e: e.tensor_mul(tmp_3, oh1_3, cnt_3), waits=[t_cntev[NT - 1]])
        t_r0 = op("vector", lambda e: e.reduce_sum(rank0_all[:, :].unsqueeze(2), tmp_3, axis=AXX))
        op("vector", lambda e: e.tensor_mul(tmp_3, oh2_3, cnt_3))
        t_r1 = op("vector", lambda e: e.reduce_sum(rank1_all[:, :].unsqueeze(2), tmp_3, axis=AXX))

        tp1 = op("vector", lambda e: e.tensor_copy(out=pack_sb[:, 0:1024], in_=carry_local[:, :]),
                 waits=[t_carry_done])
        tp2 = op("vector", lambda e: e.tensor_copy(out=pack_sb[:, 1024:1032], in_=crow_sb[:, :]),
                 waits=[t_crowev[NT - 1]])
        t_ccw = dma("sync", qs, ccin[:, :], pack_sb[:, :], waits=[tp2])
        cc.v += 1
        t_cc = cc.tok()

        def cc_fn(e, w=t_ccw, h=cc.h):
            e.wait_ge(w[0], w[1])
            e.collective_compute(
                "AllGather", ALU.bypass,
                replica_groups=[list(range(NCORE))],
                ins=[ccin.opt()], outs=[ccout.opt()],
            ).then_inc(h, 1)
        emit("gpsimd", cc_fn)
        t_ccr = dma("sync", qs, gath_sb[:, :], ccout[:, :], waits=[t_cc])

        t_offs = mmg([(PSA[0:1, 0:8], rmask_sb[:, :], gath_sb[0:8, 1024:1032], True, True)],
                     waits=[t_ccr, t_rm, t_cntev[NT - 1]])
        t_offsev = op("vector", lambda e: e.tensor_copy(out=offsrow_sb[:, :], in_=PSA[0:1, 0:8]),
                      waits=[t_offs])
        t_offsb = mmg([(PSA[:, 0:8], o1x128f_sb[:, :], offsrow_sb[:, :], True, True)],
                      waits=[t_offsev])
        t_offsbev = op("vector", lambda e: e.tensor_copy(out=offsb_sb[:, :], in_=PSA[:, 0:8]),
                       waits=[t_offsb])
        t_c0 = mmg([(PSB[0:1, 0:512], cmask_sb[:, :], gath_sb[0:8, 0:512], True, True),
                    (PSB[0:1, 512:1024], cmask_sb[:, :], gath_sb[0:8, 512:1024], True, True)],
                   waits=[t_ccr, t_cm, t_crowev[NT - 1]])
        t_c0ev = op("vector", lambda e: e.tensor_copy(out=carry0_sb[:, :], in_=PSB[0:1, 0:1024]),
                    waits=[t_c0])
        t_carr = mmg([(PSC[0:32, 0:512], trikf_sb[0:32, 0:32], carr32_sb[:, 0:512], True, False),
                      (PSC[0:32, 0:512], o1x128f_sb[:, 0:32], carry0_sb[:, 0:512], False, True),
                      (PSC[0:32, 512:1024], trikf_sb[0:32, 0:32], carr32_sb[:, 512:1024], True, False),
                      (PSC[0:32, 512:1024], o1x128f_sb[:, 0:32], carry0_sb[:, 512:1024], False, True)],
                     waits=[t_c0ev, t_c[NT - 1]])
        t_carrev = op("vector", lambda e: e.tensor_copy(out=carrb32_sb[:, :], in_=PSC[0:32, 0:1024]),
                      waits=[t_carr])
        t_carrw = dma("scalar", qc, carrbd[:, :], carrb32_sb[:, :], waits=[t_carrev])

        t_g1 = op("scalar", lambda e: e.activation(out=g1_all[:, :], in_=d12_all[:, :], func=AF.Sigmoid),
                  waits=[t_d12[NT - 1]])
        t_g2 = op("vector", lambda e: e.tensor_scalar(g2_all[:, :], g1_all[:, :], -1.0, 1.0, ALU.mult, ALU.add),
                  waits=[t_g1])

        offsb_b = offsb_sb[:, :].unsqueeze(1).broadcast_to([128, NT, E])
        eqrow_b = eqrow_sb[:, :].unsqueeze(1).broadcast_to([128, NT, E])
        t_slot = {}
        for (slot, oh_3x, rank_all_x, g_all_x, gate_all_x, dall, pdall) in (
            (0, oh1_3, rank0_all, g1_all, gate0_all, d20_all, pdest0_sb),
            (1, oh2_3, rank1_all, g2_all, gate1_all, d21_all, pdest1_sb),
        ):
            rk = rank_all_x[:, :]
            op("vector", lambda e, oh_3x=oh_3x: e.tensor_mul(tmp_3, oh_3x, offsb_b),
               waits=[t_offsbev, t_g2, t_r0, t_r1])
            op("vector", lambda e: e.reduce_sum(t32a[:, :].unsqueeze(2), tmp_3, axis=AXX))
            op("vector", lambda e, rk=rk: e.tensor_add(t32b[:, :], t32a[:, :], rk))
            op("vector", lambda e: e.tensor_scalar(t32c[:, :], t32b[:, :], CAP - 0.5, None, ALU.is_lt))
            tg_gate = op("vector", lambda e, g_all_x=g_all_x, gate_all_x=gate_all_x:
                         e.tensor_mul(gate_all_x[:, :], g_all_x[:, :], t32c[:, :]))
            op("vector", lambda e, oh_3x=oh_3x: e.tensor_mul(tmp_3, oh_3x, eqrow_b), waits=[t_eq])
            op("vector", lambda e: e.reduce_sum(t32a[:, :].unsqueeze(2), tmp_3, axis=AXX))
            op("vector", lambda e, rk=rk: e.tensor_add(t32b[:, :], t32a[:, :], rk))
            t_pd = op("vector", lambda e, pdall=pdall: e.tensor_copy(out=pdall[:, :], in_=t32b[:, :]))
            op("vector", lambda e, pdall=pdall: e.tensor_scalar(i32a[:, :], pdall[:, :], 7, None, ALU.arith_shift_right))
            op("vector", lambda e, rk=rk: e.tensor_copy(out=i32b[:, :], in_=rk))
            op("vector", lambda e: e.tensor_scalar(i32b[:, :], i32b[:, :], 127, None, ALU.bitwise_and))
            op("vector", lambda e: e.tensor_scalar(i32c[:, :], i32b[:, :], 6, None, ALU.logical_shift_left))
            op("vector", lambda e: e.tensor_scalar(i32b[:, :], i32b[:, :], 4, None, ALU.logical_shift_left))
            op("vector", lambda e: e.tensor_add(i32c[:, :], i32c[:, :], i32b[:, :]))
            t_d2 = op("vector", lambda e, dall=dall: e.tensor_add(dall[:, :], i32c[:, :], i32a[:, :]))
            t_slot[slot] = (t_d2, tg_gate, t_pd)
        t_last_meta = t_slot[1][0]

        t_scat = []
        for i in range(NT):
            for (slot, dall, gate_all_x) in ((0, d20_all, gate0_all), (1, d21_all, gate1_all)):
                t1 = idma(tidd, IndirectOffsetOnAxis(dall[:, i:i + 1], 1),
                          tokid_sb[:, i:i + 1], None,
                          waits=[t_slot[slot][0], t_slot[slot][1], t_tid0, t_gat0, t_tok])
                t2 = idma(gated, IndirectOffsetOnAxis(dall[:, i:i + 1], 1),
                          gate_all_x[:, i:i + 1], None, waits=[])
                t_scat += [t1, t2]
        t_tidld = dma("sync", qs, tidt_sb[:, :], tidd2d, waits=[t_scat[-1]])
        t_gatld = dma("sync", qs, gatet_sb[:, :], gated2d, waits=[t_scat[-1]])

        if DEBUG:
            dma("sync", qs, dbg["logits"][:, :], logits_all[:, :], waits=[t_logev[NT - 1]])
            dma("sync", qs, dbg["carry"][:, :], carry_local[:, :], waits=[t_carry_done])
            dma("sync", qs, dbg["rank0"][:, :], rank0_all[:, :], waits=[t_r0])
            dma("sync", qs, dbg["rank1"][:, :], rank1_all[:, :], waits=[t_r1])
            dma("sync", qs, dbg["gate0"][:, :], gate0_all[:, :], waits=[t_last_meta])
            dma("sync", qs, dbg["gate1"][:, :], gate1_all[:, :], waits=[t_last_meta])
            dma("sync", qs, dbg["pdest0"][:, :], pdest0_sb[:, :], waits=[t_last_meta])
            dma("sync", qs, dbg["pdest1"][:, :], pdest1_sb[:, :], waits=[t_last_meta])
            dma("sync", qs, dbg["tidT"][:, :], tidt_sb[:, :], waits=[t_tidld])
            dma("sync", qs, dbg["gateT"][:, :], gatet_sb[:, :], waits=[t_gatld])

        t_b0_psA_free = t_offsbev
        t_b0_psB_free = t_c0ev
        t_logits_psC_free = t_c[NT - 1]
        t_bu_psD_free = t_cadd[NT - 1]

    a_ctx.close()

    b_ctx = contextlib.ExitStack()
    w1p0 = b_ctx.enter_context(nc.sbuf_tensor("w1p0", [128, DC * 512], BF16))
    w1p1 = b_ctx.enter_context(nc.sbuf_tensor("w1p1", [128, DC * 512], BF16))
    w2_sb = b_ctx.enter_context(nc.sbuf_tensor("w2_sb", [128, HC * D], BF16))
    hT = b_ctx.enter_context(nc.sbuf_tensor("hT", [128, HC * HQ], BF16))
    packT = b_ctx.enter_context(nc.sbuf_tensor("packT", [128, DC * Q], BF16))
    rows0 = b_ctx.enter_context(nc.sbuf_tensor("rows0", [128, D], BF16))
    rows1 = b_ctx.enter_context(nc.sbuf_tensor("rows1", [128, D], BF16))
    rows2 = b_ctx.enter_context(nc.sbuf_tensor("rows2", [128, D], BF16))
    rows3 = b_ctx.enter_context(nc.sbuf_tensor("rows3", [128, D], BF16))
    ysb0 = b_ctx.enter_context(nc.sbuf_tensor("ysb0", [128, D], BF16))
    ysb1 = b_ctx.enter_context(nc.sbuf_tensor("ysb1", [128, D], BF16))
    b1t_sb = b_ctx.enter_context(nc.sbuf_tensor("b1t_sb", [128, E * HC], F32))
    b2_sb = b_ctx.enter_context(nc.sbuf_tensor("b2_sb", [1, 2 * D], BF16))
    w1ps = [w1p0, w1p1]
    rowss = [rows0, rows1, rows2, rows3]
    ysbs = [ysb0, ysb1]

    t_b1t = dma("sync", qs, b1t_sb[:, :], b1t_in[:, :], waits=[t_gatld])
    t_b2_e = [None] * E

    RCH = [(0, 512), (512, 128)]

    pydr = pyd.rearrange("(r p) d -> r p d", p=128)
    packdr = packd.rearrange("(r p) d -> r p d", p=128)

    t_packw = {}
    t_gelu_seq = []
    t_yev_all = []
    t_yw_all = []
    t_mm1_last = None
    t_mm2_last = None
    t_transp_done = None
    t_hdump = None
    w1k = 0

    for e in range(E):
        wb2 = [t_gatld] if e == 0 else ([t_mm2_last] if e >= 2 else [])
        t_b2_e[e] = dma("gpsimd", qb2[e % 2], b2_sb[:, (e % 2) * D:(e % 2 + 1) * D],
                        b2_in[:, e * D:(e + 1) * D], waits=wb2)
        for rb in range(RB):
            k = e * RB + rb
            war = [t_packw[k - 4]] if k >= 4 else []
            tg_ = idma(rowss[k % 4][:, :], None, xnd,
                       IndirectOffsetOnAxis(tidt_sb[:, e * RB + rb:e * RB + rb + 1], 0),
                       waits=[t_tidld] + war)
            w = [tg_]
            if e >= 1:
                w.append(t_transp_done)
            t_packw[k] = dma("sync", qs, packdr[rb], rowss[k % 4][:, :], waits=w)
        tts = []
        for dc in range(DC):
            w = [t_packw[e * RB + RB - 1]]
            if e >= 1:
                w.append(t_mm1_last)
            tts.append(dma("sync", qt, packT[:, dc * Q:(dc + 1) * Q],
                           packd[:, dc * 128:(dc + 1) * 128], waits=w, transpose=True))
        t_transp_done = tts[-1]
        if DEBUG2:
            dma("sync", qs, dbg["packT2"].rearrange("(e p) c -> e p c", p=128)[e],
                packT[:, :], waits=[t_transp_done])

        t_w2_done = None
        w2_pieces = []
        for kpc in range(8):
            w = [t_mm2_last] if (e >= 1 and kpc == 0) else []
            if e == 0 and kpc == 0:
                w = [t_gatld]
            w2_pieces.append((kpc, w))

        for half in range(2):
            t_w1p = [None] * 8
            pbase = w1k

            def issue_w1_piece(pp, e=e, pbase=pbase, t_w1p=t_w1p):
                w = []
                k = pbase + pp
                if k >= 2:
                    w.append(t_gelu_seq[(k - 1) * 4 - 1])
                t_w1p[pp] = dma("gpsimd", qw1[k % 2],
                                w1ps[k % 2][:, :].rearrange("p (dc hh) -> p dc hh", dc=DC),
                                w1_in[e][:, pp * 512:(pp + 1) * 512].rearrange("(dc p) hh -> p dc hh", p=128),
                                waits=w)

            def issue_w2_pieces(n, e=e):
                nonlocal_t = None
                for _ in range(n):
                    if w2_pieces:
                        kpc, wp = w2_pieces.pop(0)
                        nonlocal_t = dma("gpsimd", qw2,
                                         w2_sb[:, kpc * 4 * D:(kpc + 1) * 4 * D].rearrange("p (hc d) -> p hc d", hc=4),
                                         w2_in[e][kpc * 512:(kpc + 1) * 512, :].rearrange("(hc p) d -> p hc d", p=128),
                                         waits=wp)
                return nonlocal_t

            for hc in range(HC):
                pp = hc // 4
                hcl = hc % 4
                if hc == 0:
                    issue_w1_piece(0)
                    if half == 0:
                        tw2 = issue_w2_pieces(2)
                        t_w2_done = tw2 or t_w2_done
                    issue_w1_piece(1)
                    if half == 0:
                        tw2 = issue_w2_pieces(1)
                        t_w2_done = tw2 or t_w2_done
                elif hc % 4 == 0 and hc // 4 + 1 < 8:
                    issue_w1_piece(hc // 4 + 1)
                    if half == 0:
                        tw2 = issue_w2_pieces(1)
                        t_w2_done = tw2 or t_w2_done
                w1buf = w1ps[(pbase + pp) % 2]
                m1k = len(t_gelu_seq)
                pm1 = PSA if m1k % 2 == 0 else PSC
                w = [t_w1p[pp], t_transp_done]
                if m1k >= 2:
                    w.append(t_gelu_seq[m1k - 2])
                elif m1k == 0:
                    w += [t_b0_psA_free]
                elif m1k == 1:
                    w += [t_logits_psC_free]
                if hc == 0 and t_mm2_last is not None:
                    w.append(t_mm2_last)
                gro = half * HQ
                mms = []
                for dc in range(DC):
                    lhs = w1buf[:, dc * 512 + hcl * 128: dc * 512 + (hcl + 1) * 128]
                    mms.append((pm1[:, 0:512], lhs, packT[:, dc * Q + gro: dc * Q + gro + 512],
                                dc == 0, dc == DC - 1))
                    mms.append((pm1[:, 512:640], lhs, packT[:, dc * Q + gro + 512: dc * Q + gro + 640],
                                dc == 0, dc == DC - 1))
                tmm = mmg(mms, waits=w)
                wg_ = [tmm, t_b1t]
                if hc == 0 and t_mm2_last is not None:
                    wg_.append(t_mm2_last)
                if hc == 0 and t_hdump is not None:
                    wg_.append(t_hdump)
                tg = op("scalar",
                        lambda ee, pm1=pm1, hc=hc, e=e:
                        ee.activation(out=hT[:, hc * HQ: hc * HQ + HQ],
                                      in_=pm1[:, 0:HQ], func=AF.Gelu_apprx_tanh,
                                      bias=b1t_sb[:, e * HC + hc: e * HC + hc + 1]),
                        waits=wg_)
                t_gelu_seq.append(tg)
            w1k += 8
            t_mm1_last = t_gelu_seq[-1]
            if DEBUG2:
                t_hdump = dma("sync", qs, dbg["hT2"].rearrange("(eh p) c -> eh p c", p=128)[e * 2 + half],
                              hT[:, :], waits=[t_mm1_last])

            for rbl in range(5):
                yk = len(t_yev_all)
                pm2 = PSB if yk % 2 == 0 else PSD
                grb = half * 5 + rbl
                w = []
                if rbl == 0:
                    w = [t_mm1_last, t_w2_done, t_b2_e[e], t_o1b]
                if yk >= 2:
                    w.append(t_yev_all[yk - 2])
                elif yk == 0:
                    w += [t_b0_psB_free]
                elif yk == 1:
                    w += [t_bu_psD_free]
                mms = []
                for hc in range(HC):
                    lhs = hT[:, hc * HQ + rbl * 128: hc * HQ + (rbl + 1) * 128]
                    mms.append((pm2[:, 0:512], lhs, w2_sb[:, hc * D: hc * D + 512],
                                hc == 0, False))
                    mms.append((pm2[:, 512:1024], lhs, w2_sb[:, hc * D + 512: (hc + 1) * D],
                                hc == 0, False))
                eo = (e % 2) * D
                mms.append((pm2[:, 0:512], o1x128b_sb[:, :], b2_sb[:, eo: eo + 512],
                            False, True))
                mms.append((pm2[:, 512:1024], o1x128b_sb[:, :], b2_sb[:, eo + 512: eo + D],
                            False, True))
                t_mm2_last = mmg(mms, waits=w)
                tv = op("vector",
                        lambda e_, pm2=pm2, e=e, grb=grb, yk=yk:
                        e_.tensor_scalar(ysbs[yk % 2][:, :], pm2[:, 0:1024],
                                         gatet_sb[:, e * RB + grb: e * RB + grb + 1], None, ALU.mult),
                        waits=[t_mm2_last, t_gatld] + ([t_yw_all[yk - 2]] if yk >= 2 else []))
                t_yev_all.append(tv)
                t_yw_all.append(dma("scalar", qc, pydr[e * RB + grb], ysbs[yk % 2][:, :], waits=[tv]))

    t_py_all = t_yw_all[-1]
    b_ctx.close()

    c_ctx = contextlib.ExitStack()
    if True:
        _c = c_ctx.enter_context
        xnT2 = _c(nc.sbuf_tensor("xnT2", [128, DC * NTOK], BF16))
        gwt_sb = _c(nc.sbuf_tensor("gwt_sb", [128, DC * D], BF16))
        gateb_sb = _c(nc.sbuf_tensor("gateb_sb", [1, D], BF16))
        pw_c = _c(nc.sbuf_tensor("pw_c", [128, D], F32))
        ctile0 = _c(nc.sbuf_tensor("ctile0", [1, D], BF16))
        ctile1 = _c(nc.sbuf_tensor("ctile1", [1, D], BF16))
        d128c_sb = _c(nc.sbuf_tensor("d128c_sb", [128, D], F32))
        ld128c_sb = _c(nc.sbuf_tensor("ld128c_sb", [1, D], F32))
        c_c = _c(nc.sbuf_tensor("c_c", [128, D], BF16))
        gate_c = _c(nc.sbuf_tensor("gate_c", [128, D], F32))
        sg2_c = _c(nc.sbuf_tensor("sg2_c", [128, D], F32))
        s_c = _c(nc.sbuf_tensor("s_c", [128, D], F32))
        ya0 = _c(nc.sbuf_tensor("ya0", [128, D], BF16))
        ya1 = _c(nc.sbuf_tensor("ya1", [128, D], BF16))
        yb0 = _c(nc.sbuf_tensor("yb0", [128, D], BF16))
        yb1 = _c(nc.sbuf_tensor("yb1", [128, D], BF16))
        ysum_c = _c(nc.sbuf_tensor("ysum_c", [128, D], F32))
        xt0c = _c(nc.sbuf_tensor("xt0c", [128, D], F32))
        xt1c = _c(nc.sbuf_tensor("xt1c", [128, D], F32))
        osb0 = _c(nc.sbuf_tensor("osb0", [128, D], F32))
        osb1 = _c(nc.sbuf_tensor("osb1", [128, D], F32))
        yas = [ya0, ya1]
        ybs = [yb0, yb1]
        xtc = [xt0c, xt1c]
        osb = [osb0, osb1]
        xr = x_in.rearrange("(i p) d -> i p d", p=128)
        outr = out_io.rearrange("(i p) d -> i p d", p=128)

        t_bdrain = [t_py_all, t_mm2_last, t_yev_all[-1], t_gelu_seq[-1]]
        t_T2 = []
        for dc in range(DC):
            t_T2.append(dma("sync", qt, xnT2[:, dc * NTOK:(dc + 1) * NTOK],
                            xnd[:, dc * 128:(dc + 1) * 128],
                            waits=t_bdrain if dc == 0 else [], transpose=True))
        t_gwt = dma("gpsimd", qgw, gwt_sb[:, :].rearrange("p (dc e) -> p dc e", dc=DC),
                    gwt_in.rearrange("(dc p) e -> p dc e", p=128),
                    waits=t_bdrain)
        t_gateb = dma("gpsimd", qgw, gateb_sb[:, :], gateb_in[:, :], waits=[])

        def xnT2_tile(dc, i):
            return xnT2[:, dc * NTOK + i * 128: dc * NTOK + (i + 1) * 128]

        t_l128c = op("vector", lambda e: e.tensor_scalar(ld128c_sb[:, :], ldlog_sb[:, :], 128.0, None, ALU.mult))
        t_d128cmm = mmg([(PSB[:, 0:512], o1x128f_sb[:, :], ld128c_sb[:, 0:512], True, True),
                         (PSB[:, 512:1024], o1x128f_sb[:, :], ld128c_sb[:, 512:1024], True, True)],
                        waits=[t_l128c] + t_bdrain + [t_yev_all[-1]])
        t_d128c = op("scalar", lambda e: e.activation(out=d128c_sb[:, :], in_=PSB[:, :], func=AF.Exp),
                     waits=[t_d128cmm])

        ctiles = [ctile0, ctile1]
        t_ct = [None] * NT
        t_h2 = [None] * NT
        t_xl = [None] * NT
        t_out = [None] * NT
        t_oa = [None] * NT
        t_sgm = [None] * NT
        t_exp2 = [None] * NT
        t_c2 = [None] * NT
        t_s2 = [None] * NT
        t_ct[0] = dma("sync", qs, ctiles[0][:, :], carrbd[0:1, :], waits=[t_carrw] + t_bdrain)
        t_ct[1] = dma("sync", qs, ctiles[1][:, :], carrbd[1:2, :], waits=[])
        t_ysum = [None] * NT
        t_sil = [None] * NT
        t_sgmul = [None] * NT

        t_xl[0] = dma("sync", qs, xtc[0][:, :], xr[0], waits=t_bdrain)
        t_xl[1] = dma("sync", qs, xtc[1][:, :], xr[1])

        for i in range(NT):
            par = i % 2
            pbt = PSA if par == 0 else PSC
            pwt = PSB if par == 0 else PSD
            w = [t_cwt]
            if i >= 2:
                w += [t_c2[i - 2], t_s2[i - 2]]
            if i < 2:
                w += t_bdrain + [t_yev_all[-1]]
            t_pb2 = mmg(
                [(pbt[:, 0:512], bu_sb[0:64, i * 128:(i + 1) * 128], cwt_sb[:, 0:512], True, True),
                 (pbt[:, 512:1024], bu_sb[0:64, i * 128:(i + 1) * 128], cwt_sb[:, 512:1024], True, True)],
                waits=w)
            if i == 0:
                t_pw2 = mmg(
                    [(pwt[:, 0:512], iotag_sb[:, 0:128], ldlog_sb[:, 0:512], True, True),
                     (pwt[:, 512:1024], iotag_sb[:, 0:128], ldlog_sb[:, 512:1024], True, True)],
                    waits=[t_iog, t_ldlog, t_d128c])
                t_exp2[i] = op("scalar", lambda e, p=pwt: e.activation(out=pw_c[:, :], in_=p[:, :], func=AF.Exp),
                               waits=[t_pw2])
            else:
                t_exp2[i] = op("vector", lambda e: e.tensor_mul(pw_c[:, :], pw_c[:, :], d128c_sb[:, :]),
                               waits=[t_c2[i - 1], t_d128c])
            t_c2[i] = op("vector", lambda e, p=pbt: e.tensor_mul(c_c[:, :], p[:, :], pw_c[:, :]),
                         waits=[t_exp2[i], t_pb2] + ([t_h2[i - 1]] if i >= 1 else []))
            ctb = ctiles[i % 2]
            t_h2[i] = mmg(
                [(pbt[:, 0:512], trikb_sb[:, :], c_c[:, 0:512], True, False),
                 (pbt[:, 0:512], o1x128b_sb[:, :], ctb[:, 0:512], False, True),
                 (pbt[:, 512:1024], trikb_sb[:, :], c_c[:, 512:1024], True, False),
                 (pbt[:, 512:1024], o1x128b_sb[:, :], ctb[:, 512:1024], False, True)],
                waits=[t_c2[i], t_trikb, t_o1b, t_ct[i]])
            t_h = t_h2[i]
            if i + 2 < NT:
                t_ct[i + 2] = dma("sync", qs, ctb[:, :], carrbd[i + 2:i + 3, :], waits=[t_h2[i]])
            mms = []
            for dc in range(DC):
                mms.append((pwt[:, 0:512], xnT2_tile(dc, i), gwt_sb[:, dc * D: dc * D + 512],
                            dc == 0, False))
                mms.append((pwt[:, 512:1024], xnT2_tile(dc, i), gwt_sb[:, dc * D + 512: (dc + 1) * D],
                            dc == 0, False))
            mms.append((pwt[:, 0:512], o1x128b_sb[:, :], gateb_sb[:, 0:512], False, True))
            mms.append((pwt[:, 512:1024], o1x128b_sb[:, :], gateb_sb[:, 512:1024], False, True))
            w = [t_gwt, t_gateb, t_o1b]
            if i >= 2:
                w.append(t_sgm[i - 2])
            if i < 2:
                w.append(t_yev_all[-1])
            if i == 0:
                w.append(t_exp2[0])
            t_gmm = mmg(mms, waits=w + (list(t_T2) if i == 0 else []))
            t_sgm[i] = op("scalar", lambda e, p=pwt: e.activation(out=gate_c[:, :], in_=p[:, :], func=AF.Sigmoid),
                          waits=[t_gmm] + ([t_s2[i - 1]] if i >= 1 else []))
            t_s2[i] = op("vector", lambda e, p=pbt: e.tensor_mul(s_c[:, :], p[:, :], gate_c[:, :]),
                         waits=[t_sgm[i], t_h] + ([t_sgmul[i - 1]] if i >= 1 else []))
            if DEBUG:
                dma("scalar", qc, dbg["s"].rearrange("(i p) d -> i p d", p=128)[i], s_c[:, :],
                    waits=[t_s2[i]])
            w = [t_py_all]
            if i >= 2:
                w.append(t_ysum[i - 2])
            t_ya = idma(yas[par][:, :], None, pyd,
                        IndirectOffsetOnAxis(pdest0_sb[:, i:i + 1], 0), waits=w)
            t_yb = idma(ybs[par][:, :], None, pyd,
                        IndirectOffsetOnAxis(pdest1_sb[:, i:i + 1], 0), waits=[])
            t_ysum[i] = op("vector", lambda e, par=par: e.tensor_add(ysum_c[:, :], yas[par][:, :], ybs[par][:, :]),
                           waits=[t_ya, t_yb] + ([t_sil[i - 1]] if i >= 1 else []))
            tsg2 = op("scalar", lambda e: e.activation(out=sg2_c[:, :], in_=ysum_c[:, :], func=AF.Sigmoid),
                      waits=[t_ysum[i]] + ([t_sil[i - 1]] if i >= 1 else []))
            t_sil[i] = op("vector", lambda e: e.tensor_mul(ysum_c[:, :], ysum_c[:, :], sg2_c[:, :]),
                          waits=[tsg2])
            if DEBUG:
                dma("scalar", qc, dbg["g"].rearrange("(i p) d -> i p d", p=128)[i], ysum_c[:, :],
                    waits=[t_sil[i]])
            t_sgmul[i] = op("vector", lambda e: e.tensor_mul(s_c[:, :], s_c[:, :], ysum_c[:, :]),
                            waits=[t_sil[i], t_s2[i]])
            t_oa[i] = op("vector", lambda e, par=par: e.tensor_add(osb[par][:, :], s_c[:, :], xtc[par][:, :]),
                         waits=[t_sgmul[i], t_xl[i]] + ([t_out[i - 2]] if i >= 2 else []))
            t_out[i] = dma("sync", qs, outr[i], osb[par][:, :], waits=[t_oa[i]])
            if i + 2 < NT:
                t_xl[i + 2] = dma("sync", qs, xtc[par][:, :], xr[i + 2], waits=[t_oa[i]])

        if DEBUG2:
            dma("sync", qs, dbg["pyd2"][:, :], pyd[:, :], waits=[t_py_all])
            dma("sync", qs, dbg["xnd2"][:, :], xnd[:, :], waits=[t_py_all])
            dma("sync", qs, dbg["tidd2"][:, :], tidd[:, :], waits=[t_py_all])
            dma("sync", qs, dbg["carrbd2"][:, :], carrbd[:, :], waits=[t_py_all])
        final_qs = qs.tok()

        def f_final(e, tok=final_qs):
            e.wait_ge(tok[0], tok[1])
        emit("sync", f_final)
    c_ctx.close()

    with nc.Block() as block:
        @block.sync
        def _(e):
            for f in prog["sync"]:
                f(e)

        @block.scalar
        def _(e):
            for f in prog["scalar"]:
                f(e)

        @block.vector
        def _(e):
            for f in prog["vector"]:
                f(e)

        @block.tensor
        def _(e):
            for f in prog["tensor"]:
                f(e)

        @block.gpsimd
        def _(e):
            for f in prog["gpsimd"]:
                f(e)

    stack.close()
    return nc


def _consts():
    bf = ml_dtypes.bfloat16
    c = {}
    c["trikf"] = np.triu(np.ones((128, 128), np.float32), 1)
    c["trikb"] = c["trikf"].astype(bf)
    c["o128f"] = np.ones((128, 1), np.float32)
    c["o128b"] = c["o128f"].astype(bf)
    c["o1x128f"] = np.ones((1, 128), np.float32)
    c["o1x128b"] = c["o1x128f"].astype(bf)
    c["eqrow"] = np.ascontiguousarray(np.tile((np.arange(E) * Q).astype(np.float32), (128, 1)))
    c["e10row"] = np.ascontiguousarray(np.tile((np.arange(E) * RB).astype(np.float32), (128, 1)))
    c["tokid"] = np.ascontiguousarray(np.arange(NTOK, dtype=np.int32).reshape(NT, 128).T)
    return c


def _host_prep(L0_attn_x, L0_attn_B_w, L0_attn_C_w, L0_attn_gate_w, L0_attn_gate_b,
               L0_attn_log_decay, L0_ffn_Wg, L0_ffn_W1, L0_ffn_b1, L0_ffn_W2, L0_ffn_b2):
    x = np.ascontiguousarray(np.asarray(L0_attn_x, dtype=np.float32)).reshape(NCORE * NTOK, D)
    shared = {
        "w1": np.ascontiguousarray(np.asarray(L0_ffn_W1, dtype=np.float32)),
        "w2": np.ascontiguousarray(np.asarray(L0_ffn_W2, dtype=np.float32)),
        "wg": np.ascontiguousarray(np.asarray(L0_ffn_Wg, dtype=np.float32)),
        "bwt": np.ascontiguousarray(np.asarray(L0_attn_B_w, dtype=np.float32).T),
        "cwt": np.ascontiguousarray(np.asarray(L0_attn_C_w, dtype=np.float32).T),
        "gwt": np.ascontiguousarray(np.asarray(L0_attn_gate_w, dtype=np.float32).T),
        "gateb": np.asarray(L0_attn_gate_b, dtype=np.float32).reshape(1, D),
        "ldec": np.asarray(L0_attn_log_decay, dtype=np.float32).reshape(1, D),
        "b1t": np.ascontiguousarray(
            np.asarray(L0_ffn_b1, dtype=np.float32).reshape(E, HC, 128).transpose(2, 0, 1).reshape(128, E * HC)),
        "b2": np.ascontiguousarray(np.asarray(L0_ffn_b2, dtype=np.float32).reshape(1, E * D)),
    }
    shared.update(_consts())
    in_maps = []
    for c in range(NCORE):
        m = dict(shared)
        m["x"] = np.ascontiguousarray(x[c * NTOK:(c + 1) * NTOK])
        m["iotag"] = ((c % 2) * NTOK + np.arange(128, dtype=np.float32)).reshape(1, 128)
        m["rmask"] = (np.arange(8) < c).astype(np.float32).reshape(8, 1)
        m["cmask"] = (np.eye(8, dtype=np.float32)[c - 1].reshape(8, 1)
                      if c % 2 == 1 else np.zeros((8, 1), np.float32))
        in_maps.append(m)
    return in_maps


def kernel(**inputs):
    in_maps = _host_prep(**inputs)
    nc = build_nc()
    res = run_bass_kernel_spmd(nc, in_maps, core_ids=list(range(NCORE)))
    out = np.concatenate([np.asarray(r["out"]) for r in res.results], axis=0)
    return out.reshape(4, 8192, 1024)


# version 08
# speedup vs baseline: 1.0153x; 1.0153x over previous
import numpy as np
import ml_dtypes
import contextlib
import concourse.bass as bass
import concourse.mybir as mybir
from concourse.bass import IndirectOffsetOnAxis
from concourse.bass_utils import run_bass_kernel_spmd

F32 = mybir.dt.float32
BF16 = mybir.dt.bfloat16
I32 = mybir.dt.int32
AF = mybir.ActivationFunctionType
ALU = mybir.AluOpType
AXX = mybir.AxisListType.X

NCORE = 8
NTOK = 4096
NT = 32
D = 1024
DC = 8
S = 64
E = 8
H = 4096
HC = 32
Q = 1280
RB = 10
HQ = 640
CAP = 8602
EPS = 1e-5

ENGINES = ("sync", "scalar", "vector", "tensor", "gpsimd")

DEBUG = False
DEBUG2 = False


class TSem:
    __slots__ = ("h", "v", "name")

    def __init__(self, h, name):
        self.h = h
        self.v = 0
        self.name = name

    def bump(self, n):
        self.v += n
        return (self.h, self.v)

    def tok(self):
        return (self.h, self.v)


def build_nc():
    nc = bass.Bass("TRN2", target_bir_lowering=False)

    x_in = nc.dram_tensor("x", [NTOK, D], F32, kind="ExternalInput").ap()
    w1_in = nc.dram_tensor("w1", [E, D, H], F32, kind="ExternalInput").ap()
    w2_in = nc.dram_tensor("w2", [E, H, D], F32, kind="ExternalInput").ap()
    wg_in = nc.dram_tensor("wg", [D, E], F32, kind="ExternalInput").ap()
    bwt_in = nc.dram_tensor("bwt", [D, S], F32, kind="ExternalInput").ap()
    cwt_in = nc.dram_tensor("cwt", [S, D], F32, kind="ExternalInput").ap()
    gwt_in = nc.dram_tensor("gwt", [D, D], F32, kind="ExternalInput").ap()
    gateb_in = nc.dram_tensor("gateb", [1, D], F32, kind="ExternalInput").ap()
    ldec_in = nc.dram_tensor("ldec", [1, D], F32, kind="ExternalInput").ap()
    b1t_in = nc.dram_tensor("b1t", [128, E * HC], F32, kind="ExternalInput").ap()
    b2_in = nc.dram_tensor("b2", [1, E * D], F32, kind="ExternalInput").ap()
    trikf_in = nc.dram_tensor("trikf", [128, 128], F32, kind="ExternalInput").ap()
    trikb_in = nc.dram_tensor("trikb", [128, 128], BF16, kind="ExternalInput").ap()
    o128f_in = nc.dram_tensor("o128f", [128, 1], F32, kind="ExternalInput").ap()
    o128b_in = nc.dram_tensor("o128b", [128, 1], BF16, kind="ExternalInput").ap()
    o1x128f_in = nc.dram_tensor("o1x128f", [1, 128], F32, kind="ExternalInput").ap()
    o1x128b_in = nc.dram_tensor("o1x128b", [1, 128], BF16, kind="ExternalInput").ap()
    eqrow_in = nc.dram_tensor("eqrow", [128, E], F32, kind="ExternalInput").ap()
    e10row_in = nc.dram_tensor("e10row", [128, E], F32, kind="ExternalInput").ap()
    tokid_in = nc.dram_tensor("tokid", [128, NT], I32, kind="ExternalInput").ap()
    iotag_in = nc.dram_tensor("iotag", [1, 128], F32, kind="ExternalInput").ap()
    rmask_in = nc.dram_tensor("rmask", [8, 1], F32, kind="ExternalInput").ap()
    cmask_in = nc.dram_tensor("cmask", [8, 1], F32, kind="ExternalInput").ap()
    out_io = nc.dram_tensor("out", [NTOK, D], F32, kind="ExternalOutput").ap()

    dbg = {}
    if DEBUG2:
        dbg["pyd2"] = nc.dram_tensor("dbg_pyd2", [E * Q, D], BF16, kind="ExternalOutput").ap()
        dbg["xnd2"] = nc.dram_tensor("dbg_xnd2", [NTOK, D], BF16, kind="ExternalOutput").ap()
        dbg["tidd2"] = nc.dram_tensor("dbg_tidd2", [1, 128 * E * RB], I32, kind="ExternalOutput").ap()
        dbg["carrbd2"] = nc.dram_tensor("dbg_carrbd2", [NT, D], BF16, kind="ExternalOutput").ap()
        dbg["packT2"] = nc.dram_tensor("dbg_packT2", [E * 128, DC * Q], BF16, kind="ExternalOutput").ap()
        dbg["hT2"] = nc.dram_tensor("dbg_hT2", [E * 2 * 128, HC * HQ], BF16, kind="ExternalOutput").ap()
    if DEBUG:
        dbg["logits"] = nc.dram_tensor("dbg_logits", [128, NT * E], F32, kind="ExternalOutput").ap()
        dbg["carry"] = nc.dram_tensor("dbg_carry", [1, D], F32, kind="ExternalOutput").ap()
        dbg["rank0"] = nc.dram_tensor("dbg_rank0", [128, NT], F32, kind="ExternalOutput").ap()
        dbg["rank1"] = nc.dram_tensor("dbg_rank1", [128, NT], F32, kind="ExternalOutput").ap()
        dbg["gate0"] = nc.dram_tensor("dbg_gate0", [128, NT], F32, kind="ExternalOutput").ap()
        dbg["gate1"] = nc.dram_tensor("dbg_gate1", [128, NT], F32, kind="ExternalOutput").ap()
        dbg["pdest0"] = nc.dram_tensor("dbg_pdest0", [128, NT], I32, kind="ExternalOutput").ap()
        dbg["pdest1"] = nc.dram_tensor("dbg_pdest1", [128, NT], I32, kind="ExternalOutput").ap()
        dbg["tidT"] = nc.dram_tensor("dbg_tidT", [128, E * RB], I32, kind="ExternalOutput").ap()
        dbg["gateT"] = nc.dram_tensor("dbg_gateT", [128, E * RB], F32, kind="ExternalOutput").ap()
        dbg["s"] = nc.dram_tensor("dbg_s", [NTOK, D], F32, kind="ExternalOutput").ap()
        dbg["g"] = nc.dram_tensor("dbg_g", [NTOK, D], F32, kind="ExternalOutput").ap()

    xnd = nc.dram_tensor("xnd", [NTOK, D], BF16).ap()
    packd0 = nc.dram_tensor("packd0", [Q, D], BF16).ap()
    packd1 = nc.dram_tensor("packd1", [Q, D], BF16).ap()
    pyd = nc.dram_tensor("pyd", [E * Q, D], BF16).ap()
    tidd = nc.dram_tensor("tidd", [1, 128 * E * RB], I32).ap()
    tidd2d = tidd.rearrange("o (p c) -> (o p) c", p=128)
    carrd = nc.dram_tensor("carrd", [NT, D], F32).ap()
    carrbd = nc.dram_tensor("carrbd", [NT, D], BF16).ap()
    ccin = nc.dram_tensor("ccin", [1, 1040], F32).ap()
    ccout = nc.dram_tensor("ccout", [8, 1040], F32).ap()

    prog = {k: [] for k in ENGINES}

    def emit(eng, fn):
        prog[eng].append(fn)

    stack = contextlib.ExitStack()

    def sem(name):
        return TSem(stack.enter_context(nc.semaphore(name)), name)

    qs = sem("qs")
    qc = sem("qc")
    qt = sem("qt")
    qgi = sem("qgi")
    qw1 = [sem("qw1a"), sem("qw1b")]
    qw2 = sem("qw2")
    qb2 = [sem("qb2a"), sem("qb2b")]
    qgw = sem("qgw")
    qg = sem("qg")
    vs = sem("vs")
    ss = sem("ss")
    ps = sem("ps")
    cc = sem("cc")
    ge = sem("ge")
    last_v = [None]
    last_s = [None]

    def dma(queue, qsem, out, in_, waits=(), **kw):
        tok = qsem.bump(16)
        wl = tuple(t for t in waits if t is not None)

        def f(e, out=out, in_=in_, wl=wl, kw=kw, h=qsem.h):
            for (sh, v) in wl:
                e.wait_ge(sh, v)
            e.dma_start(out=out, in_=in_, **kw).then_inc(h, 16)
        emit(queue, f)
        return tok

    def idma(out, out_off, in_, in_off, waits=()):
        tok = qgi.bump(16)
        wl = tuple(t for t in waits if t is not None)

        def f(e, out=out, oo=out_off, in_=in_, io=in_off, wl=wl, h=qgi.h):
            for (sh, v) in wl:
                e.wait_ge(sh, v)
            e.indirect_dma_start(out=out, out_offset=oo, in_=in_, in_offset=io).then_inc(h, 16)
        emit("gpsimd", f)
        return tok

    def op(eng, build, waits=()):
        esem = vs if eng == "vector" else ss
        last = last_v if eng == "vector" else last_s
        wl = [t for t in waits if t is not None]
        if last[0] is not None:
            wl.append(last[0])
        tok = esem.bump(1)
        wl = tuple(wl)

        def f(e, build=build, wl=wl, h=esem.h):
            for (sh, v) in wl:
                e.wait_ge(sh, v)
            build(e).then_inc(h, 1)
        emit(eng, f)
        last[0] = tok
        return tok

    def mmg(mms, waits=()):
        tok = ps.bump(1)
        wl = tuple(t for t in waits if t is not None)

        def f(e, mms=mms, wl=wl, h=ps.h):
            for (sh, v) in wl:
                e.wait_ge(sh, v)
            for (o, l, r, st, sp) in mms:
                ins = e.matmul(o, l, r, start=st, stop=sp)
            ins.then_inc(h, 1)
        emit("tensor", f)
        return tok

    logits_all = stack.enter_context(nc.sbuf_tensor("logits_all", [128, NT * E], F32))
    bu_sb = stack.enter_context(nc.sbuf_tensor("bu_sb", [64, NTOK], BF16))
    cwt_sb = stack.enter_context(nc.sbuf_tensor("cwt_sb", [64, D], BF16))
    ldlog_sb = stack.enter_context(nc.sbuf_tensor("ldlog_sb", [1, D], F32))
    trikf_sb = stack.enter_context(nc.sbuf_tensor("trikf_sb", [128, 128], F32))
    trikb_sb = stack.enter_context(nc.sbuf_tensor("trikb_sb", [128, 128], BF16))
    o128f_sb = stack.enter_context(nc.sbuf_tensor("o128f_sb", [128, 1], F32))
    o128b_sb = stack.enter_context(nc.sbuf_tensor("o128b_sb", [128, 1], BF16))
    o1x128f_sb = stack.enter_context(nc.sbuf_tensor("o1x128f_sb", [1, 128], F32))
    o1x128b_sb = stack.enter_context(nc.sbuf_tensor("o1x128b_sb", [1, 128], BF16))
    iotag_sb = stack.enter_context(nc.sbuf_tensor("iotag_sb", [1, 128], F32))
    pdest0_sb = stack.enter_context(nc.sbuf_tensor("pdest0_sb", [128, NT], I32))
    pdest1_sb = stack.enter_context(nc.sbuf_tensor("pdest1_sb", [128, NT], I32))
    crow_sb = stack.enter_context(nc.sbuf_tensor("crow_sb", [1, E], F32))
    tidt_sb = stack.enter_context(nc.sbuf_tensor("tidt_sb", [128, E * RB], I32))
    stat_sb = stack.enter_context(nc.sbuf_tensor("stat_sb", [128, 16], F32))
    oh1_all = stack.enter_context(nc.sbuf_tensor("oh1_all", [128, NT * E], F32))
    oh2_all = stack.enter_context(nc.sbuf_tensor("oh2_all", [128, NT * E], F32))
    rank0_all = stack.enter_context(nc.sbuf_tensor("rank0_all", [128, NT], F32))
    rank1_all = stack.enter_context(nc.sbuf_tensor("rank1_all", [128, NT], F32))
    g1_all = stack.enter_context(nc.sbuf_tensor("g1_all", [128, NT], F32))
    g2_all = stack.enter_context(nc.sbuf_tensor("g2_all", [128, NT], F32))
    gate0_all = stack.enter_context(nc.sbuf_tensor("gate0_all", [128, NT], F32))
    gate1_all = stack.enter_context(nc.sbuf_tensor("gate1_all", [128, NT], F32))
    d20_all = stack.enter_context(nc.sbuf_tensor("d20_all", [128, NT], I32))
    d21_all = stack.enter_context(nc.sbuf_tensor("d21_all", [128, NT], I32))
    tmp256 = stack.enter_context(nc.sbuf_tensor("tmp256", [128, NT * E], F32))
    t32a = stack.enter_context(nc.sbuf_tensor("t32a", [128, NT], F32))
    t32b = stack.enter_context(nc.sbuf_tensor("t32b", [128, NT], F32))
    t32c = stack.enter_context(nc.sbuf_tensor("t32c", [128, NT], F32))
    i32a = stack.enter_context(nc.sbuf_tensor("i32a", [128, NT], I32))
    i32b = stack.enter_context(nc.sbuf_tensor("i32b", [128, NT], I32))
    i32c = stack.enter_context(nc.sbuf_tensor("i32c", [128, NT], I32))
    rmask_sb = stack.enter_context(nc.sbuf_tensor("rmask_sb", [8, 1], F32))
    cmask_sb = stack.enter_context(nc.sbuf_tensor("cmask_sb", [8, 1], F32))
    offsrow_sb = stack.enter_context(nc.sbuf_tensor("offsrow_sb", [1, E], F32))
    offsb_sb = stack.enter_context(nc.sbuf_tensor("offsb_sb", [128, E], F32))
    pack_sb = stack.enter_context(nc.sbuf_tensor("pack_sb", [1, 1040], F32))
    gath_sb = stack.enter_context(nc.sbuf_tensor("gath_sb", [8, 1040], F32))
    carry_local = stack.enter_context(nc.sbuf_tensor("carry_local", [1, D], F32))
    carry0_sb = stack.enter_context(nc.sbuf_tensor("carry0_sb", [1, D], F32))
    carr32_sb = stack.enter_context(nc.sbuf_tensor("carr32_sb", [32, D], F32))
    carrb32_sb = stack.enter_context(nc.sbuf_tensor("carrb32_sb", [32, D], BF16))

    PSA = stack.enter_context(nc.psum_tensor("PSA", [128, 1024], F32))
    PSB = stack.enter_context(nc.psum_tensor("PSB", [128, 1024], F32))
    PSC = stack.enter_context(nc.psum_tensor("PSC", [128, 1024], F32))
    PSD = stack.enter_context(nc.psum_tensor("PSD", [128, 1024], F32))

    t_cwt = dma("gpsimd", qg, cwt_sb[:, :], cwt_in[:, :])
    t_ldec_sb = stack.enter_context(nc.sbuf_tensor("ldec_sb", [1, D], F32))
    ldec_sb = t_ldec_sb
    t_ldec = dma("sync", qs, ldec_sb[:, :], ldec_in[:, :])
    t_trikf = dma("sync", qs, trikf_sb[:, :], trikf_in[:, :])
    t_trikb = dma("sync", qs, trikb_sb[:, :], trikb_in[:, :])
    t_o128f = dma("sync", qs, o128f_sb[:, :], o128f_in[:, :])
    t_o128b = dma("sync", qs, o128b_sb[:, :], o128b_in[:, :])
    t_o1f = dma("sync", qs, o1x128f_sb[:, :], o1x128f_in[:, :])
    t_o1b = dma("sync", qs, o1x128b_sb[:, :], o1x128b_in[:, :])
    t_iog = dma("sync", qs, iotag_sb[:, :], iotag_in[:, :])

    t_sg0 = op("scalar", lambda e: e.activation(out=ldlog_sb[:, :], in_=ldec_sb[:, :], func=AF.Sigmoid),
               waits=[t_ldec])
    t_ldlog = op("scalar", lambda e: e.activation(out=ldlog_sb[:, :], in_=ldlog_sb[:, :], func=AF.Ln),
                 waits=[t_sg0])
    t_z3 = op("vector", lambda e: e.memset(carry_local[:, :], 0.0))

    a_ctx = contextlib.ExitStack()
    xnT = a_ctx.enter_context(nc.sbuf_tensor("xnT", [128, DC * NTOK], BF16))

    def xnT_tile(dc, i):
        return xnT[:, dc * NTOK + i * 128: dc * NTOK + (i + 1) * 128]

    if True:
        _a = a_ctx.enter_context
        bwt_sb = _a(nc.sbuf_tensor("bwt_sb", [128, DC * S], BF16))
        wg_sb = _a(nc.sbuf_tensor("wg_sb", [128, DC * E], BF16))
        xt0 = _a(nc.sbuf_tensor("xt0", [128, D], F32))
        xt1 = _a(nc.sbuf_tensor("xt1", [128, D], F32))
        xnb0 = _a(nc.sbuf_tensor("xnb0", [128, D], BF16))
        xnb1 = _a(nc.sbuf_tensor("xnb1", [128, D], BF16))
        pw_a = _a(nc.sbuf_tensor("pw_a", [128, D], F32))
        ld128_sb = _a(nc.sbuf_tensor("ld128_sb", [1, D], F32))
        d128_sb = _a(nc.sbuf_tensor("d128_sb", [128, D], F32))
        c_a = _a(nc.sbuf_tensor("c_a", [128, D], BF16))
        d12_all = _a(nc.sbuf_tensor("d12_all", [128, NT], F32))
        cnt_all = _a(nc.sbuf_tensor("cnt_all", [128, NT * E], F32))
        srt_sb = _a(nc.sbuf_tensor("srt_sb", [128, E], F32))
        s01_sb = _a(nc.sbuf_tensor("s01_sb", [128, E], F32))
        eqrow_sb = _a(nc.sbuf_tensor("eqrow_sb", [128, E], F32))
        e10row_sb = _a(nc.sbuf_tensor("e10row_sb", [128, E], F32))
        tokid_sb = _a(nc.sbuf_tensor("tokid_sb", [128, NT], I32))
        colrow0 = _a(nc.sbuf_tensor("colrow0", [1, D], F32))
        colrow1 = _a(nc.sbuf_tensor("colrow1", [1, D], F32))
        zeroi_sb = _a(nc.sbuf_tensor("zeroi_sb", [128, E * RB], I32))

        t_bwt = dma("gpsimd", qg, bwt_sb[:, :].rearrange("p (dc s) -> p dc s", dc=DC),
                     bwt_in.rearrange("(dc p) s -> p dc s", p=128))
        t_wg = dma("gpsimd", qg, wg_sb[:, :].rearrange("p (dc e) -> p dc e", dc=DC),
                    wg_in.rearrange("(dc p) e -> p dc e", p=128))
        t_cwt = t_bwt = t_wg = qg.tok()
        t_eq = dma("sync", qs, eqrow_sb[:, :], eqrow_in[:, :])
        t_e10 = dma("sync", qs, e10row_sb[:, :], e10row_in[:, :])
        t_tok = dma("sync", qs, tokid_sb[:, :], tokid_in[:, :])
        t_rm = dma("sync", qs, rmask_sb[:, :], rmask_in[:, :])
        t_cm = dma("sync", qs, cmask_sb[:, :], cmask_in[:, :])
        t_z1 = op("vector", lambda e: e.memset(zeroi_sb[:, :], 0))
        t_tid0 = dma("sync", qs, tidd2d, zeroi_sb[:, :], waits=[t_z1])

        xts = [xt0, xt1]
        xnbs = [xnb0, xnb1]
        xr = x_in.rearrange("(i p) d -> i p d", p=128)
        xndr = xnd.rearrange("(i p) d -> i p d", p=128)

        t_xload = [None] * NT
        t_xm = [None] * NT
        t_xnb = [None] * NT
        t_xnw = [None] * NT

        t_xload[0] = dma("sync", qs, xts[0][:, :], xr[0])
        if NT > 1:
            t_xload[1] = dma("sync", qs, xts[1][:, :], xr[1])
        for i in range(NT):
            b = i % 2
            t1 = op("vector", lambda e, b=b: e.bn_stats(stat_sb[:, 0:6], xts[b][:, 0:512]),
                    waits=[t_xload[i]])
            t2 = op("vector", lambda e, b=b: e.bn_stats(stat_sb[:, 6:12], xts[b][:, 512:1024]))
            t3 = op("vector", lambda e: e.bn_aggr(stat_sb[:, 12:14], stat_sb[:, 0:12]))
            t3b = op("vector", lambda e: e.tensor_scalar(stat_sb[:, 13:14], stat_sb[:, 13:14], float(EPS), None, ALU.add))
            t6 = op("scalar", lambda e: e.activation(out=stat_sb[:, 14:15], in_=stat_sb[:, 13:14], func=AF.Sqrt),
                    waits=[t3b])
            t7 = op("vector", lambda e: e.reciprocal(stat_sb[:, 14:15], stat_sb[:, 14:15]),
                    waits=[t6])
            t_xnb[i] = op("vector", lambda e, b=b: e.tensor_scalar(xnbs[b][:, :], xts[b][:, :],
                                                                   stat_sb[:, 12:13], stat_sb[:, 14:15],
                                                                   ALU.subtract, ALU.mult),
                          waits=[t_xnw[i - 2]] if i >= 2 else [])
            t_xm[i] = t_xnb[i]
            t_xnw[i] = dma("scalar", qc, xndr[i], xnbs[b][:, :], waits=[t_xnb[i]])
            if i + 2 < NT:
                t_xload[i + 2] = dma("sync", qs, xts[b][:, :], xr[i + 2], waits=[t_xnb[i]])

        t_xnall = qc.tok()
        t_T = []
        for dc in range(DC):
            t_T.append(dma("sync", qt, xnT[:, dc * NTOK:(dc + 1) * NTOK],
                           xnd[:, dc * 128:(dc + 1) * 128], waits=[t_xnall], transpose=True))

        t_logev = [None] * NT
        t_buev = [None] * (NT // 4)
        for g in range(NT // 4):
            for i in range(4 * g, 4 * g + 4):
                pslot = PSC[:, 0:8] if i % 2 == 0 else PSC[:, 512:520]
                w = (list(t_T) + [t_wg]) if i == 0 else []
                if i >= 2:
                    w = w + [t_logev[i - 2]]
                tpe = mmg([(pslot, xnT_tile(dc, i), wg_sb[:, dc * E:(dc + 1) * E],
                            dc == 0, dc == DC - 1) for dc in range(DC)], waits=w)
                t_logev[i] = op("vector", lambda e, i=i, pslot=pslot:
                                e.tensor_copy(out=logits_all[:, i * E:(i + 1) * E], in_=pslot),
                                waits=[tpe])
            pslot = PSD[0:64, 0:512] if g % 2 == 0 else PSD[0:64, 512:1024]
            w = (list(t_T) + [t_bwt]) if g == 0 else []
            if g >= 2:
                w = w + [t_buev[g - 2]]
            tpe = mmg([(pslot, bwt_sb[:, dc * S:(dc + 1) * S],
                        xnT[:, dc * NTOK + g * 512: dc * NTOK + (g + 1) * 512],
                        dc == 0, dc == DC - 1) for dc in range(DC)], waits=w)
            t_buev[g] = op("vector", lambda e, g=g, pslot=pslot:
                           e.tensor_copy(out=bu_sb[:, g * 512:(g + 1) * 512], in_=pslot),
                           waits=[tpe])

        t_l128 = op("vector", lambda e: e.tensor_scalar(ld128_sb[:, :], ldlog_sb[:, :], 128.0, None, ALU.mult),
                    waits=[t_ldlog])
        t_d128mm = mmg([(PSB[:, 0:512], o1x128f_sb[:, :], ld128_sb[:, 0:512], True, True),
                        (PSB[:, 512:1024], o1x128f_sb[:, :], ld128_sb[:, 512:1024], True, True)],
                       waits=[t_l128, t_o1f])
        t_d128 = op("scalar", lambda e: e.activation(out=d128_sb[:, :], in_=PSB[:, :], func=AF.Exp),
                    waits=[t_d128mm])
        t_pb = [None] * NT
        t_c = [None] * NT
        t_cs = [None] * NT
        t_cadd = [None] * NT
        t_pwv = [None] * NT
        for i in range(NT):
            pbt = PSA if i % 2 == 0 else PSC
            pwt = PSB if i % 2 == 0 else PSD
            w = [t_buev[i // 4], t_cwt]
            if i >= 2:
                w.append(t_c[i - 2])
            if i in (0, 1):
                w.append(t_logev[NT - 1])
            t_pb[i] = mmg(
                [(pbt[:, 0:512], bu_sb[0:64, i * 128:(i + 1) * 128], cwt_sb[:, 0:512], True, True),
                 (pbt[:, 512:1024], bu_sb[0:64, i * 128:(i + 1) * 128], cwt_sb[:, 512:1024], True, True)],
                waits=w)
            if i == 0:
                t_pw = mmg(
                    [(pwt[:, 0:512], iotag_sb[:, 0:128], ldlog_sb[:, 0:512], True, True),
                     (pwt[:, 512:1024], iotag_sb[:, 0:128], ldlog_sb[:, 512:1024], True, True)],
                    waits=[t_iog, t_ldlog, t_d128])
                t_pwv[i] = op("scalar", lambda e, p=pwt: e.activation(out=pw_a[:, :], in_=p[:, :], func=AF.Exp),
                              waits=[t_pw])
            else:
                t_pwv[i] = op("vector", lambda e: e.tensor_mul(pw_a[:, :], pw_a[:, :], d128_sb[:, :]),
                              waits=[t_c[i - 1], t_d128])
            t_c[i] = op("vector", lambda e, p=pbt: e.tensor_mul(c_a[:, :], p[:, :], pw_a[:, :]),
                        waits=[t_pwv[i], t_pb[i]] + ([t_cs[i - 1]] if i >= 1 else []))
            w = [t_c[i], t_o128b]
            if i == 0:
                w.append(t_pwv[0])
            if i == 1:
                w.append(t_buev[NT // 4 - 1])
            if i >= 2:
                w.append(t_cadd[i - 2])
            t_cs[i] = mmg(
                [(pwt[0:1, 0:512], o128b_sb[:, :], c_a[:, 0:512], True, True),
                 (pwt[0:1, 512:1024], o128b_sb[:, :], c_a[:, 512:1024], True, True)],
                waits=w)
            cr = [colrow0, colrow1][i % 2]
            tcp = op("scalar", lambda e, p=pwt, cr=cr: e.copy(out=cr[:, :], in_=p[0:1, 0:1024]),
                     waits=[t_cs[i]] + ([t_cadd[i - 2]] if i >= 2 else []))
            t_cadd[i] = dma("scalar", qc, carrd[i:i + 1, :], cr[:, :], waits=[tcp])
        t_c32 = dma("sync", qs, carr32_sb[:, :], carrd[:, :], waits=[t_cadd[NT - 1], t_cadd[NT - 2]])
        t_cl = mmg([(PSB[0:1, 0:512], o128f_sb[0:32, 0:1], carr32_sb[:, 0:512], True, True),
                    (PSB[0:1, 512:1024], o128f_sb[0:32, 0:1], carr32_sb[:, 512:1024], True, True)],
                   waits=[t_c32, t_o128f, t_cs[NT - 1], t_cs[NT - 2]])
        t_carry_done = op("vector", lambda e: e.tensor_copy(out=carry_local[:, :], in_=PSB[0:1, 0:1024]),
                          waits=[t_cl, t_z3])

        t_crow = [None] * NT
        t_crowev = [None] * NT
        t_cntev = [None] * NT
        t_oh1 = [None] * NT
        t_oh2 = [None] * NT
        t_d12 = [None] * NT
        for i in range(NT):
            lg = logits_all[:, i * E:(i + 1) * E]
            tsrt = op("vector", lambda e, lg=lg: e.max(srt_sb[:, :], lg), waits=[t_logev[i]])
            t_oh1[i] = op("vector", lambda e, lg=lg, i=i:
                          e.tensor_scalar(oh1_all[:, i * E:(i + 1) * E], lg, srt_sb[:, 0:1], None, ALU.is_equal))
            t_oh2[i] = op("vector", lambda e, lg=lg, i=i:
                          e.tensor_scalar(oh2_all[:, i * E:(i + 1) * E], lg, srt_sb[:, 1:2], None, ALU.is_equal))
            t_d12[i] = op("vector", lambda e, i=i:
                          e.tensor_scalar(d12_all[:, i:i + 1], srt_sb[:, 0:1], srt_sb[:, 1:2], None, ALU.subtract))
            ts01 = op("vector", lambda e, i=i:
                      e.tensor_add(s01_sb[:, :], oh1_all[:, i * E:(i + 1) * E], oh2_all[:, i * E:(i + 1) * E]))
            w = [ts01, t_trikf, t_o1f, t_cadd[NT - 1]]
            if i >= 1:
                w += [t_crowev[i - 1], t_cntev[i - 1]]
            if i == 0:
                w.append(t_cadd[NT - 1])
            mlist = [(PSA[:, 0:8], trikf_sb[:, :], s01_sb[:, :], True, i == 0)]
            if i >= 1:
                mlist.append((PSA[:, 0:8], o1x128f_sb[:, :], crow_sb[:, :], False, True))
            t_cnt = mmg(mlist, waits=w)
            t_cntev[i] = op("vector", lambda e, i=i: e.tensor_copy(out=cnt_all[:, i * E:(i + 1) * E], in_=PSA[:, 0:8]),
                            waits=[t_cnt])
            t_crow[i] = mmg([(PSB[0:1, 0:8], o128f_sb[:, :], s01_sb[:, :], i == 0, True)],
                            waits=[ts01, t_o128f] + ([t_cadd[NT - 1]] if i == 0 else [t_crowev[i - 1]]))
            t_crowev[i] = op("scalar", lambda e: e.copy(out=crow_sb[:, :], in_=PSB[0:1, 0:8]),
                             waits=[t_crow[i]])

        oh1_3 = oh1_all[:, :].rearrange("p (t e) -> p t e", e=E)
        oh2_3 = oh2_all[:, :].rearrange("p (t e) -> p t e", e=E)
        cnt_3 = cnt_all[:, :].rearrange("p (t e) -> p t e", e=E)
        tmp_3 = tmp256[:, :].rearrange("p (t e) -> p t e", e=E)
        op("vector", lambda e: e.tensor_mul(tmp_3, oh1_3, cnt_3), waits=[t_cntev[NT - 1]])
        t_r0 = op("vector", lambda e: e.reduce_sum(rank0_all[:, :].unsqueeze(2), tmp_3, axis=AXX))
        op("vector", lambda e: e.tensor_mul(tmp_3, oh2_3, cnt_3))
        t_r1 = op("vector", lambda e: e.reduce_sum(rank1_all[:, :].unsqueeze(2), tmp_3, axis=AXX))

        t_g1 = op("scalar", lambda e: e.activation(out=g1_all[:, :], in_=d12_all[:, :], func=AF.Sigmoid),
                  waits=[t_d12[NT - 1]])
        t_g2 = op("vector", lambda e: e.tensor_scalar(g2_all[:, :], g1_all[:, :], -1.0, 1.0, ALU.mult, ALU.add),
                  waits=[t_g1])

        eqrow_b = eqrow_sb[:, :].unsqueeze(1).broadcast_to([128, NT, E])
        t_dest = {}
        for (slot, oh_3x, rank_all_x, dall, pdall) in (
            (0, oh1_3, rank0_all, d20_all, pdest0_sb),
            (1, oh2_3, rank1_all, d21_all, pdest1_sb),
        ):
            rk = rank_all_x[:, :]
            op("vector", lambda e, oh_3x=oh_3x: e.tensor_mul(tmp_3, oh_3x, eqrow_b),
               waits=[t_eq, t_r0, t_r1])
            op("vector", lambda e: e.reduce_sum(t32a[:, :].unsqueeze(2), tmp_3, axis=AXX))
            op("vector", lambda e, rk=rk: e.tensor_add(t32b[:, :], t32a[:, :], rk))
            t_pd = op("vector", lambda e, pdall=pdall: e.tensor_copy(out=pdall[:, :], in_=t32b[:, :]))
            op("vector", lambda e, pdall=pdall: e.tensor_scalar(i32a[:, :], pdall[:, :], 7, None, ALU.arith_shift_right))
            op("vector", lambda e, rk=rk: e.tensor_copy(out=i32b[:, :], in_=rk))
            op("vector", lambda e: e.tensor_scalar(i32b[:, :], i32b[:, :], 127, None, ALU.bitwise_and))
            op("vector", lambda e: e.tensor_scalar(i32c[:, :], i32b[:, :], 6, None, ALU.logical_shift_left))
            op("vector", lambda e: e.tensor_scalar(i32b[:, :], i32b[:, :], 4, None, ALU.logical_shift_left))
            op("vector", lambda e: e.tensor_add(i32c[:, :], i32c[:, :], i32b[:, :]))
            t_d2 = op("vector", lambda e, dall=dall: e.tensor_add(dall[:, :], i32c[:, :], i32a[:, :]))
            t_dest[slot] = (t_d2, t_pd)
        t_last_meta = t_dest[1][0]

        t_tscat = []
        for i in range(NT):
            for (slot, dall) in ((0, d20_all), (1, d21_all)):
                t_tscat.append(idma(tidd, IndirectOffsetOnAxis(dall[:, i:i + 1], 1),
                                    tokid_sb[:, i:i + 1], None,
                                    waits=[t_dest[slot][0], t_tid0, t_tok]))
        t_tidld = dma("sync", qs, tidt_sb[:, :], tidd2d, waits=[t_tscat[-1]])

        tp1 = op("vector", lambda e: e.tensor_copy(out=pack_sb[:, 0:1024], in_=carry_local[:, :]),
                 waits=[t_carry_done])
        tp2 = op("vector", lambda e: e.tensor_copy(out=pack_sb[:, 1024:1032], in_=crow_sb[:, :]),
                 waits=[t_crowev[NT - 1]])
        t_ccw = dma("gpsimd", qg, ccin[:, :], pack_sb[:, :], waits=[tp2])
        cc.v += 1
        t_cc = cc.tok()

        def cc_fn(e, w=t_ccw, h=cc.h):
            e.wait_ge(w[0], w[1])
            e.collective_compute(
                "AllGather", ALU.bypass,
                replica_groups=[list(range(NCORE))],
                ins=[ccin.opt()], outs=[ccout.opt()],
            ).then_inc(h, 1)
        emit("gpsimd", cc_fn)
        t_ccr = dma("gpsimd", qg, gath_sb[:, :], ccout[:, :], waits=[t_cc])

        t_offs = mmg([(PSA[0:1, 0:8], rmask_sb[:, :], gath_sb[0:8, 1024:1032], True, True)],
                     waits=[t_ccr, t_rm, t_cntev[NT - 1]])
        t_offsev = op("vector", lambda e: e.tensor_copy(out=offsrow_sb[:, :], in_=PSA[0:1, 0:8]),
                      waits=[t_offs])
        t_offsb = mmg([(PSA[:, 0:8], o1x128f_sb[:, :], offsrow_sb[:, :], True, True)],
                      waits=[t_offsev])
        t_offsbev = op("vector", lambda e: e.tensor_copy(out=offsb_sb[:, :], in_=PSA[:, 0:8]),
                       waits=[t_offsb])
        t_c0 = mmg([(PSB[0:1, 0:512], cmask_sb[:, :], gath_sb[0:8, 0:512], True, True),
                    (PSB[0:1, 512:1024], cmask_sb[:, :], gath_sb[0:8, 512:1024], True, True)],
                   waits=[t_ccr, t_cm, t_crowev[NT - 1]])
        t_c0ev = op("vector", lambda e: e.tensor_copy(out=carry0_sb[:, :], in_=PSB[0:1, 0:1024]),
                    waits=[t_c0])
        t_carr = mmg([(PSC[0:32, 0:512], trikf_sb[0:32, 0:32], carr32_sb[:, 0:512], True, False),
                      (PSC[0:32, 0:512], o1x128f_sb[:, 0:32], carry0_sb[:, 0:512], False, True),
                      (PSC[0:32, 512:1024], trikf_sb[0:32, 0:32], carr32_sb[:, 512:1024], True, False),
                      (PSC[0:32, 512:1024], o1x128f_sb[:, 0:32], carry0_sb[:, 512:1024], False, True)],
                     waits=[t_c0ev, t_c[NT - 1]])
        t_carrev = op("vector", lambda e: e.tensor_copy(out=carrb32_sb[:, :], in_=PSC[0:32, 0:1024]),
                      waits=[t_carr])
        t_carrw = dma("scalar", qc, carrbd[:, :], carrb32_sb[:, :], waits=[t_carrev])

        offsb_b = offsb_sb[:, :].unsqueeze(1).broadcast_to([128, NT, E])
        t_gate = {}
        for (slot, oh_3x, rank_all_x, g_all_x, gate_all_x) in (
            (0, oh1_3, rank0_all, g1_all, gate0_all),
            (1, oh2_3, rank1_all, g2_all, gate1_all),
        ):
            rk = rank_all_x[:, :]
            op("vector", lambda e, oh_3x=oh_3x: e.tensor_mul(tmp_3, oh_3x, offsb_b),
               waits=[t_offsbev, t_g2])
            op("vector", lambda e: e.reduce_sum(t32a[:, :].unsqueeze(2), tmp_3, axis=AXX))
            op("vector", lambda e, rk=rk: e.tensor_add(t32b[:, :], t32a[:, :], rk))
            op("vector", lambda e: e.tensor_scalar(t32c[:, :], t32b[:, :], CAP - 0.5, None, ALU.is_lt))
            t_gate[slot] = op("vector", lambda e, g_all_x=g_all_x, gate_all_x=gate_all_x:
                              e.tensor_mul(gate_all_x[:, :], g_all_x[:, :], t32c[:, :]))

        if DEBUG:
            dma("sync", qs, dbg["logits"][:, :], logits_all[:, :], waits=[t_logev[NT - 1]])
            dma("sync", qs, dbg["carry"][:, :], carry_local[:, :], waits=[t_carry_done])
            dma("sync", qs, dbg["rank0"][:, :], rank0_all[:, :], waits=[t_r0])
            dma("sync", qs, dbg["rank1"][:, :], rank1_all[:, :], waits=[t_r1])
            dma("sync", qs, dbg["gate0"][:, :], gate0_all[:, :], waits=[t_gate[0]])
            dma("sync", qs, dbg["gate1"][:, :], gate1_all[:, :], waits=[t_gate[1]])
            dma("sync", qs, dbg["pdest0"][:, :], pdest0_sb[:, :], waits=[t_last_meta])
            dma("sync", qs, dbg["pdest1"][:, :], pdest1_sb[:, :], waits=[t_last_meta])
            dma("sync", qs, dbg["tidT"][:, :], tidt_sb[:, :], waits=[t_tidld])

        t_b0_psA_free = t_offsbev
        t_b0_psB_free = t_c0ev
        t_logits_psC_free = t_c[NT - 1]
        t_bu_psD_free = t_cadd[NT - 1]

    a_ctx.close()

    b_ctx = contextlib.ExitStack()
    w1p0 = b_ctx.enter_context(nc.sbuf_tensor("w1p0", [128, DC * 512], BF16))
    w1p1 = b_ctx.enter_context(nc.sbuf_tensor("w1p1", [128, DC * 512], BF16))
    w2_sb = b_ctx.enter_context(nc.sbuf_tensor("w2_sb", [128, HC * D], BF16))
    hT = b_ctx.enter_context(nc.sbuf_tensor("hT", [128, HC * HQ], BF16))
    packT = b_ctx.enter_context(nc.sbuf_tensor("packT", [128, DC * Q], BF16))
    rows0 = b_ctx.enter_context(nc.sbuf_tensor("rows0", [128, D], BF16))
    rows1 = b_ctx.enter_context(nc.sbuf_tensor("rows1", [128, D], BF16))
    rows2 = b_ctx.enter_context(nc.sbuf_tensor("rows2", [128, D], BF16))
    rows3 = b_ctx.enter_context(nc.sbuf_tensor("rows3", [128, D], BF16))
    ysb0 = b_ctx.enter_context(nc.sbuf_tensor("ysb0", [128, D], BF16))
    ysb1 = b_ctx.enter_context(nc.sbuf_tensor("ysb1", [128, D], BF16))
    b1t_sb = b_ctx.enter_context(nc.sbuf_tensor("b1t_sb", [128, E * HC], F32))
    b2_sb = b_ctx.enter_context(nc.sbuf_tensor("b2_sb", [1, 2 * D], BF16))
    w1ps = [w1p0, w1p1]
    rowss = [rows0, rows1, rows2, rows3]
    ysbs = [ysb0, ysb1]

    t_b1t = dma("sync", qs, b1t_sb[:, :], b1t_in[:, :], waits=[t_tidld])
    t_b2_e = [None] * E

    RCH = [(0, 512), (512, 128)]

    pydr = pyd.rearrange("(r p) d -> r p d", p=128)
    packdrs = [packd0.rearrange("(r p) d -> r p d", p=128),
               packd1.rearrange("(r p) d -> r p d", p=128)]
    packds = [packd0, packd1]

    t_packw = {}
    t_gelu_seq = []
    t_yev_all = []
    t_yw_all = []
    t_mm1_last = None
    t_mm2_last = None
    t_transp_done = None
    t_transp_prev = None
    t_hdump = None
    w1k = 0

    for e in range(E):
        wb2 = [t_tidld] if e == 0 else ([t_mm2_last] if e >= 2 else [])
        t_b2_e[e] = dma("gpsimd", qb2[e % 2], b2_sb[:, (e % 2) * D:(e % 2 + 1) * D],
                        b2_in[:, e * D:(e + 1) * D], waits=wb2)
        for rb in range(RB):
            k = e * RB + rb
            war = [t_packw[k - 4]] if k >= 4 else []
            tg_ = idma(rowss[k % 4][:, :], None, xnd,
                       IndirectOffsetOnAxis(tidt_sb[:, e * RB + rb:e * RB + rb + 1], 0),
                       waits=[t_tidld] + war)
            w = [tg_]
            if e >= 2:
                w.append(t_transp_prev)
            t_packw[k] = dma("sync", qs, packdrs[e % 2][rb], rowss[k % 4][:, :], waits=w)
        tts = []
        for dc in range(DC):
            w = [t_packw[e * RB + RB - 1]]
            if e >= 1:
                w.append(t_mm1_last)
            tts.append(dma("sync", qt, packT[:, dc * Q:(dc + 1) * Q],
                           packds[e % 2][:, dc * 128:(dc + 1) * 128], waits=w, transpose=True))
        t_transp_prev = t_transp_done
        t_transp_done = tts[-1]
        if DEBUG2:
            dma("sync", qs, dbg["packT2"].rearrange("(e p) c -> e p c", p=128)[e],
                packT[:, :], waits=[t_transp_done])

        t_w2_done = None
        w2_pieces = []
        for kpc in range(8):
            w = [t_mm2_last] if (e >= 1 and kpc == 0) else []
            if e == 0 and kpc == 0:
                w = [t_tidld]
            w2_pieces.append((kpc, w))

        for half in range(2):
            t_w1p = [None] * 8
            pbase = w1k

            def issue_w1_piece(pp, e=e, pbase=pbase, t_w1p=t_w1p):
                w = []
                k = pbase + pp
                if k >= 2:
                    w.append(t_gelu_seq[(k - 1) * 4 - 1])
                t_w1p[pp] = dma("gpsimd", qw1[k % 2],
                                w1ps[k % 2][:, :].rearrange("p (dc hh) -> p dc hh", dc=DC),
                                w1_in[e][:, pp * 512:(pp + 1) * 512].rearrange("(dc p) hh -> p dc hh", p=128),
                                waits=w)

            def issue_w2_pieces(n, e=e):
                nonlocal_t = None
                for _ in range(n):
                    if w2_pieces:
                        kpc, wp = w2_pieces.pop(0)
                        nonlocal_t = dma("gpsimd", qw2,
                                         w2_sb[:, kpc * 4 * D:(kpc + 1) * 4 * D].rearrange("p (hc d) -> p hc d", hc=4),
                                         w2_in[e][kpc * 512:(kpc + 1) * 512, :].rearrange("(hc p) d -> p hc d", p=128),
                                         waits=wp)
                return nonlocal_t

            for hc in range(HC):
                pp = hc // 4
                hcl = hc % 4
                if hc == 0:
                    issue_w1_piece(0)
                    if half == 0:
                        tw2 = issue_w2_pieces(2)
                        t_w2_done = tw2 or t_w2_done
                    issue_w1_piece(1)
                    if half == 0:
                        tw2 = issue_w2_pieces(1)
                        t_w2_done = tw2 or t_w2_done
                elif hc % 4 == 0 and hc // 4 + 1 < 8:
                    issue_w1_piece(hc // 4 + 1)
                    if half == 0:
                        tw2 = issue_w2_pieces(1)
                        t_w2_done = tw2 or t_w2_done
                w1buf = w1ps[(pbase + pp) % 2]
                m1k = len(t_gelu_seq)
                pm1 = PSA if m1k % 2 == 0 else PSC
                w = [t_w1p[pp], t_transp_done]
                if m1k >= 2:
                    w.append(t_gelu_seq[m1k - 2])
                elif m1k == 0:
                    w += [t_b0_psA_free]
                elif m1k == 1:
                    w += [t_logits_psC_free]
                if hc == 0 and t_mm2_last is not None:
                    w.append(t_mm2_last)
                gro = half * HQ
                mms = []
                for dc in range(DC):
                    lhs = w1buf[:, dc * 512 + hcl * 128: dc * 512 + (hcl + 1) * 128]
                    mms.append((pm1[:, 0:512], lhs, packT[:, dc * Q + gro: dc * Q + gro + 512],
                                dc == 0, dc == DC - 1))
                    mms.append((pm1[:, 512:640], lhs, packT[:, dc * Q + gro + 512: dc * Q + gro + 640],
                                dc == 0, dc == DC - 1))
                tmm = mmg(mms, waits=w)
                wg_ = [tmm, t_b1t]
                if hc == 0 and t_mm2_last is not None:
                    wg_.append(t_mm2_last)
                if hc == 0 and t_hdump is not None:
                    wg_.append(t_hdump)
                tg = op("scalar",
                        lambda ee, pm1=pm1, hc=hc, e=e:
                        ee.activation(out=hT[:, hc * HQ: hc * HQ + HQ],
                                      in_=pm1[:, 0:HQ], func=AF.Gelu_apprx_tanh,
                                      bias=b1t_sb[:, e * HC + hc: e * HC + hc + 1]),
                        waits=wg_)
                t_gelu_seq.append(tg)
            w1k += 8
            t_mm1_last = t_gelu_seq[-1]
            if DEBUG2:
                t_hdump = dma("sync", qs, dbg["hT2"].rearrange("(eh p) c -> eh p c", p=128)[e * 2 + half],
                              hT[:, :], waits=[t_mm1_last])

            for rbl in range(5):
                yk = len(t_yev_all)
                pm2 = PSB if yk % 2 == 0 else PSD
                grb = half * 5 + rbl
                w = []
                if rbl == 0:
                    w = [t_mm1_last, t_w2_done, t_b2_e[e], t_o1b]
                if yk >= 2:
                    w.append(t_yev_all[yk - 2])
                elif yk == 0:
                    w += [t_b0_psB_free]
                elif yk == 1:
                    w += [t_bu_psD_free]
                mms = []
                for hc in range(HC):
                    lhs = hT[:, hc * HQ + rbl * 128: hc * HQ + (rbl + 1) * 128]
                    mms.append((pm2[:, 0:512], lhs, w2_sb[:, hc * D: hc * D + 512],
                                hc == 0, False))
                    mms.append((pm2[:, 512:1024], lhs, w2_sb[:, hc * D + 512: (hc + 1) * D],
                                hc == 0, False))
                eo = (e % 2) * D
                mms.append((pm2[:, 0:512], o1x128b_sb[:, :], b2_sb[:, eo: eo + 512],
                            False, True))
                mms.append((pm2[:, 512:1024], o1x128b_sb[:, :], b2_sb[:, eo + 512: eo + D],
                            False, True))
                t_mm2_last = mmg(mms, waits=w)
                tv = op("vector",
                        lambda e_, pm2=pm2, yk=yk:
                        e_.tensor_copy(out=ysbs[yk % 2][:, :], in_=pm2[:, 0:1024]),
                        waits=[t_mm2_last] + ([t_yw_all[yk - 2]] if yk >= 2 else []))
                t_yev_all.append(tv)
                t_yw_all.append(dma("scalar", qc, pydr[e * RB + grb], ysbs[yk % 2][:, :], waits=[tv]))

    t_py_all = t_yw_all[-1]
    b_ctx.close()

    c_ctx = contextlib.ExitStack()
    if True:
        _c = c_ctx.enter_context
        xnT2 = _c(nc.sbuf_tensor("xnT2", [128, DC * NTOK], BF16))
        gwt_sb = _c(nc.sbuf_tensor("gwt_sb", [128, DC * D], BF16))
        gateb_sb = _c(nc.sbuf_tensor("gateb_sb", [1, D], BF16))
        pw_c = _c(nc.sbuf_tensor("pw_c", [128, D], F32))
        ctile0 = _c(nc.sbuf_tensor("ctile0", [1, D], BF16))
        ctile1 = _c(nc.sbuf_tensor("ctile1", [1, D], BF16))
        d128c_sb = _c(nc.sbuf_tensor("d128c_sb", [128, D], F32))
        ld128c_sb = _c(nc.sbuf_tensor("ld128c_sb", [1, D], F32))
        c_c = _c(nc.sbuf_tensor("c_c", [128, D], BF16))
        gate_c = _c(nc.sbuf_tensor("gate_c", [128, D], F32))
        sg2_c = _c(nc.sbuf_tensor("sg2_c", [128, D], F32))
        sg3_c = _c(nc.sbuf_tensor("sg3_c", [128, D], F32))
        s_c = _c(nc.sbuf_tensor("s_c", [128, D], F32))
        ya0 = _c(nc.sbuf_tensor("ya0", [128, D], BF16))
        ya1 = _c(nc.sbuf_tensor("ya1", [128, D], BF16))
        yb0 = _c(nc.sbuf_tensor("yb0", [128, D], BF16))
        yb1 = _c(nc.sbuf_tensor("yb1", [128, D], BF16))
        ysum_c = _c(nc.sbuf_tensor("ysum_c", [128, D], F32))
        xt0c = _c(nc.sbuf_tensor("xt0c", [128, D], F32))
        xt1c = _c(nc.sbuf_tensor("xt1c", [128, D], F32))
        osb0 = _c(nc.sbuf_tensor("osb0", [128, D], F32))
        osb1 = _c(nc.sbuf_tensor("osb1", [128, D], F32))
        yas = [ya0, ya1]
        ybs = [yb0, yb1]
        xtc = [xt0c, xt1c]
        osb = [osb0, osb1]
        xr = x_in.rearrange("(i p) d -> i p d", p=128)
        outr = out_io.rearrange("(i p) d -> i p d", p=128)

        t_bdrain = [t_py_all, t_mm2_last, t_yev_all[-1], t_gelu_seq[-1]]
        t_T2 = []
        for dc in range(DC):
            t_T2.append(dma("sync", qt, xnT2[:, dc * NTOK:(dc + 1) * NTOK],
                            xnd[:, dc * 128:(dc + 1) * 128],
                            waits=t_bdrain if dc == 0 else [], transpose=True))
        t_gwt = dma("gpsimd", qgw, gwt_sb[:, :].rearrange("p (dc e) -> p dc e", dc=DC),
                    gwt_in.rearrange("(dc p) e -> p dc e", p=128),
                    waits=t_bdrain)
        t_gateb = dma("gpsimd", qgw, gateb_sb[:, :], gateb_in[:, :], waits=[])

        def xnT2_tile(dc, i):
            return xnT2[:, dc * NTOK + i * 128: dc * NTOK + (i + 1) * 128]

        t_l128c = op("vector", lambda e: e.tensor_scalar(ld128c_sb[:, :], ldlog_sb[:, :], 128.0, None, ALU.mult))
        t_d128cmm = mmg([(PSB[:, 0:512], o1x128f_sb[:, :], ld128c_sb[:, 0:512], True, True),
                         (PSB[:, 512:1024], o1x128f_sb[:, :], ld128c_sb[:, 512:1024], True, True)],
                        waits=[t_l128c] + t_bdrain + [t_yev_all[-1]])
        t_d128c = op("scalar", lambda e: e.activation(out=d128c_sb[:, :], in_=PSB[:, :], func=AF.Exp),
                     waits=[t_d128cmm])

        ctiles = [ctile0, ctile1]
        t_ct = [None] * NT
        t_h2 = [None] * NT
        t_xl = [None] * NT
        t_out = [None] * NT
        t_oa = [None] * NT
        t_sgm = [None] * NT
        t_exp2 = [None] * NT
        t_c2 = [None] * NT
        t_s2 = [None] * NT
        t_ct[0] = dma("sync", qs, ctiles[0][:, :], carrbd[0:1, :], waits=[t_carrw] + t_bdrain)
        t_ct[1] = dma("sync", qs, ctiles[1][:, :], carrbd[1:2, :], waits=[])
        t_ysum = [None] * NT
        t_sil = [None] * NT
        t_sgmul = [None] * NT

        t_xl[0] = dma("sync", qs, xtc[0][:, :], xr[0], waits=t_bdrain)
        t_xl[1] = dma("sync", qs, xtc[1][:, :], xr[1])

        for i in range(NT):
            par = i % 2
            pbt = PSA if par == 0 else PSC
            pwt = PSB if par == 0 else PSD
            w = [t_cwt]
            if i >= 2:
                w += [t_c2[i - 2], t_s2[i - 2]]
            if i < 2:
                w += t_bdrain + [t_yev_all[-1]]
            t_pb2 = mmg(
                [(pbt[:, 0:512], bu_sb[0:64, i * 128:(i + 1) * 128], cwt_sb[:, 0:512], True, True),
                 (pbt[:, 512:1024], bu_sb[0:64, i * 128:(i + 1) * 128], cwt_sb[:, 512:1024], True, True)],
                waits=w)
            if i == 0:
                t_pw2 = mmg(
                    [(pwt[:, 0:512], iotag_sb[:, 0:128], ldlog_sb[:, 0:512], True, True),
                     (pwt[:, 512:1024], iotag_sb[:, 0:128], ldlog_sb[:, 512:1024], True, True)],
                    waits=[t_iog, t_ldlog, t_d128c])
                t_exp2[i] = op("scalar", lambda e, p=pwt: e.activation(out=pw_c[:, :], in_=p[:, :], func=AF.Exp),
                               waits=[t_pw2])
            else:
                t_exp2[i] = op("vector", lambda e: e.tensor_mul(pw_c[:, :], pw_c[:, :], d128c_sb[:, :]),
                               waits=[t_c2[i - 1], t_d128c])
            t_c2[i] = op("vector", lambda e, p=pbt: e.tensor_mul(c_c[:, :], p[:, :], pw_c[:, :]),
                         waits=[t_exp2[i], t_pb2] + ([t_h2[i - 1]] if i >= 1 else []))
            ctb = ctiles[i % 2]
            t_h2[i] = mmg(
                [(pbt[:, 0:512], trikb_sb[:, :], c_c[:, 0:512], True, False),
                 (pbt[:, 0:512], o1x128b_sb[:, :], ctb[:, 0:512], False, True),
                 (pbt[:, 512:1024], trikb_sb[:, :], c_c[:, 512:1024], True, False),
                 (pbt[:, 512:1024], o1x128b_sb[:, :], ctb[:, 512:1024], False, True)],
                waits=[t_c2[i], t_trikb, t_o1b, t_ct[i]])
            t_h = t_h2[i]
            if i + 2 < NT:
                t_ct[i + 2] = dma("sync", qs, ctb[:, :], carrbd[i + 2:i + 3, :], waits=[t_h2[i]])
            mms = []
            for dc in range(DC):
                mms.append((pwt[:, 0:512], xnT2_tile(dc, i), gwt_sb[:, dc * D: dc * D + 512],
                            dc == 0, False))
                mms.append((pwt[:, 512:1024], xnT2_tile(dc, i), gwt_sb[:, dc * D + 512: (dc + 1) * D],
                            dc == 0, False))
            mms.append((pwt[:, 0:512], o1x128b_sb[:, :], gateb_sb[:, 0:512], False, True))
            mms.append((pwt[:, 512:1024], o1x128b_sb[:, :], gateb_sb[:, 512:1024], False, True))
            w = [t_gwt, t_gateb, t_o1b]
            if i >= 2:
                w.append(t_sgm[i - 2])
            if i < 2:
                w.append(t_yev_all[-1])
            if i == 0:
                w.append(t_exp2[0])
            t_gmm = mmg(mms, waits=w + (list(t_T2) if i == 0 else []))
            t_sgm[i] = op("scalar", lambda e, p=pwt: e.activation(out=gate_c[:, :], in_=p[:, :], func=AF.Sigmoid),
                          waits=[t_gmm] + ([t_s2[i - 1]] if i >= 1 else []))
            t_s2[i] = op("vector", lambda e, p=pbt: e.tensor_mul(s_c[:, :], p[:, :], gate_c[:, :]),
                         waits=[t_sgm[i], t_h] + ([t_sgmul[i - 1]] if i >= 1 else []))
            if DEBUG:
                dma("scalar", qc, dbg["s"].rearrange("(i p) d -> i p d", p=128)[i], s_c[:, :],
                    waits=[t_s2[i]])
            w = [t_py_all]
            if i >= 2:
                w.append(t_ysum[i - 2])
            t_ya = idma(yas[par][:, :], None, pyd,
                        IndirectOffsetOnAxis(pdest0_sb[:, i:i + 1], 0), waits=w)
            t_yb = idma(ybs[par][:, :], None, pyd,
                        IndirectOffsetOnAxis(pdest1_sb[:, i:i + 1], 0), waits=[])
            tyb = op("vector", lambda e, par=par, i=i: e.tensor_scalar(sg2_c[:, :], ybs[par][:, :],
                                                                        gate1_all[:, i:i + 1], None, ALU.mult),
                     waits=[t_yb, t_gate[1]] + ([t_ysum[i - 1]] if i >= 1 else []))
            tya2 = op("vector", lambda e, par=par, i=i: e.tensor_scalar(ysum_c[:, :], yas[par][:, :],
                                                                        gate0_all[:, i:i + 1], None, ALU.mult),
                      waits=[t_ya, t_gate[0]] + ([t_sil[i - 1]] if i >= 1 else []))
            t_ysum[i] = op("vector", lambda e: e.tensor_add(ysum_c[:, :], ysum_c[:, :], sg2_c[:, :]),
                           waits=[tya2, tyb])
            tsg2 = op("scalar", lambda e: e.activation(out=sg3_c[:, :], in_=ysum_c[:, :], func=AF.Sigmoid),
                      waits=[t_ysum[i]] + ([t_sil[i - 1]] if i >= 1 else []))
            t_sil[i] = op("vector", lambda e: e.tensor_mul(ysum_c[:, :], ysum_c[:, :], sg3_c[:, :]),
                          waits=[tsg2])
            if DEBUG:
                dma("scalar", qc, dbg["g"].rearrange("(i p) d -> i p d", p=128)[i], ysum_c[:, :],
                    waits=[t_sil[i]])
            t_sgmul[i] = op("vector", lambda e: e.tensor_mul(s_c[:, :], s_c[:, :], ysum_c[:, :]),
                            waits=[t_sil[i], t_s2[i]])
            t_oa[i] = op("vector", lambda e, par=par: e.tensor_add(osb[par][:, :], s_c[:, :], xtc[par][:, :]),
                         waits=[t_sgmul[i], t_xl[i]] + ([t_out[i - 2]] if i >= 2 else []))
            t_out[i] = dma("sync", qs, outr[i], osb[par][:, :], waits=[t_oa[i]])
            if i + 2 < NT:
                t_xl[i + 2] = dma("sync", qs, xtc[par][:, :], xr[i + 2], waits=[t_oa[i]])

        if DEBUG2:
            dma("sync", qs, dbg["pyd2"][:, :], pyd[:, :], waits=[t_py_all])
            dma("sync", qs, dbg["xnd2"][:, :], xnd[:, :], waits=[t_py_all])
            dma("sync", qs, dbg["tidd2"][:, :], tidd[:, :], waits=[t_py_all])
            dma("sync", qs, dbg["carrbd2"][:, :], carrbd[:, :], waits=[t_py_all])
        final_qs = qs.tok()

        def f_final(e, tok=final_qs):
            e.wait_ge(tok[0], tok[1])
        emit("sync", f_final)
    c_ctx.close()

    with nc.Block() as block:
        @block.sync
        def _(e):
            for f in prog["sync"]:
                f(e)

        @block.scalar
        def _(e):
            for f in prog["scalar"]:
                f(e)

        @block.vector
        def _(e):
            for f in prog["vector"]:
                f(e)

        @block.tensor
        def _(e):
            for f in prog["tensor"]:
                f(e)

        @block.gpsimd
        def _(e):
            for f in prog["gpsimd"]:
                f(e)

    stack.close()
    return nc


def _consts():
    bf = ml_dtypes.bfloat16
    c = {}
    c["trikf"] = np.triu(np.ones((128, 128), np.float32), 1)
    c["trikb"] = c["trikf"].astype(bf)
    c["o128f"] = np.ones((128, 1), np.float32)
    c["o128b"] = c["o128f"].astype(bf)
    c["o1x128f"] = np.ones((1, 128), np.float32)
    c["o1x128b"] = c["o1x128f"].astype(bf)
    c["eqrow"] = np.ascontiguousarray(np.tile((np.arange(E) * Q).astype(np.float32), (128, 1)))
    c["e10row"] = np.ascontiguousarray(np.tile((np.arange(E) * RB).astype(np.float32), (128, 1)))
    c["tokid"] = np.ascontiguousarray(np.arange(NTOK, dtype=np.int32).reshape(NT, 128).T)
    return c


def _host_prep(L0_attn_x, L0_attn_B_w, L0_attn_C_w, L0_attn_gate_w, L0_attn_gate_b,
               L0_attn_log_decay, L0_ffn_Wg, L0_ffn_W1, L0_ffn_b1, L0_ffn_W2, L0_ffn_b2):
    x = np.ascontiguousarray(np.asarray(L0_attn_x, dtype=np.float32)).reshape(NCORE * NTOK, D)
    shared = {
        "w1": np.ascontiguousarray(np.asarray(L0_ffn_W1, dtype=np.float32)),
        "w2": np.ascontiguousarray(np.asarray(L0_ffn_W2, dtype=np.float32)),
        "wg": np.ascontiguousarray(np.asarray(L0_ffn_Wg, dtype=np.float32)),
        "bwt": np.ascontiguousarray(np.asarray(L0_attn_B_w, dtype=np.float32).T),
        "cwt": np.ascontiguousarray(np.asarray(L0_attn_C_w, dtype=np.float32).T),
        "gwt": np.ascontiguousarray(np.asarray(L0_attn_gate_w, dtype=np.float32).T),
        "gateb": np.asarray(L0_attn_gate_b, dtype=np.float32).reshape(1, D),
        "ldec": np.asarray(L0_attn_log_decay, dtype=np.float32).reshape(1, D),
        "b1t": np.ascontiguousarray(
            np.asarray(L0_ffn_b1, dtype=np.float32).reshape(E, HC, 128).transpose(2, 0, 1).reshape(128, E * HC)),
        "b2": np.ascontiguousarray(np.asarray(L0_ffn_b2, dtype=np.float32).reshape(1, E * D)),
    }
    shared.update(_consts())
    in_maps = []
    for c in range(NCORE):
        m = dict(shared)
        m["x"] = np.ascontiguousarray(x[c * NTOK:(c + 1) * NTOK])
        m["iotag"] = ((c % 2) * NTOK + np.arange(128, dtype=np.float32)).reshape(1, 128)
        m["rmask"] = (np.arange(8) < c).astype(np.float32).reshape(8, 1)
        m["cmask"] = (np.eye(8, dtype=np.float32)[c - 1].reshape(8, 1)
                      if c % 2 == 1 else np.zeros((8, 1), np.float32))
        in_maps.append(m)
    return in_maps


def kernel(**inputs):
    in_maps = _host_prep(**inputs)
    nc = build_nc()
    res = run_bass_kernel_spmd(nc, in_maps, core_ids=list(range(NCORE)))
    out = np.concatenate([np.asarray(r["out"]) for r in res.results], axis=0)
    return out.reshape(4, 8192, 1024)


# version 09
# speedup vs baseline: 1.0904x; 1.0740x over previous
import numpy as np
import ml_dtypes
import contextlib
import concourse.bass as bass
import concourse.mybir as mybir
from concourse.bass import IndirectOffsetOnAxis
from concourse.bass_utils import run_bass_kernel_spmd

F32 = mybir.dt.float32
BF16 = mybir.dt.bfloat16
I32 = mybir.dt.int32
AF = mybir.ActivationFunctionType
ALU = mybir.AluOpType
AXX = mybir.AxisListType.X

NCORE = 8
NTOK = 4096
NT = 32
D = 1024
DC = 8
S = 64
E = 8
H = 4096
HC = 32
Q = 1280
RB = 10
HQ = 640
CAP = 8602
EPS = 1e-5

ENGINES = ("sync", "scalar", "vector", "tensor", "gpsimd")

DEBUG = False
DEBUG2 = False


class TSem:
    __slots__ = ("h", "v", "name")

    def __init__(self, h, name):
        self.h = h
        self.v = 0
        self.name = name

    def bump(self, n):
        self.v += n
        return (self.h, self.v)

    def tok(self):
        return (self.h, self.v)


def build_nc():
    nc = bass.Bass("TRN2", target_bir_lowering=False)

    x_in = nc.dram_tensor("x", [NTOK, D], F32, kind="ExternalInput").ap()
    w1_in = nc.dram_tensor("w1", [E, D, H], F32, kind="ExternalInput").ap()
    w2_in = nc.dram_tensor("w2", [E, H, D], F32, kind="ExternalInput").ap()
    wg_in = nc.dram_tensor("wg", [D, E], F32, kind="ExternalInput").ap()
    bwt_in = nc.dram_tensor("bwt", [D, S], F32, kind="ExternalInput").ap()
    cwt_in = nc.dram_tensor("cwt", [S, D], F32, kind="ExternalInput").ap()
    gwt_in = nc.dram_tensor("gwt", [D, D], F32, kind="ExternalInput").ap()
    gateb_in = nc.dram_tensor("gateb", [1, D], F32, kind="ExternalInput").ap()
    ldec_in = nc.dram_tensor("ldec", [1, D], F32, kind="ExternalInput").ap()
    b1t_in = nc.dram_tensor("b1t", [128, E * HC], F32, kind="ExternalInput").ap()
    b2_in = nc.dram_tensor("b2", [1, E * D], F32, kind="ExternalInput").ap()
    trikf_in = nc.dram_tensor("trikf", [128, 128], F32, kind="ExternalInput").ap()
    trikb_in = nc.dram_tensor("trikb", [128, 128], BF16, kind="ExternalInput").ap()
    o128f_in = nc.dram_tensor("o128f", [128, 1], F32, kind="ExternalInput").ap()
    o128b_in = nc.dram_tensor("o128b", [128, 1], BF16, kind="ExternalInput").ap()
    o1x128f_in = nc.dram_tensor("o1x128f", [1, 128], F32, kind="ExternalInput").ap()
    o1x128b_in = nc.dram_tensor("o1x128b", [1, 128], BF16, kind="ExternalInput").ap()
    eqrow_in = nc.dram_tensor("eqrow", [128, E], F32, kind="ExternalInput").ap()
    e10row_in = nc.dram_tensor("e10row", [128, E], F32, kind="ExternalInput").ap()
    tokid_in = nc.dram_tensor("tokid", [128, NT], I32, kind="ExternalInput").ap()
    iotag_in = nc.dram_tensor("iotag", [1, 128], F32, kind="ExternalInput").ap()
    rmask_in = nc.dram_tensor("rmask", [8, 1], F32, kind="ExternalInput").ap()
    cmask_in = nc.dram_tensor("cmask", [8, 1], F32, kind="ExternalInput").ap()
    out_io = nc.dram_tensor("out", [NTOK, D], F32, kind="ExternalOutput").ap()

    dbg = {}
    if DEBUG2:
        dbg["pyd2"] = nc.dram_tensor("dbg_pyd2", [E * Q, D], BF16, kind="ExternalOutput").ap()
        dbg["xnd2"] = nc.dram_tensor("dbg_xnd2", [NTOK, D], BF16, kind="ExternalOutput").ap()
        dbg["tidd2"] = nc.dram_tensor("dbg_tidd2", [1, 128 * E * RB], I32, kind="ExternalOutput").ap()
        dbg["carrbd2"] = nc.dram_tensor("dbg_carrbd2", [NT, D], BF16, kind="ExternalOutput").ap()
        dbg["packT2"] = nc.dram_tensor("dbg_packT2", [E * 128, DC * Q], BF16, kind="ExternalOutput").ap()
        dbg["hT2"] = nc.dram_tensor("dbg_hT2", [E * 2 * 128, HC * HQ], BF16, kind="ExternalOutput").ap()
    if DEBUG:
        dbg["logits"] = nc.dram_tensor("dbg_logits", [128, NT * E], F32, kind="ExternalOutput").ap()
        dbg["carry"] = nc.dram_tensor("dbg_carry", [1, D], F32, kind="ExternalOutput").ap()
        dbg["rank0"] = nc.dram_tensor("dbg_rank0", [128, NT], F32, kind="ExternalOutput").ap()
        dbg["rank1"] = nc.dram_tensor("dbg_rank1", [128, NT], F32, kind="ExternalOutput").ap()
        dbg["gate0"] = nc.dram_tensor("dbg_gate0", [128, NT], F32, kind="ExternalOutput").ap()
        dbg["gate1"] = nc.dram_tensor("dbg_gate1", [128, NT], F32, kind="ExternalOutput").ap()
        dbg["pdest0"] = nc.dram_tensor("dbg_pdest0", [128, NT], I32, kind="ExternalOutput").ap()
        dbg["pdest1"] = nc.dram_tensor("dbg_pdest1", [128, NT], I32, kind="ExternalOutput").ap()
        dbg["tidT"] = nc.dram_tensor("dbg_tidT", [128, E * RB], I32, kind="ExternalOutput").ap()
        dbg["gateT"] = nc.dram_tensor("dbg_gateT", [128, E * RB], F32, kind="ExternalOutput").ap()
        dbg["s"] = nc.dram_tensor("dbg_s", [NTOK, D], F32, kind="ExternalOutput").ap()
        dbg["g"] = nc.dram_tensor("dbg_g", [NTOK, D], F32, kind="ExternalOutput").ap()

    xnd = nc.dram_tensor("xnd", [NTOK, D], BF16).ap()
    packd0 = nc.dram_tensor("packd0", [Q, D], BF16).ap()
    packd1 = nc.dram_tensor("packd1", [Q, D], BF16).ap()
    pyd = nc.dram_tensor("pyd", [E * Q, D], BF16).ap()
    tidd = nc.dram_tensor("tidd", [1, 128 * E * RB], I32).ap()
    tidd2d = tidd.rearrange("o (p c) -> (o p) c", p=128)
    carrd = nc.dram_tensor("carrd", [NT, D], F32).ap()
    carrbd = nc.dram_tensor("carrbd", [NT, D], BF16).ap()
    ccin = nc.dram_tensor("ccin", [1, 1040], F32).ap()
    ccout = nc.dram_tensor("ccout", [8, 1040], F32).ap()

    prog = {k: [] for k in ENGINES}

    def emit(eng, fn):
        prog[eng].append(fn)

    stack = contextlib.ExitStack()

    def sem(name):
        return TSem(stack.enter_context(nc.semaphore(name)), name)

    qs = sem("qs")
    qc = sem("qc")
    qt = sem("qt")
    qgi = sem("qgi")
    qw1 = [sem("qw1a"), sem("qw1b")]
    qw2 = sem("qw2")
    qb2 = [sem("qb2a"), sem("qb2b")]
    qgw = sem("qgw")
    qg = sem("qg")
    vs = sem("vs")
    ss = sem("ss")
    ps = sem("ps")
    cc = sem("cc")
    ge = sem("ge")
    last_v = [None]
    last_s = [None]

    def dma(queue, qsem, out, in_, waits=(), **kw):
        tok = qsem.bump(16)
        wl = tuple(t for t in waits if t is not None)

        def f(e, out=out, in_=in_, wl=wl, kw=kw, h=qsem.h):
            for (sh, v) in wl:
                e.wait_ge(sh, v)
            e.dma_start(out=out, in_=in_, **kw).then_inc(h, 16)
        emit(queue, f)
        return tok

    def idma(out, out_off, in_, in_off, waits=()):
        tok = qgi.bump(16)
        wl = tuple(t for t in waits if t is not None)

        def f(e, out=out, oo=out_off, in_=in_, io=in_off, wl=wl, h=qgi.h):
            for (sh, v) in wl:
                e.wait_ge(sh, v)
            e.indirect_dma_start(out=out, out_offset=oo, in_=in_, in_offset=io).then_inc(h, 16)
        emit("gpsimd", f)
        return tok

    def op(eng, build, waits=()):
        esem = vs if eng == "vector" else ss
        last = last_v if eng == "vector" else last_s
        wl = [t for t in waits if t is not None]
        if last[0] is not None:
            wl.append(last[0])
        tok = esem.bump(1)
        wl = tuple(wl)

        def f(e, build=build, wl=wl, h=esem.h):
            for (sh, v) in wl:
                e.wait_ge(sh, v)
            build(e).then_inc(h, 1)
        emit(eng, f)
        last[0] = tok
        return tok

    def mmg(mms, waits=()):
        tok = ps.bump(1)
        wl = tuple(t for t in waits if t is not None)

        def f(e, mms=mms, wl=wl, h=ps.h):
            for (sh, v) in wl:
                e.wait_ge(sh, v)
            for (o, l, r, st, sp) in mms:
                ins = e.matmul(o, l, r, start=st, stop=sp)
            ins.then_inc(h, 1)
        emit("tensor", f)
        return tok

    logits_all = stack.enter_context(nc.sbuf_tensor("logits_all", [128, NT * E], F32))
    bu_sb = stack.enter_context(nc.sbuf_tensor("bu_sb", [64, NTOK], BF16))
    cwt_sb = stack.enter_context(nc.sbuf_tensor("cwt_sb", [64, D], BF16))
    ldlog_sb = stack.enter_context(nc.sbuf_tensor("ldlog_sb", [1, D], F32))
    trikf_sb = stack.enter_context(nc.sbuf_tensor("trikf_sb", [128, 128], F32))
    trikb_sb = stack.enter_context(nc.sbuf_tensor("trikb_sb", [128, 128], BF16))
    o128f_sb = stack.enter_context(nc.sbuf_tensor("o128f_sb", [128, 1], F32))
    o128b_sb = stack.enter_context(nc.sbuf_tensor("o128b_sb", [128, 1], BF16))
    o1x128f_sb = stack.enter_context(nc.sbuf_tensor("o1x128f_sb", [1, 128], F32))
    o1x128b_sb = stack.enter_context(nc.sbuf_tensor("o1x128b_sb", [1, 128], BF16))
    iotag_sb = stack.enter_context(nc.sbuf_tensor("iotag_sb", [1, 128], F32))
    pdest0_sb = stack.enter_context(nc.sbuf_tensor("pdest0_sb", [128, NT], I32))
    pdest1_sb = stack.enter_context(nc.sbuf_tensor("pdest1_sb", [128, NT], I32))
    crow_sb = stack.enter_context(nc.sbuf_tensor("crow_sb", [1, E], F32))
    tidt_sb = stack.enter_context(nc.sbuf_tensor("tidt_sb", [128, E * RB], I32))
    stat_sb = stack.enter_context(nc.sbuf_tensor("stat_sb", [128, 16], F32))
    oh1_all = stack.enter_context(nc.sbuf_tensor("oh1_all", [128, NT * E], F32))
    oh2_all = stack.enter_context(nc.sbuf_tensor("oh2_all", [128, NT * E], F32))
    rank0_all = stack.enter_context(nc.sbuf_tensor("rank0_all", [128, NT], F32))
    rank1_all = stack.enter_context(nc.sbuf_tensor("rank1_all", [128, NT], F32))
    g1_all = stack.enter_context(nc.sbuf_tensor("g1_all", [128, NT], F32))
    g2_all = stack.enter_context(nc.sbuf_tensor("g2_all", [128, NT], F32))
    gate0_all = stack.enter_context(nc.sbuf_tensor("gate0_all", [128, NT], F32))
    gate1_all = stack.enter_context(nc.sbuf_tensor("gate1_all", [128, NT], F32))
    d20_all = stack.enter_context(nc.sbuf_tensor("d20_all", [128, NT], I32))
    d21_all = stack.enter_context(nc.sbuf_tensor("d21_all", [128, NT], I32))
    tmp256 = stack.enter_context(nc.sbuf_tensor("tmp256", [128, NT * E], F32))
    t32a = stack.enter_context(nc.sbuf_tensor("t32a", [128, NT], F32))
    t32b = stack.enter_context(nc.sbuf_tensor("t32b", [128, NT], F32))
    t32c = stack.enter_context(nc.sbuf_tensor("t32c", [128, NT], F32))
    i32a = stack.enter_context(nc.sbuf_tensor("i32a", [128, NT], I32))
    i32b = stack.enter_context(nc.sbuf_tensor("i32b", [128, NT], I32))
    i32c = stack.enter_context(nc.sbuf_tensor("i32c", [128, NT], I32))
    rmask_sb = stack.enter_context(nc.sbuf_tensor("rmask_sb", [8, 1], F32))
    cmask_sb = stack.enter_context(nc.sbuf_tensor("cmask_sb", [8, 1], F32))
    offsrow_sb = stack.enter_context(nc.sbuf_tensor("offsrow_sb", [1, E], F32))
    offsb_sb = stack.enter_context(nc.sbuf_tensor("offsb_sb", [128, E], F32))
    pack_sb = stack.enter_context(nc.sbuf_tensor("pack_sb", [1, 1040], F32))
    gath_sb = stack.enter_context(nc.sbuf_tensor("gath_sb", [8, 1040], F32))
    carry_local = stack.enter_context(nc.sbuf_tensor("carry_local", [1, D], F32))
    carry0_sb = stack.enter_context(nc.sbuf_tensor("carry0_sb", [1, D], F32))
    carr32_sb = stack.enter_context(nc.sbuf_tensor("carr32_sb", [32, D], F32))
    carrb32_sb = stack.enter_context(nc.sbuf_tensor("carrb32_sb", [32, D], BF16))

    PSA = stack.enter_context(nc.psum_tensor("PSA", [128, 1024], F32))
    PSB = stack.enter_context(nc.psum_tensor("PSB", [128, 1024], F32))
    PSC = stack.enter_context(nc.psum_tensor("PSC", [128, 1024], F32))
    PSD = stack.enter_context(nc.psum_tensor("PSD", [128, 1024], F32))

    t_cwt = dma("gpsimd", qg, cwt_sb[:, :], cwt_in[:, :])
    t_ldec_sb = stack.enter_context(nc.sbuf_tensor("ldec_sb", [1, D], F32))
    ldec_sb = t_ldec_sb
    t_ldec = dma("sync", qs, ldec_sb[:, :], ldec_in[:, :])
    t_trikf = dma("sync", qs, trikf_sb[:, :], trikf_in[:, :])
    t_trikb = dma("sync", qs, trikb_sb[:, :], trikb_in[:, :])
    t_o128f = dma("sync", qs, o128f_sb[:, :], o128f_in[:, :])
    t_o128b = dma("sync", qs, o128b_sb[:, :], o128b_in[:, :])
    t_o1f = dma("sync", qs, o1x128f_sb[:, :], o1x128f_in[:, :])
    t_o1b = dma("sync", qs, o1x128b_sb[:, :], o1x128b_in[:, :])
    t_iog = dma("sync", qs, iotag_sb[:, :], iotag_in[:, :])

    t_sg0 = op("scalar", lambda e: e.activation(out=ldlog_sb[:, :], in_=ldec_sb[:, :], func=AF.Sigmoid),
               waits=[t_ldec])
    t_ldlog = op("scalar", lambda e: e.activation(out=ldlog_sb[:, :], in_=ldlog_sb[:, :], func=AF.Ln),
                 waits=[t_sg0])
    t_z3 = op("vector", lambda e: e.memset(carry_local[:, :], 0.0))

    a_ctx = contextlib.ExitStack()
    xnT = a_ctx.enter_context(nc.sbuf_tensor("xnT", [128, DC * NTOK], BF16))

    def xnT_tile(dc, i):
        return xnT[:, dc * NTOK + i * 128: dc * NTOK + (i + 1) * 128]

    if True:
        _a = a_ctx.enter_context
        bwt_sb = _a(nc.sbuf_tensor("bwt_sb", [128, DC * S], BF16))
        wg_sb = _a(nc.sbuf_tensor("wg_sb", [128, DC * E], BF16))
        xt0 = _a(nc.sbuf_tensor("xt0", [128, D], F32))
        xt1 = _a(nc.sbuf_tensor("xt1", [128, D], F32))
        xnb0 = _a(nc.sbuf_tensor("xnb0", [128, D], BF16))
        xnb1 = _a(nc.sbuf_tensor("xnb1", [128, D], BF16))
        pw_a = _a(nc.sbuf_tensor("pw_a", [128, D], F32))
        ld128_sb = _a(nc.sbuf_tensor("ld128_sb", [1, D], F32))
        d128_sb = _a(nc.sbuf_tensor("d128_sb", [128, D], F32))
        c_a = _a(nc.sbuf_tensor("c_a", [128, D], BF16))
        d12_all = _a(nc.sbuf_tensor("d12_all", [128, NT], F32))
        cnt_all = _a(nc.sbuf_tensor("cnt_all", [128, NT * E], F32))
        srt_sb = _a(nc.sbuf_tensor("srt_sb", [128, E], F32))
        s01_sb = _a(nc.sbuf_tensor("s01_sb", [128, E], F32))
        eqrow_sb = _a(nc.sbuf_tensor("eqrow_sb", [128, E], F32))
        e10row_sb = _a(nc.sbuf_tensor("e10row_sb", [128, E], F32))
        tokid_sb = _a(nc.sbuf_tensor("tokid_sb", [128, NT], I32))
        colrow0 = _a(nc.sbuf_tensor("colrow0", [1, D], F32))
        colrow1 = _a(nc.sbuf_tensor("colrow1", [1, D], F32))
        zeroi_sb = _a(nc.sbuf_tensor("zeroi_sb", [128, E * RB], I32))

        t_bwt = dma("gpsimd", qg, bwt_sb[:, :].rearrange("p (dc s) -> p dc s", dc=DC),
                     bwt_in.rearrange("(dc p) s -> p dc s", p=128))
        t_wg = dma("gpsimd", qg, wg_sb[:, :].rearrange("p (dc e) -> p dc e", dc=DC),
                    wg_in.rearrange("(dc p) e -> p dc e", p=128))
        t_cwt = t_bwt = t_wg = qg.tok()
        t_eq = dma("sync", qs, eqrow_sb[:, :], eqrow_in[:, :])
        t_e10 = dma("sync", qs, e10row_sb[:, :], e10row_in[:, :])
        t_tok = dma("sync", qs, tokid_sb[:, :], tokid_in[:, :])
        t_rm = dma("sync", qs, rmask_sb[:, :], rmask_in[:, :])
        t_cm = dma("sync", qs, cmask_sb[:, :], cmask_in[:, :])
        t_z1 = op("vector", lambda e: e.memset(zeroi_sb[:, :], 0))
        t_tid0 = dma("sync", qs, tidd2d, zeroi_sb[:, :], waits=[t_z1])

        xts = [xt0, xt1]
        xnbs = [xnb0, xnb1]
        xr = x_in.rearrange("(i p) d -> i p d", p=128)
        xndr = xnd.rearrange("(i p) d -> i p d", p=128)

        t_xload = [None] * NT
        t_xm = [None] * NT
        t_xnb = [None] * NT
        t_xnw = [None] * NT

        t_xload[0] = dma("sync", qs, xts[0][:, :], xr[0])
        if NT > 1:
            t_xload[1] = dma("sync", qs, xts[1][:, :], xr[1])
        for i in range(NT):
            b = i % 2
            t1 = op("vector", lambda e, b=b: e.bn_stats(stat_sb[:, 0:6], xts[b][:, 0:512]),
                    waits=[t_xload[i]])
            t2 = op("vector", lambda e, b=b: e.bn_stats(stat_sb[:, 6:12], xts[b][:, 512:1024]))
            t3 = op("vector", lambda e: e.bn_aggr(stat_sb[:, 12:14], stat_sb[:, 0:12]))
            t3b = op("vector", lambda e: e.tensor_scalar(stat_sb[:, 13:14], stat_sb[:, 13:14], float(EPS), None, ALU.add))
            t6 = op("scalar", lambda e: e.activation(out=stat_sb[:, 14:15], in_=stat_sb[:, 13:14], func=AF.Sqrt),
                    waits=[t3b])
            t7 = op("vector", lambda e: e.reciprocal(stat_sb[:, 14:15], stat_sb[:, 14:15]),
                    waits=[t6])
            t_xnb[i] = op("vector", lambda e, b=b: e.tensor_scalar(xnbs[b][:, :], xts[b][:, :],
                                                                   stat_sb[:, 12:13], stat_sb[:, 14:15],
                                                                   ALU.subtract, ALU.mult),
                          waits=[t_xnw[i - 2]] if i >= 2 else [])
            t_xm[i] = t_xnb[i]
            t_xnw[i] = dma("scalar", qc, xndr[i], xnbs[b][:, :], waits=[t_xnb[i]])
            if i + 2 < NT:
                t_xload[i + 2] = dma("sync", qs, xts[b][:, :], xr[i + 2], waits=[t_xnb[i]])

        t_xnall = qc.tok()
        t_T = []
        for dc in range(DC):
            t_T.append(dma("sync", qt, xnT[:, dc * NTOK:(dc + 1) * NTOK],
                           xnd[:, dc * 128:(dc + 1) * 128], waits=[t_xnall], transpose=True))

        t_logev = [None] * NT
        t_buev = [None] * (NT // 4)
        for g in range(NT // 4):
            for i in range(4 * g, 4 * g + 4):
                pslot = PSC[:, 0:8] if i % 2 == 0 else PSC[:, 512:520]
                w = (list(t_T) + [t_wg]) if i == 0 else []
                if i >= 2:
                    w = w + [t_logev[i - 2]]
                tpe = mmg([(pslot, xnT_tile(dc, i), wg_sb[:, dc * E:(dc + 1) * E],
                            dc == 0, dc == DC - 1) for dc in range(DC)], waits=w)
                t_logev[i] = op("vector", lambda e, i=i, pslot=pslot:
                                e.tensor_copy(out=logits_all[:, i * E:(i + 1) * E], in_=pslot),
                                waits=[tpe])
            pslot = PSD[0:64, 0:512] if g % 2 == 0 else PSD[0:64, 512:1024]
            w = (list(t_T) + [t_bwt]) if g == 0 else []
            if g >= 2:
                w = w + [t_buev[g - 2]]
            tpe = mmg([(pslot, bwt_sb[:, dc * S:(dc + 1) * S],
                        xnT[:, dc * NTOK + g * 512: dc * NTOK + (g + 1) * 512],
                        dc == 0, dc == DC - 1) for dc in range(DC)], waits=w)
            t_buev[g] = op("vector", lambda e, g=g, pslot=pslot:
                           e.tensor_copy(out=bu_sb[:, g * 512:(g + 1) * 512], in_=pslot),
                           waits=[tpe])

        t_l128 = op("vector", lambda e: e.tensor_scalar(ld128_sb[:, :], ldlog_sb[:, :], 128.0, None, ALU.mult),
                    waits=[t_ldlog])
        t_d128mm = mmg([(PSB[:, 0:512], o1x128f_sb[:, :], ld128_sb[:, 0:512], True, True),
                        (PSB[:, 512:1024], o1x128f_sb[:, :], ld128_sb[:, 512:1024], True, True)],
                       waits=[t_l128, t_o1f])
        t_d128 = op("scalar", lambda e: e.activation(out=d128_sb[:, :], in_=PSB[:, :], func=AF.Exp),
                    waits=[t_d128mm])
        t_pb = [None] * NT
        t_c = [None] * NT
        t_cs = [None] * NT
        t_cadd = [None] * NT
        t_pwv = [None] * NT
        for i in range(NT):
            pbt = PSA if i % 2 == 0 else PSC
            pwt = PSB if i % 2 == 0 else PSD
            w = [t_buev[i // 4], t_cwt]
            if i >= 2:
                w.append(t_c[i - 2])
            if i in (0, 1):
                w.append(t_logev[NT - 1])
            t_pb[i] = mmg(
                [(pbt[:, 0:512], bu_sb[0:64, i * 128:(i + 1) * 128], cwt_sb[:, 0:512], True, True),
                 (pbt[:, 512:1024], bu_sb[0:64, i * 128:(i + 1) * 128], cwt_sb[:, 512:1024], True, True)],
                waits=w)
            if i == 0:
                t_pw = mmg(
                    [(pwt[:, 0:512], iotag_sb[:, 0:128], ldlog_sb[:, 0:512], True, True),
                     (pwt[:, 512:1024], iotag_sb[:, 0:128], ldlog_sb[:, 512:1024], True, True)],
                    waits=[t_iog, t_ldlog, t_d128])
                t_pwv[i] = op("scalar", lambda e, p=pwt: e.activation(out=pw_a[:, :], in_=p[:, :], func=AF.Exp),
                              waits=[t_pw])
            else:
                t_pwv[i] = op("vector", lambda e: e.tensor_mul(pw_a[:, :], pw_a[:, :], d128_sb[:, :]),
                              waits=[t_c[i - 1], t_d128])
            t_c[i] = op("vector", lambda e, p=pbt: e.tensor_mul(c_a[:, :], p[:, :], pw_a[:, :]),
                        waits=[t_pwv[i], t_pb[i]] + ([t_cs[i - 1]] if i >= 1 else []))
            w = [t_c[i], t_o128b]
            if i == 0:
                w.append(t_pwv[0])
            if i == 1:
                w.append(t_buev[NT // 4 - 1])
            if i >= 2:
                w.append(t_cadd[i - 2])
            t_cs[i] = mmg(
                [(pwt[0:1, 0:512], o128b_sb[:, :], c_a[:, 0:512], True, True),
                 (pwt[0:1, 512:1024], o128b_sb[:, :], c_a[:, 512:1024], True, True)],
                waits=w)
            cr = [colrow0, colrow1][i % 2]
            tcp = op("scalar", lambda e, p=pwt, cr=cr: e.copy(out=cr[:, :], in_=p[0:1, 0:1024]),
                     waits=[t_cs[i]] + ([t_cadd[i - 2]] if i >= 2 else []))
            t_cadd[i] = dma("scalar", qc, carrd[i:i + 1, :], cr[:, :], waits=[tcp])
        t_c32 = dma("sync", qs, carr32_sb[:, :], carrd[:, :], waits=[t_cadd[NT - 1], t_cadd[NT - 2]])
        t_cl = mmg([(PSB[0:1, 0:512], o128f_sb[0:32, 0:1], carr32_sb[:, 0:512], True, True),
                    (PSB[0:1, 512:1024], o128f_sb[0:32, 0:1], carr32_sb[:, 512:1024], True, True)],
                   waits=[t_c32, t_o128f, t_cs[NT - 1], t_cs[NT - 2]])
        t_carry_done = op("vector", lambda e: e.tensor_copy(out=carry_local[:, :], in_=PSB[0:1, 0:1024]),
                          waits=[t_cl, t_z3])

        t_crow = [None] * NT
        t_crowev = [None] * NT
        t_cntev = [None] * NT
        t_oh1 = [None] * NT
        t_oh2 = [None] * NT
        t_d12 = [None] * NT
        for i in range(NT):
            lg = logits_all[:, i * E:(i + 1) * E]
            tsrt = op("vector", lambda e, lg=lg: e.max(srt_sb[:, :], lg), waits=[t_logev[i]])
            t_oh1[i] = op("vector", lambda e, lg=lg, i=i:
                          e.tensor_scalar(oh1_all[:, i * E:(i + 1) * E], lg, srt_sb[:, 0:1], None, ALU.is_equal))
            t_oh2[i] = op("vector", lambda e, lg=lg, i=i:
                          e.tensor_scalar(oh2_all[:, i * E:(i + 1) * E], lg, srt_sb[:, 1:2], None, ALU.is_equal))
            t_d12[i] = op("vector", lambda e, i=i:
                          e.tensor_scalar(d12_all[:, i:i + 1], srt_sb[:, 0:1], srt_sb[:, 1:2], None, ALU.subtract))
            ts01 = op("vector", lambda e, i=i:
                      e.tensor_add(s01_sb[:, :], oh1_all[:, i * E:(i + 1) * E], oh2_all[:, i * E:(i + 1) * E]))
            w = [ts01, t_trikf, t_o1f, t_cadd[NT - 1]]
            if i >= 1:
                w += [t_crowev[i - 1], t_cntev[i - 1]]
            if i == 0:
                w.append(t_cadd[NT - 1])
            mlist = [(PSA[:, 0:8], trikf_sb[:, :], s01_sb[:, :], True, i == 0)]
            if i >= 1:
                mlist.append((PSA[:, 0:8], o1x128f_sb[:, :], crow_sb[:, :], False, True))
            t_cnt = mmg(mlist, waits=w)
            t_cntev[i] = op("vector", lambda e, i=i: e.tensor_copy(out=cnt_all[:, i * E:(i + 1) * E], in_=PSA[:, 0:8]),
                            waits=[t_cnt])
            t_crow[i] = mmg([(PSB[0:1, 0:8], o128f_sb[:, :], s01_sb[:, :], i == 0, True)],
                            waits=[ts01, t_o128f] + ([t_cadd[NT - 1]] if i == 0 else [t_crowev[i - 1]]))
            t_crowev[i] = op("scalar", lambda e: e.copy(out=crow_sb[:, :], in_=PSB[0:1, 0:8]),
                             waits=[t_crow[i]])

        oh1_3 = oh1_all[:, :].rearrange("p (t e) -> p t e", e=E)
        oh2_3 = oh2_all[:, :].rearrange("p (t e) -> p t e", e=E)
        cnt_3 = cnt_all[:, :].rearrange("p (t e) -> p t e", e=E)
        tmp_3 = tmp256[:, :].rearrange("p (t e) -> p t e", e=E)
        op("vector", lambda e: e.tensor_mul(tmp_3, oh1_3, cnt_3), waits=[t_cntev[NT - 1]])
        t_r0 = op("vector", lambda e: e.reduce_sum(rank0_all[:, :].unsqueeze(2), tmp_3, axis=AXX))
        op("vector", lambda e: e.tensor_mul(tmp_3, oh2_3, cnt_3))
        t_r1 = op("vector", lambda e: e.reduce_sum(rank1_all[:, :].unsqueeze(2), tmp_3, axis=AXX))

        t_g1 = op("scalar", lambda e: e.activation(out=g1_all[:, :], in_=d12_all[:, :], func=AF.Sigmoid),
                  waits=[t_d12[NT - 1]])
        t_g2 = op("vector", lambda e: e.tensor_scalar(g2_all[:, :], g1_all[:, :], -1.0, 1.0, ALU.mult, ALU.add),
                  waits=[t_g1])

        eqrow_b = eqrow_sb[:, :].unsqueeze(1).broadcast_to([128, NT, E])
        t_dest = {}
        for (slot, oh_3x, rank_all_x, dall, pdall) in (
            (0, oh1_3, rank0_all, d20_all, pdest0_sb),
            (1, oh2_3, rank1_all, d21_all, pdest1_sb),
        ):
            rk = rank_all_x[:, :]
            op("vector", lambda e, oh_3x=oh_3x: e.tensor_mul(tmp_3, oh_3x, eqrow_b),
               waits=[t_eq, t_r0, t_r1])
            op("vector", lambda e: e.reduce_sum(t32a[:, :].unsqueeze(2), tmp_3, axis=AXX))
            op("vector", lambda e, rk=rk: e.tensor_add(t32b[:, :], t32a[:, :], rk))
            t_pd = op("vector", lambda e, pdall=pdall: e.tensor_copy(out=pdall[:, :], in_=t32b[:, :]))
            op("vector", lambda e, pdall=pdall: e.tensor_scalar(i32a[:, :], pdall[:, :], 7, None, ALU.arith_shift_right))
            op("vector", lambda e, rk=rk: e.tensor_copy(out=i32b[:, :], in_=rk))
            op("vector", lambda e: e.tensor_scalar(i32b[:, :], i32b[:, :], 127, None, ALU.bitwise_and))
            op("vector", lambda e: e.tensor_scalar(i32c[:, :], i32b[:, :], 6, None, ALU.logical_shift_left))
            op("vector", lambda e: e.tensor_scalar(i32b[:, :], i32b[:, :], 4, None, ALU.logical_shift_left))
            op("vector", lambda e: e.tensor_add(i32c[:, :], i32c[:, :], i32b[:, :]))
            t_d2 = op("vector", lambda e, dall=dall: e.tensor_add(dall[:, :], i32c[:, :], i32a[:, :]))
            t_dest[slot] = (t_d2, t_pd)
        t_last_meta = t_dest[1][0]

        t_tscat = []
        for i in range(NT):
            for (slot, dall) in ((0, d20_all), (1, d21_all)):
                t_tscat.append(idma(tidd, IndirectOffsetOnAxis(dall[:, i:i + 1], 1),
                                    tokid_sb[:, i:i + 1], None,
                                    waits=[t_dest[slot][0], t_tid0, t_tok]))
        t_tidld = dma("sync", qs, tidt_sb[:, :], tidd2d, waits=[t_tscat[-1]])

        tp1 = op("vector", lambda e: e.tensor_copy(out=pack_sb[:, 0:1024], in_=carry_local[:, :]),
                 waits=[t_carry_done])
        tp2 = op("vector", lambda e: e.tensor_copy(out=pack_sb[:, 1024:1032], in_=crow_sb[:, :]),
                 waits=[t_crowev[NT - 1]])
        t_ccw = dma("gpsimd", qg, ccin[:, :], pack_sb[:, :], waits=[tp2])
        cc.v += 1
        t_cc = cc.tok()

        def cc_fn(e, w=t_ccw, h=cc.h):
            e.wait_ge(w[0], w[1])
            e.collective_compute(
                "AllGather", ALU.bypass,
                replica_groups=[list(range(NCORE))],
                ins=[ccin.opt()], outs=[ccout.opt()],
            ).then_inc(h, 1)
        emit("gpsimd", cc_fn)
        t_ccr = dma("gpsimd", qg, gath_sb[:, :], ccout[:, :], waits=[t_cc])

        t_offs = mmg([(PSA[0:1, 0:8], rmask_sb[:, :], gath_sb[0:8, 1024:1032], True, True)],
                     waits=[t_ccr, t_rm, t_cntev[NT - 1]])
        t_offsev = op("vector", lambda e: e.tensor_copy(out=offsrow_sb[:, :], in_=PSA[0:1, 0:8]),
                      waits=[t_offs])
        t_offsb = mmg([(PSA[:, 0:8], o1x128f_sb[:, :], offsrow_sb[:, :], True, True)],
                      waits=[t_offsev])
        t_offsbev = op("vector", lambda e: e.tensor_copy(out=offsb_sb[:, :], in_=PSA[:, 0:8]),
                       waits=[t_offsb])
        t_c0 = mmg([(PSB[0:1, 0:512], cmask_sb[:, :], gath_sb[0:8, 0:512], True, True),
                    (PSB[0:1, 512:1024], cmask_sb[:, :], gath_sb[0:8, 512:1024], True, True)],
                   waits=[t_ccr, t_cm, t_crowev[NT - 1]])
        t_c0ev = op("vector", lambda e: e.tensor_copy(out=carry0_sb[:, :], in_=PSB[0:1, 0:1024]),
                    waits=[t_c0])
        t_carr = mmg([(PSC[0:32, 0:512], trikf_sb[0:32, 0:32], carr32_sb[:, 0:512], True, False),
                      (PSC[0:32, 0:512], o1x128f_sb[:, 0:32], carry0_sb[:, 0:512], False, True),
                      (PSC[0:32, 512:1024], trikf_sb[0:32, 0:32], carr32_sb[:, 512:1024], True, False),
                      (PSC[0:32, 512:1024], o1x128f_sb[:, 0:32], carry0_sb[:, 512:1024], False, True)],
                     waits=[t_c0ev, t_c[NT - 1]])
        t_carrev = op("vector", lambda e: e.tensor_copy(out=carrb32_sb[:, :], in_=PSC[0:32, 0:1024]),
                      waits=[t_carr])
        t_carrw = dma("scalar", qc, carrbd[:, :], carrb32_sb[:, :], waits=[t_carrev])

        offsb_b = offsb_sb[:, :].unsqueeze(1).broadcast_to([128, NT, E])
        t_gate = {}
        for (slot, oh_3x, rank_all_x, g_all_x, gate_all_x) in (
            (0, oh1_3, rank0_all, g1_all, gate0_all),
            (1, oh2_3, rank1_all, g2_all, gate1_all),
        ):
            rk = rank_all_x[:, :]
            op("vector", lambda e, oh_3x=oh_3x: e.tensor_mul(tmp_3, oh_3x, offsb_b),
               waits=[t_offsbev, t_g2])
            op("vector", lambda e: e.reduce_sum(t32a[:, :].unsqueeze(2), tmp_3, axis=AXX))
            op("vector", lambda e, rk=rk: e.tensor_add(t32b[:, :], t32a[:, :], rk))
            op("vector", lambda e: e.tensor_scalar(t32c[:, :], t32b[:, :], CAP - 0.5, None, ALU.is_lt))
            t_gate[slot] = op("vector", lambda e, g_all_x=g_all_x, gate_all_x=gate_all_x:
                              e.tensor_mul(gate_all_x[:, :], g_all_x[:, :], t32c[:, :]))

        if DEBUG:
            dma("sync", qs, dbg["logits"][:, :], logits_all[:, :], waits=[t_logev[NT - 1]])
            dma("sync", qs, dbg["carry"][:, :], carry_local[:, :], waits=[t_carry_done])
            dma("sync", qs, dbg["rank0"][:, :], rank0_all[:, :], waits=[t_r0])
            dma("sync", qs, dbg["rank1"][:, :], rank1_all[:, :], waits=[t_r1])
            dma("sync", qs, dbg["gate0"][:, :], gate0_all[:, :], waits=[t_gate[0]])
            dma("sync", qs, dbg["gate1"][:, :], gate1_all[:, :], waits=[t_gate[1]])
            dma("sync", qs, dbg["pdest0"][:, :], pdest0_sb[:, :], waits=[t_last_meta])
            dma("sync", qs, dbg["pdest1"][:, :], pdest1_sb[:, :], waits=[t_last_meta])
            dma("sync", qs, dbg["tidT"][:, :], tidt_sb[:, :], waits=[t_tidld])

        t_b0_psA_free = t_offsbev
        t_b0_psB_free = t_c0ev
        t_logits_psC_free = t_c[NT - 1]
        t_bu_psD_free = t_cadd[NT - 1]

    a_ctx.close()

    b_ctx = contextlib.ExitStack()
    w1p0 = b_ctx.enter_context(nc.sbuf_tensor("w1p0", [128, DC * 512], BF16))
    w1p1 = b_ctx.enter_context(nc.sbuf_tensor("w1p1", [128, DC * 512], BF16))
    w2_sb = b_ctx.enter_context(nc.sbuf_tensor("w2_sb", [128, HC * D], BF16))
    hT = b_ctx.enter_context(nc.sbuf_tensor("hT", [128, HC * HQ], BF16))
    packT = b_ctx.enter_context(nc.sbuf_tensor("packT", [128, DC * Q], BF16))
    rows0 = b_ctx.enter_context(nc.sbuf_tensor("rows0", [128, D], BF16))
    rows1 = b_ctx.enter_context(nc.sbuf_tensor("rows1", [128, D], BF16))
    rows2 = b_ctx.enter_context(nc.sbuf_tensor("rows2", [128, D], BF16))
    rows3 = b_ctx.enter_context(nc.sbuf_tensor("rows3", [128, D], BF16))
    ysb0 = b_ctx.enter_context(nc.sbuf_tensor("ysb0", [128, D], BF16))
    ysb1 = b_ctx.enter_context(nc.sbuf_tensor("ysb1", [128, D], BF16))
    b1t_sb = b_ctx.enter_context(nc.sbuf_tensor("b1t_sb", [128, E * HC], F32))
    b2_sb = b_ctx.enter_context(nc.sbuf_tensor("b2_sb", [1, 2 * D], BF16))
    w1ps = [w1p0, w1p1]
    rowss = [rows0, rows1, rows2, rows3]
    ysbs = [ysb0, ysb1]

    t_b1t = dma("sync", qs, b1t_sb[:, :], b1t_in[:, :], waits=[t_tidld])
    t_b2_e = [None] * E

    RCH = [(0, 512), (512, 128)]

    pydr = pyd.rearrange("(r p) d -> r p d", p=128)
    packdrs = [packd0.rearrange("(r p) d -> r p d", p=128),
               packd1.rearrange("(r p) d -> r p d", p=128)]
    packds = [packd0, packd1]

    t_packw = {}
    t_gelu_seq = []
    t_yev_all = []
    t_yw_all = []
    t_mm1_last = None
    t_mm2_last = None
    t_mm2_by_e = {}
    t_transp_done = None
    t_transp_prev = None
    t_hdump = None
    w1k = 0

    for e in range(E):
        for rb in range(RB):
            k = e * RB + rb
            war = [t_packw[k - 4]] if k >= 4 else []
            tg_ = idma(rowss[k % 4][:, :], None, xnd,
                       IndirectOffsetOnAxis(tidt_sb[:, e * RB + rb:e * RB + rb + 1], 0),
                       waits=[t_tidld] + war)
            w = [tg_]
            if e >= 2:
                w.append(t_transp_prev)
            t_packw[k] = dma("sync", qs, packdrs[e % 2][rb], rowss[k % 4][:, :], waits=w)
        tts = []
        for dc in range(DC):
            w = [t_packw[e * RB + RB - 1]]
            if e >= 1:
                w.append(t_mm1_last)
            tts.append(dma("sync", qt, packT[:, dc * Q:(dc + 1) * Q],
                           packds[e % 2][:, dc * 128:(dc + 1) * 128], waits=w, transpose=True))
        t_transp_prev = t_transp_done
        t_transp_done = tts[-1]
        if DEBUG2:
            dma("sync", qs, dbg["packT2"].rearrange("(e p) c -> e p c", p=128)[e],
                packT[:, :], waits=[t_transp_done])

        t_w2_done = None
        w2_pieces = []
        for kpc in range(8):
            w = [t_mm2_last] if (e >= 1 and kpc == 0) else []
            if e == 0 and kpc == 0:
                w = [t_tidld]
            w2_pieces.append((kpc, w))

        for half in range(2):
            t_w1p = [None] * 8
            pbase = w1k

            def issue_w1_piece(pp, e=e, pbase=pbase, t_w1p=t_w1p):
                w = []
                k = pbase + pp
                if k >= 2:
                    w.append(t_gelu_seq[(k - 1) * 4 - 1])
                t_w1p[pp] = dma("gpsimd", qw1[k % 2],
                                w1ps[k % 2][:, :].rearrange("p (dc hh) -> p dc hh", dc=DC),
                                w1_in[e][:, pp * 512:(pp + 1) * 512].rearrange("(dc p) hh -> p dc hh", p=128),
                                waits=w)

            def issue_w2_pieces(n, e=e):
                nonlocal_t = None
                for _ in range(n):
                    if w2_pieces:
                        kpc, wp = w2_pieces.pop(0)
                        nonlocal_t = dma("gpsimd", qw2,
                                         w2_sb[:, kpc * 4 * D:(kpc + 1) * 4 * D].rearrange("p (hc d) -> p hc d", hc=4),
                                         w2_in[e][kpc * 512:(kpc + 1) * 512, :].rearrange("(hc p) d -> p hc d", p=128),
                                         waits=wp)
                return nonlocal_t

            for hc in range(HC):
                pp = hc // 4
                hcl = hc % 4
                if hc == 0:
                    issue_w1_piece(0)
                    if half == 0:
                        tw2 = issue_w2_pieces(2)
                        t_w2_done = tw2 or t_w2_done
                    issue_w1_piece(1)
                    if half == 0:
                        tw2 = issue_w2_pieces(1)
                        t_w2_done = tw2 or t_w2_done
                elif hc % 4 == 0 and hc // 4 + 1 < 8:
                    issue_w1_piece(hc // 4 + 1)
                    if half == 0:
                        tw2 = issue_w2_pieces(1)
                        t_w2_done = tw2 or t_w2_done
                w1buf = w1ps[(pbase + pp) % 2]
                m1k = len(t_gelu_seq)
                pm1 = PSA if m1k % 2 == 0 else PSC
                w = [t_w1p[pp], t_transp_done]
                if m1k >= 2:
                    w.append(t_gelu_seq[m1k - 2])
                elif m1k == 0:
                    w += [t_b0_psA_free]
                elif m1k == 1:
                    w += [t_logits_psC_free]
                if hc == 0 and t_mm2_last is not None:
                    w.append(t_mm2_last)
                gro = half * HQ
                mms = []
                for dc in range(DC):
                    lhs = w1buf[:, dc * 512 + hcl * 128: dc * 512 + (hcl + 1) * 128]
                    mms.append((pm1[:, 0:512], lhs, packT[:, dc * Q + gro: dc * Q + gro + 512],
                                dc == 0, dc == DC - 1))
                    mms.append((pm1[:, 512:640], lhs, packT[:, dc * Q + gro + 512: dc * Q + gro + 640],
                                dc == 0, dc == DC - 1))
                tmm = mmg(mms, waits=w)
                wg_ = [tmm, t_b1t]
                if hc == 0 and t_mm2_last is not None:
                    wg_.append(t_mm2_last)
                if hc == 0 and t_hdump is not None:
                    wg_.append(t_hdump)
                tg = op("scalar",
                        lambda ee, pm1=pm1, hc=hc, e=e:
                        ee.activation(out=hT[:, hc * HQ: hc * HQ + HQ],
                                      in_=pm1[:, 0:HQ], func=AF.Gelu_apprx_tanh,
                                      bias=b1t_sb[:, e * HC + hc: e * HC + hc + 1]),
                        waits=wg_)
                t_gelu_seq.append(tg)
            w1k += 8
            t_mm1_last = t_gelu_seq[-1]
            if half == 0:
                wb2 = [t_tidld] if e == 0 else ([t_mm2_by_e[e - 2]] if e >= 2 else [])
                t_b2_e[e] = dma("gpsimd", qb2[e % 2], b2_sb[:, (e % 2) * D:(e % 2 + 1) * D],
                                b2_in[:, e * D:(e + 1) * D], waits=wb2)
            if DEBUG2:
                t_hdump = dma("sync", qs, dbg["hT2"].rearrange("(eh p) c -> eh p c", p=128)[e * 2 + half],
                              hT[:, :], waits=[t_mm1_last])

            for rbl in range(5):
                yk = len(t_yev_all)
                pm2 = PSB if yk % 2 == 0 else PSD
                grb = half * 5 + rbl
                w = []
                if rbl == 0:
                    w = [t_mm1_last, t_w2_done, t_b2_e[e], t_o1b]
                if yk >= 2:
                    w.append(t_yev_all[yk - 2])
                elif yk == 0:
                    w += [t_b0_psB_free]
                elif yk == 1:
                    w += [t_bu_psD_free]
                mms = []
                for hc in range(HC):
                    lhs = hT[:, hc * HQ + rbl * 128: hc * HQ + (rbl + 1) * 128]
                    mms.append((pm2[:, 0:512], lhs, w2_sb[:, hc * D: hc * D + 512],
                                hc == 0, False))
                    mms.append((pm2[:, 512:1024], lhs, w2_sb[:, hc * D + 512: (hc + 1) * D],
                                hc == 0, False))
                eo = (e % 2) * D
                mms.append((pm2[:, 0:512], o1x128b_sb[:, :], b2_sb[:, eo: eo + 512],
                            False, True))
                mms.append((pm2[:, 512:1024], o1x128b_sb[:, :], b2_sb[:, eo + 512: eo + D],
                            False, True))
                t_mm2_last = mmg(mms, waits=w)
                tv = op("vector",
                        lambda e_, pm2=pm2, yk=yk:
                        e_.tensor_copy(out=ysbs[yk % 2][:, :], in_=pm2[:, 0:1024]),
                        waits=[t_mm2_last] + ([t_yw_all[yk - 2]] if yk >= 2 else []))
                t_yev_all.append(tv)
                t_yw_all.append(dma("scalar", qc, pydr[e * RB + grb], ysbs[yk % 2][:, :], waits=[tv]))
        t_mm2_by_e[e] = t_mm2_last

    t_py_all = t_yw_all[-1]
    b_ctx.close()

    c_ctx = contextlib.ExitStack()
    if True:
        _c = c_ctx.enter_context
        xnT2 = _c(nc.sbuf_tensor("xnT2", [128, DC * NTOK], BF16))
        gwt_sb = _c(nc.sbuf_tensor("gwt_sb", [128, DC * D], BF16))
        gateb_sb = _c(nc.sbuf_tensor("gateb_sb", [1, D], BF16))
        pw_c = _c(nc.sbuf_tensor("pw_c", [128, D], F32))
        ctile0 = _c(nc.sbuf_tensor("ctile0", [1, D], BF16))
        ctile1 = _c(nc.sbuf_tensor("ctile1", [1, D], BF16))
        d128c_sb = _c(nc.sbuf_tensor("d128c_sb", [128, D], F32))
        ld128c_sb = _c(nc.sbuf_tensor("ld128c_sb", [1, D], F32))
        c_c = _c(nc.sbuf_tensor("c_c", [128, D], BF16))
        gate_c = _c(nc.sbuf_tensor("gate_c", [128, D], F32))
        sg2_c = _c(nc.sbuf_tensor("sg2_c", [128, D], F32))
        sg3_c = _c(nc.sbuf_tensor("sg3_c", [128, D], F32))
        s_c = _c(nc.sbuf_tensor("s_c", [128, D], F32))
        ya0 = _c(nc.sbuf_tensor("ya0", [128, D], BF16))
        ya1 = _c(nc.sbuf_tensor("ya1", [128, D], BF16))
        yb0 = _c(nc.sbuf_tensor("yb0", [128, D], BF16))
        yb1 = _c(nc.sbuf_tensor("yb1", [128, D], BF16))
        ysum_c = _c(nc.sbuf_tensor("ysum_c", [128, D], F32))
        xt0c = _c(nc.sbuf_tensor("xt0c", [128, D], F32))
        xt1c = _c(nc.sbuf_tensor("xt1c", [128, D], F32))
        osb0 = _c(nc.sbuf_tensor("osb0", [128, D], F32))
        osb1 = _c(nc.sbuf_tensor("osb1", [128, D], F32))
        yas = [ya0, ya1]
        ybs = [yb0, yb1]
        xtc = [xt0c, xt1c]
        osb = [osb0, osb1]
        xr = x_in.rearrange("(i p) d -> i p d", p=128)
        outr = out_io.rearrange("(i p) d -> i p d", p=128)

        t_bdrain = [t_py_all, t_mm2_last, t_yev_all[-1], t_gelu_seq[-1]]
        t_T2 = []
        for dc in range(DC):
            t_T2.append(dma("sync", qt, xnT2[:, dc * NTOK:(dc + 1) * NTOK],
                            xnd[:, dc * 128:(dc + 1) * 128],
                            waits=t_bdrain if dc == 0 else [], transpose=True))
        t_gwt = dma("gpsimd", qgw, gwt_sb[:, :].rearrange("p (dc e) -> p dc e", dc=DC),
                    gwt_in.rearrange("(dc p) e -> p dc e", p=128),
                    waits=t_bdrain)
        t_gateb = dma("gpsimd", qgw, gateb_sb[:, :], gateb_in[:, :], waits=[])

        def xnT2_tile(dc, i):
            return xnT2[:, dc * NTOK + i * 128: dc * NTOK + (i + 1) * 128]

        t_l128c = op("vector", lambda e: e.tensor_scalar(ld128c_sb[:, :], ldlog_sb[:, :], 128.0, None, ALU.mult))
        t_d128cmm = mmg([(PSB[:, 0:512], o1x128f_sb[:, :], ld128c_sb[:, 0:512], True, True),
                         (PSB[:, 512:1024], o1x128f_sb[:, :], ld128c_sb[:, 512:1024], True, True)],
                        waits=[t_l128c] + t_bdrain + [t_yev_all[-1]])
        t_d128c = op("scalar", lambda e: e.activation(out=d128c_sb[:, :], in_=PSB[:, :], func=AF.Exp),
                     waits=[t_d128cmm])

        ctiles = [ctile0, ctile1]
        t_ct = [None] * NT
        t_h2 = [None] * NT
        t_xl = [None] * NT
        t_out = [None] * NT
        t_oa = [None] * NT
        t_sgm = [None] * NT
        t_exp2 = [None] * NT
        t_c2 = [None] * NT
        t_s2 = [None] * NT
        t_ct[0] = dma("sync", qs, ctiles[0][:, :], carrbd[0:1, :], waits=[t_carrw] + t_bdrain)
        t_ct[1] = dma("sync", qs, ctiles[1][:, :], carrbd[1:2, :], waits=[])
        t_ysum = [None] * NT
        t_sil = [None] * NT
        t_sgmul = [None] * NT

        t_xl[0] = dma("sync", qs, xtc[0][:, :], xr[0], waits=t_bdrain)
        t_xl[1] = dma("sync", qs, xtc[1][:, :], xr[1])

        for i in range(NT):
            par = i % 2
            pbt = PSA if par == 0 else PSC
            pwt = PSB if par == 0 else PSD
            w = [t_cwt]
            if i >= 2:
                w += [t_c2[i - 2], t_s2[i - 2]]
            if i < 2:
                w += t_bdrain + [t_yev_all[-1]]
            t_pb2 = mmg(
                [(pbt[:, 0:512], bu_sb[0:64, i * 128:(i + 1) * 128], cwt_sb[:, 0:512], True, True),
                 (pbt[:, 512:1024], bu_sb[0:64, i * 128:(i + 1) * 128], cwt_sb[:, 512:1024], True, True)],
                waits=w)
            if i == 0:
                t_pw2 = mmg(
                    [(pwt[:, 0:512], iotag_sb[:, 0:128], ldlog_sb[:, 0:512], True, True),
                     (pwt[:, 512:1024], iotag_sb[:, 0:128], ldlog_sb[:, 512:1024], True, True)],
                    waits=[t_iog, t_ldlog, t_d128c])
                t_exp2[i] = op("scalar", lambda e, p=pwt: e.activation(out=pw_c[:, :], in_=p[:, :], func=AF.Exp),
                               waits=[t_pw2])
            else:
                t_exp2[i] = op("vector", lambda e: e.tensor_mul(pw_c[:, :], pw_c[:, :], d128c_sb[:, :]),
                               waits=[t_c2[i - 1], t_d128c])
            t_c2[i] = op("vector", lambda e, p=pbt: e.tensor_mul(c_c[:, :], p[:, :], pw_c[:, :]),
                         waits=[t_exp2[i], t_pb2] + ([t_h2[i - 1]] if i >= 1 else []))
            ctb = ctiles[i % 2]
            t_h2[i] = mmg(
                [(pbt[:, 0:512], trikb_sb[:, :], c_c[:, 0:512], True, False),
                 (pbt[:, 0:512], o1x128b_sb[:, :], ctb[:, 0:512], False, True),
                 (pbt[:, 512:1024], trikb_sb[:, :], c_c[:, 512:1024], True, False),
                 (pbt[:, 512:1024], o1x128b_sb[:, :], ctb[:, 512:1024], False, True)],
                waits=[t_c2[i], t_trikb, t_o1b, t_ct[i]])
            t_h = t_h2[i]
            if i + 2 < NT:
                t_ct[i + 2] = dma("sync", qs, ctb[:, :], carrbd[i + 2:i + 3, :], waits=[t_h2[i]])
            mms = []
            for dc in range(DC):
                mms.append((pwt[:, 0:512], xnT2_tile(dc, i), gwt_sb[:, dc * D: dc * D + 512],
                            dc == 0, False))
                mms.append((pwt[:, 512:1024], xnT2_tile(dc, i), gwt_sb[:, dc * D + 512: (dc + 1) * D],
                            dc == 0, False))
            mms.append((pwt[:, 0:512], o1x128b_sb[:, :], gateb_sb[:, 0:512], False, True))
            mms.append((pwt[:, 512:1024], o1x128b_sb[:, :], gateb_sb[:, 512:1024], False, True))
            w = [t_gwt, t_gateb, t_o1b]
            if i >= 2:
                w.append(t_sgm[i - 2])
            if i < 2:
                w.append(t_yev_all[-1])
            if i == 0:
                w.append(t_exp2[0])
            t_gmm = mmg(mms, waits=w + (list(t_T2) if i == 0 else []))
            t_sgm[i] = op("scalar", lambda e, p=pwt: e.activation(out=gate_c[:, :], in_=p[:, :], func=AF.Sigmoid),
                          waits=[t_gmm] + ([t_s2[i - 1]] if i >= 1 else []))
            t_s2[i] = op("vector", lambda e, p=pbt: e.tensor_mul(s_c[:, :], p[:, :], gate_c[:, :]),
                         waits=[t_sgm[i], t_h] + ([t_sgmul[i - 1]] if i >= 1 else []))
            if DEBUG:
                dma("scalar", qc, dbg["s"].rearrange("(i p) d -> i p d", p=128)[i], s_c[:, :],
                    waits=[t_s2[i]])
            w = [t_py_all]
            if i >= 2:
                w.append(t_ysum[i - 2])
            t_ya = idma(yas[par][:, :], None, pyd,
                        IndirectOffsetOnAxis(pdest0_sb[:, i:i + 1], 0), waits=w)
            t_yb = idma(ybs[par][:, :], None, pyd,
                        IndirectOffsetOnAxis(pdest1_sb[:, i:i + 1], 0), waits=[])
            tyb = op("vector", lambda e, par=par, i=i: e.tensor_scalar(sg2_c[:, :], ybs[par][:, :],
                                                                        gate1_all[:, i:i + 1], None, ALU.mult),
                     waits=[t_yb, t_gate[1]] + ([t_ysum[i - 1]] if i >= 1 else []))
            tya2 = op("vector", lambda e, par=par, i=i: e.tensor_scalar(ysum_c[:, :], yas[par][:, :],
                                                                        gate0_all[:, i:i + 1], None, ALU.mult),
                      waits=[t_ya, t_gate[0]] + ([t_sil[i - 1]] if i >= 1 else []))
            t_ysum[i] = op("vector", lambda e: e.tensor_add(ysum_c[:, :], ysum_c[:, :], sg2_c[:, :]),
                           waits=[tya2, tyb])
            tsg2 = op("scalar", lambda e: e.activation(out=sg3_c[:, :], in_=ysum_c[:, :], func=AF.Sigmoid),
                      waits=[t_ysum[i]] + ([t_sil[i - 1]] if i >= 1 else []))
            t_sil[i] = op("vector", lambda e: e.tensor_mul(ysum_c[:, :], ysum_c[:, :], sg3_c[:, :]),
                          waits=[tsg2])
            if DEBUG:
                dma("scalar", qc, dbg["g"].rearrange("(i p) d -> i p d", p=128)[i], ysum_c[:, :],
                    waits=[t_sil[i]])
            t_sgmul[i] = op("vector", lambda e: e.tensor_mul(s_c[:, :], s_c[:, :], ysum_c[:, :]),
                            waits=[t_sil[i], t_s2[i]])
            t_oa[i] = op("vector", lambda e, par=par: e.tensor_add(osb[par][:, :], s_c[:, :], xtc[par][:, :]),
                         waits=[t_sgmul[i], t_xl[i]] + ([t_out[i - 2]] if i >= 2 else []))
            t_out[i] = dma("sync", qs, outr[i], osb[par][:, :], waits=[t_oa[i]])
            if i + 2 < NT:
                t_xl[i + 2] = dma("sync", qs, xtc[par][:, :], xr[i + 2], waits=[t_oa[i]])

        if DEBUG2:
            dma("sync", qs, dbg["pyd2"][:, :], pyd[:, :], waits=[t_py_all])
            dma("sync", qs, dbg["xnd2"][:, :], xnd[:, :], waits=[t_py_all])
            dma("sync", qs, dbg["tidd2"][:, :], tidd[:, :], waits=[t_py_all])
            dma("sync", qs, dbg["carrbd2"][:, :], carrbd[:, :], waits=[t_py_all])
        final_qs = qs.tok()

        def f_final(e, tok=final_qs):
            e.wait_ge(tok[0], tok[1])
        emit("sync", f_final)
    c_ctx.close()

    with nc.Block() as block:
        @block.sync
        def _(e):
            for f in prog["sync"]:
                f(e)

        @block.scalar
        def _(e):
            for f in prog["scalar"]:
                f(e)

        @block.vector
        def _(e):
            for f in prog["vector"]:
                f(e)

        @block.tensor
        def _(e):
            for f in prog["tensor"]:
                f(e)

        @block.gpsimd
        def _(e):
            for f in prog["gpsimd"]:
                f(e)

    stack.close()
    return nc


def _consts():
    bf = ml_dtypes.bfloat16
    c = {}
    c["trikf"] = np.triu(np.ones((128, 128), np.float32), 1)
    c["trikb"] = c["trikf"].astype(bf)
    c["o128f"] = np.ones((128, 1), np.float32)
    c["o128b"] = c["o128f"].astype(bf)
    c["o1x128f"] = np.ones((1, 128), np.float32)
    c["o1x128b"] = c["o1x128f"].astype(bf)
    c["eqrow"] = np.ascontiguousarray(np.tile((np.arange(E) * Q).astype(np.float32), (128, 1)))
    c["e10row"] = np.ascontiguousarray(np.tile((np.arange(E) * RB).astype(np.float32), (128, 1)))
    c["tokid"] = np.ascontiguousarray(np.arange(NTOK, dtype=np.int32).reshape(NT, 128).T)
    return c


def _host_prep(L0_attn_x, L0_attn_B_w, L0_attn_C_w, L0_attn_gate_w, L0_attn_gate_b,
               L0_attn_log_decay, L0_ffn_Wg, L0_ffn_W1, L0_ffn_b1, L0_ffn_W2, L0_ffn_b2):
    x = np.ascontiguousarray(np.asarray(L0_attn_x, dtype=np.float32)).reshape(NCORE * NTOK, D)
    shared = {
        "w1": np.ascontiguousarray(np.asarray(L0_ffn_W1, dtype=np.float32)),
        "w2": np.ascontiguousarray(np.asarray(L0_ffn_W2, dtype=np.float32)),
        "wg": np.ascontiguousarray(np.asarray(L0_ffn_Wg, dtype=np.float32)),
        "bwt": np.ascontiguousarray(np.asarray(L0_attn_B_w, dtype=np.float32).T),
        "cwt": np.ascontiguousarray(np.asarray(L0_attn_C_w, dtype=np.float32).T),
        "gwt": np.ascontiguousarray(np.asarray(L0_attn_gate_w, dtype=np.float32).T),
        "gateb": np.asarray(L0_attn_gate_b, dtype=np.float32).reshape(1, D),
        "ldec": np.asarray(L0_attn_log_decay, dtype=np.float32).reshape(1, D),
        "b1t": np.ascontiguousarray(
            np.asarray(L0_ffn_b1, dtype=np.float32).reshape(E, HC, 128).transpose(2, 0, 1).reshape(128, E * HC)),
        "b2": np.ascontiguousarray(np.asarray(L0_ffn_b2, dtype=np.float32).reshape(1, E * D)),
    }
    shared.update(_consts())
    in_maps = []
    for c in range(NCORE):
        m = dict(shared)
        m["x"] = np.ascontiguousarray(x[c * NTOK:(c + 1) * NTOK])
        m["iotag"] = ((c % 2) * NTOK + np.arange(128, dtype=np.float32)).reshape(1, 128)
        m["rmask"] = (np.arange(8) < c).astype(np.float32).reshape(8, 1)
        m["cmask"] = (np.eye(8, dtype=np.float32)[c - 1].reshape(8, 1)
                      if c % 2 == 1 else np.zeros((8, 1), np.float32))
        in_maps.append(m)
    return in_maps


def kernel(**inputs):
    in_maps = _host_prep(**inputs)
    nc = build_nc()
    res = run_bass_kernel_spmd(nc, in_maps, core_ids=list(range(NCORE)))
    out = np.concatenate([np.asarray(r["out"]) for r in res.results], axis=0)
    return out.reshape(4, 8192, 1024)


# version 10
# speedup vs baseline: 1.1212x; 1.0282x over previous
import numpy as np
import ml_dtypes
import contextlib
import concourse.bass as bass
import concourse.mybir as mybir
from concourse.bass import IndirectOffsetOnAxis
from concourse.bass_utils import run_bass_kernel_spmd

F32 = mybir.dt.float32
BF16 = mybir.dt.bfloat16
I32 = mybir.dt.int32
AF = mybir.ActivationFunctionType
ALU = mybir.AluOpType
AXX = mybir.AxisListType.X

NCORE = 8
NTOK = 4096
NT = 32
D = 1024
DC = 8
S = 64
E = 8
H = 4096
HC = 32
Q = 1280
RB = 10
HQ = 640
CAP = 8602
EPS = 1e-5

ENGINES = ("sync", "scalar", "vector", "tensor", "gpsimd")

DEBUG = False
DEBUG2 = False
PAD_DRAM = 0


class TSem:
    __slots__ = ("h", "v", "name")

    def __init__(self, h, name):
        self.h = h
        self.v = 0
        self.name = name

    def bump(self, n):
        self.v += n
        return (self.h, self.v)

    def tok(self):
        return (self.h, self.v)


def build_nc():
    nc = bass.Bass("TRN2", target_bir_lowering=False)

    x_in = nc.dram_tensor("x", [NTOK, D], F32, kind="ExternalInput").ap()
    w1_in = nc.dram_tensor("w1", [E, D, H], F32, kind="ExternalInput").ap()
    w2_in = nc.dram_tensor("w2", [E, H, D], F32, kind="ExternalInput").ap()
    wg_in = nc.dram_tensor("wg", [D, E], F32, kind="ExternalInput").ap()
    bwt_in = nc.dram_tensor("bwt", [D, S], F32, kind="ExternalInput").ap()
    cwt_in = nc.dram_tensor("cwt", [S, D], F32, kind="ExternalInput").ap()
    gwt_in = nc.dram_tensor("gwt", [D, D], F32, kind="ExternalInput").ap()
    gateb_in = nc.dram_tensor("gateb", [1, D], F32, kind="ExternalInput").ap()
    ldec_in = nc.dram_tensor("ldec", [1, D], F32, kind="ExternalInput").ap()
    b1t_in = nc.dram_tensor("b1t", [128, E * HC], F32, kind="ExternalInput").ap()
    b2_in = nc.dram_tensor("b2", [1, E * D], F32, kind="ExternalInput").ap()
    trikf_in = nc.dram_tensor("trikf", [128, 128], F32, kind="ExternalInput").ap()
    trikb_in = nc.dram_tensor("trikb", [128, 128], BF16, kind="ExternalInput").ap()
    o128f_in = nc.dram_tensor("o128f", [128, 1], F32, kind="ExternalInput").ap()
    o128b_in = nc.dram_tensor("o128b", [128, 1], BF16, kind="ExternalInput").ap()
    o1x128f_in = nc.dram_tensor("o1x128f", [1, 128], F32, kind="ExternalInput").ap()
    o1x128b_in = nc.dram_tensor("o1x128b", [1, 128], BF16, kind="ExternalInput").ap()
    eqrow_in = nc.dram_tensor("eqrow", [128, E], F32, kind="ExternalInput").ap()
    e10row_in = nc.dram_tensor("e10row", [128, E], F32, kind="ExternalInput").ap()
    tokid_in = nc.dram_tensor("tokid", [128, NT], I32, kind="ExternalInput").ap()
    iotag_in = nc.dram_tensor("iotag", [1, 128], F32, kind="ExternalInput").ap()
    rmask_in = nc.dram_tensor("rmask", [8, 1], F32, kind="ExternalInput").ap()
    cmask_in = nc.dram_tensor("cmask", [8, 1], F32, kind="ExternalInput").ap()
    out_io = nc.dram_tensor("out", [NTOK, D], F32, kind="ExternalOutput").ap()

    dbg = {}
    if DEBUG2:
        dbg["pyd2"] = nc.dram_tensor("dbg_pyd2", [E * Q, D], BF16, kind="ExternalOutput").ap()
        dbg["xnd2"] = nc.dram_tensor("dbg_xnd2", [NTOK, D], BF16, kind="ExternalOutput").ap()
        dbg["tidd2"] = nc.dram_tensor("dbg_tidd2", [1, 128 * E * RB], I32, kind="ExternalOutput").ap()
        dbg["carrbd2"] = nc.dram_tensor("dbg_carrbd2", [NT, D], BF16, kind="ExternalOutput").ap()
        dbg["packT2"] = nc.dram_tensor("dbg_packT2", [E * 128, DC * Q], BF16, kind="ExternalOutput").ap()
        dbg["hT2"] = nc.dram_tensor("dbg_hT2", [E * 2 * 128, HC * HQ], BF16, kind="ExternalOutput").ap()
        dbg["log2"] = nc.dram_tensor("dbg_log2", [128, NT * E], F32, kind="ExternalOutput").ap()
        dbg["bu2"] = nc.dram_tensor("dbg_bu2", [64, NTOK], BF16, kind="ExternalOutput").ap()
    if DEBUG:
        dbg["logits"] = nc.dram_tensor("dbg_logits", [128, NT * E], F32, kind="ExternalOutput").ap()
        dbg["carry"] = nc.dram_tensor("dbg_carry", [1, D], F32, kind="ExternalOutput").ap()
        dbg["rank0"] = nc.dram_tensor("dbg_rank0", [128, NT], F32, kind="ExternalOutput").ap()
        dbg["rank1"] = nc.dram_tensor("dbg_rank1", [128, NT], F32, kind="ExternalOutput").ap()
        dbg["gate0"] = nc.dram_tensor("dbg_gate0", [128, NT], F32, kind="ExternalOutput").ap()
        dbg["gate1"] = nc.dram_tensor("dbg_gate1", [128, NT], F32, kind="ExternalOutput").ap()
        dbg["pdest0"] = nc.dram_tensor("dbg_pdest0", [128, NT], I32, kind="ExternalOutput").ap()
        dbg["pdest1"] = nc.dram_tensor("dbg_pdest1", [128, NT], I32, kind="ExternalOutput").ap()
        dbg["tidT"] = nc.dram_tensor("dbg_tidT", [128, E * RB], I32, kind="ExternalOutput").ap()
        dbg["gateT"] = nc.dram_tensor("dbg_gateT", [128, E * RB], F32, kind="ExternalOutput").ap()
        dbg["s"] = nc.dram_tensor("dbg_s", [NTOK, D], F32, kind="ExternalOutput").ap()
        dbg["g"] = nc.dram_tensor("dbg_g", [NTOK, D], F32, kind="ExternalOutput").ap()

    if PAD_DRAM:
        nc.dram_tensor("pad_dram", [1, PAD_DRAM], F32)
    xnd = nc.dram_tensor("xnd", [NTOK, D], BF16).ap()
    packd0 = nc.dram_tensor("packd0", [Q, D], BF16).ap()
    packd1 = nc.dram_tensor("packd1", [Q, D], BF16).ap()
    pyd = nc.dram_tensor("pyd", [E * Q, D], BF16).ap()
    tidd = nc.dram_tensor("tidd", [1, 128 * E * RB], I32).ap()
    tidd2d = tidd.rearrange("o (p c) -> (o p) c", p=128)
    carrd = nc.dram_tensor("carrd", [NT, D], F32).ap()
    carrbd = nc.dram_tensor("carrbd", [NT, D], BF16).ap()
    ccin = nc.dram_tensor("ccin", [1, 1040], F32).ap()
    ccout = nc.dram_tensor("ccout", [8, 1040], F32).ap()

    prog = {k: [] for k in ENGINES}

    def emit(eng, fn):
        prog[eng].append(fn)

    stack = contextlib.ExitStack()

    def sem(name):
        return TSem(stack.enter_context(nc.semaphore(name)), name)

    qs = sem("qs")
    qc = sem("qc")
    qt = sem("qt")
    qgi = sem("qgi")
    qw1 = [sem("qw1a"), sem("qw1b")]
    qw2 = sem("qw2")
    qb2 = [sem("qb2a"), sem("qb2b")]
    qgw = sem("qgw")
    qcc = sem("qcc")
    qg = sem("qg")
    vs = sem("vs")
    ss = sem("ss")
    ps = sem("ps")
    cc = sem("cc")
    ge = sem("ge")
    last_v = [None]
    last_s = [None]

    def dma(queue, qsem, out, in_, waits=(), **kw):
        tok = qsem.bump(16)
        wl = tuple(t for t in waits if t is not None)

        def f(e, out=out, in_=in_, wl=wl, kw=kw, h=qsem.h):
            for (sh, v) in wl:
                e.wait_ge(sh, v)
            e.dma_start(out=out, in_=in_, **kw).then_inc(h, 16)
        emit(queue, f)
        return tok

    def idma(out, out_off, in_, in_off, waits=()):
        tok = qgi.bump(16)
        wl = tuple(t for t in waits if t is not None)

        def f(e, out=out, oo=out_off, in_=in_, io=in_off, wl=wl, h=qgi.h):
            for (sh, v) in wl:
                e.wait_ge(sh, v)
            e.indirect_dma_start(out=out, out_offset=oo, in_=in_, in_offset=io).then_inc(h, 16)
        emit("gpsimd", f)
        return tok

    def op(eng, build, waits=()):
        esem = vs if eng == "vector" else ss
        last = last_v if eng == "vector" else last_s
        wl = [t for t in waits if t is not None]
        if last[0] is not None:
            wl.append(last[0])
        tok = esem.bump(1)
        wl = tuple(wl)

        def f(e, build=build, wl=wl, h=esem.h):
            for (sh, v) in wl:
                e.wait_ge(sh, v)
            build(e).then_inc(h, 1)
        emit(eng, f)
        last[0] = tok
        return tok

    def mmg(mms, waits=()):
        tok = ps.bump(1)
        wl = tuple(t for t in waits if t is not None)

        def f(e, mms=mms, wl=wl, h=ps.h):
            for (sh, v) in wl:
                e.wait_ge(sh, v)
            for (o, l, r, st, sp) in mms:
                ins = e.matmul(o, l, r, start=st, stop=sp)
            ins.then_inc(h, 1)
        emit("tensor", f)
        return tok

    logits_all = stack.enter_context(nc.sbuf_tensor("logits_all", [128, NT * E], F32))
    bu_sb = stack.enter_context(nc.sbuf_tensor("bu_sb", [64, NTOK], BF16))
    cwt_sb = stack.enter_context(nc.sbuf_tensor("cwt_sb", [64, D], BF16))
    ldlog_sb = stack.enter_context(nc.sbuf_tensor("ldlog_sb", [1, D], F32))
    trikf_sb = stack.enter_context(nc.sbuf_tensor("trikf_sb", [128, 128], F32))
    trikb_sb = stack.enter_context(nc.sbuf_tensor("trikb_sb", [128, 128], BF16))
    o128f_sb = stack.enter_context(nc.sbuf_tensor("o128f_sb", [128, 1], F32))
    o128b_sb = stack.enter_context(nc.sbuf_tensor("o128b_sb", [128, 1], BF16))
    o1x128f_sb = stack.enter_context(nc.sbuf_tensor("o1x128f_sb", [1, 128], F32))
    o1x128b_sb = stack.enter_context(nc.sbuf_tensor("o1x128b_sb", [1, 128], BF16))
    iotag_sb = stack.enter_context(nc.sbuf_tensor("iotag_sb", [1, 128], F32))
    pdest0_sb = stack.enter_context(nc.sbuf_tensor("pdest0_sb", [128, NT], I32))
    pdest1_sb = stack.enter_context(nc.sbuf_tensor("pdest1_sb", [128, NT], I32))
    crow_sb = stack.enter_context(nc.sbuf_tensor("crow_sb", [1, E], F32))
    tidt_sb = stack.enter_context(nc.sbuf_tensor("tidt_sb", [128, E * RB], I32))
    stat_sb = stack.enter_context(nc.sbuf_tensor("stat_sb", [128, 16], F32))
    scrv = stack.enter_context(nc.sbuf_tensor("scrv", [128, 1], F32))
    scrs = stack.enter_context(nc.sbuf_tensor("scrs", [128, 1], F32))
    oh1_all = stack.enter_context(nc.sbuf_tensor("oh1_all", [128, NT * E], F32))
    oh2_all = stack.enter_context(nc.sbuf_tensor("oh2_all", [128, NT * E], F32))
    rank0_all = stack.enter_context(nc.sbuf_tensor("rank0_all", [128, NT], F32))
    rank1_all = stack.enter_context(nc.sbuf_tensor("rank1_all", [128, NT], F32))
    g1_all = stack.enter_context(nc.sbuf_tensor("g1_all", [128, NT], F32))
    g2_all = stack.enter_context(nc.sbuf_tensor("g2_all", [128, NT], F32))
    gate0_all = stack.enter_context(nc.sbuf_tensor("gate0_all", [128, NT], F32))
    gate1_all = stack.enter_context(nc.sbuf_tensor("gate1_all", [128, NT], F32))
    d20_all = stack.enter_context(nc.sbuf_tensor("d20_all", [128, NT], I32))
    d21_all = stack.enter_context(nc.sbuf_tensor("d21_all", [128, NT], I32))
    tmp256 = stack.enter_context(nc.sbuf_tensor("tmp256", [128, NT * E], F32))
    t32a = stack.enter_context(nc.sbuf_tensor("t32a", [128, NT], F32))
    t32b = stack.enter_context(nc.sbuf_tensor("t32b", [128, NT], F32))
    t32c = stack.enter_context(nc.sbuf_tensor("t32c", [128, NT], F32))
    i32a = stack.enter_context(nc.sbuf_tensor("i32a", [128, NT], I32))
    i32b = stack.enter_context(nc.sbuf_tensor("i32b", [128, NT], I32))
    i32c = stack.enter_context(nc.sbuf_tensor("i32c", [128, NT], I32))
    rmask_sb = stack.enter_context(nc.sbuf_tensor("rmask_sb", [8, 1], F32))
    cmask_sb = stack.enter_context(nc.sbuf_tensor("cmask_sb", [8, 1], F32))
    offsrow_sb = stack.enter_context(nc.sbuf_tensor("offsrow_sb", [1, E], F32))
    offsb_sb = stack.enter_context(nc.sbuf_tensor("offsb_sb", [128, E], F32))
    pack_sb = stack.enter_context(nc.sbuf_tensor("pack_sb", [1, 1040], F32))
    gath_sb = stack.enter_context(nc.sbuf_tensor("gath_sb", [8, 1040], F32))
    carry_local = stack.enter_context(nc.sbuf_tensor("carry_local", [1, D], F32))
    carry0_sb = stack.enter_context(nc.sbuf_tensor("carry0_sb", [1, D], F32))
    carr32_sb = stack.enter_context(nc.sbuf_tensor("carr32_sb", [32, D], F32))
    carrb32_sb = stack.enter_context(nc.sbuf_tensor("carrb32_sb", [32, D], BF16))

    PSA = stack.enter_context(nc.psum_tensor("PSA", [128, 1024], F32))
    PSB = stack.enter_context(nc.psum_tensor("PSB", [128, 1024], F32))
    PSC = stack.enter_context(nc.psum_tensor("PSC", [128, 1024], F32))
    PSD = stack.enter_context(nc.psum_tensor("PSD", [128, 1024], F32))

    t_cwt = dma("gpsimd", qg, cwt_sb[:, :], cwt_in[:, :])
    t_ldec_sb = stack.enter_context(nc.sbuf_tensor("ldec_sb", [1, D], F32))
    ldec_sb = t_ldec_sb
    t_ldec = dma("sync", qs, ldec_sb[:, :], ldec_in[:, :])
    t_trikf = dma("sync", qs, trikf_sb[:, :], trikf_in[:, :])
    t_trikb = dma("sync", qs, trikb_sb[:, :], trikb_in[:, :])
    t_o128f = dma("sync", qs, o128f_sb[:, :], o128f_in[:, :])
    t_o128b = dma("sync", qs, o128b_sb[:, :], o128b_in[:, :])
    t_o1f = dma("sync", qs, o1x128f_sb[:, :], o1x128f_in[:, :])
    t_o1b = dma("sync", qs, o1x128b_sb[:, :], o1x128b_in[:, :])
    t_iog = dma("sync", qs, iotag_sb[:, :], iotag_in[:, :])

    t_sg0 = op("scalar", lambda e: e.activation(out=ldlog_sb[:, :], in_=ldec_sb[:, :], func=AF.Sigmoid),
               waits=[t_ldec])
    t_ldlog = op("scalar", lambda e: e.activation(out=ldlog_sb[:, :], in_=ldlog_sb[:, :], func=AF.Ln),
                 waits=[t_sg0])
    t_z3 = op("vector", lambda e: e.memset(carry_local[:, :], 0.0))

    a_ctx = contextlib.ExitStack()
    xnT = a_ctx.enter_context(nc.sbuf_tensor("xnT", [128, DC * NTOK], BF16))

    def xnT_tile(dc, i):
        return xnT[:, dc * NTOK + i * 128: dc * NTOK + (i + 1) * 128]

    if True:
        _a = a_ctx.enter_context
        bwt_sb = _a(nc.sbuf_tensor("bwt_sb", [128, DC * S], BF16))
        wg_sb = _a(nc.sbuf_tensor("wg_sb", [128, DC * E], BF16))
        xt0 = _a(nc.sbuf_tensor("xt0", [128, D], F32))
        xt1 = _a(nc.sbuf_tensor("xt1", [128, D], F32))
        xnb0 = _a(nc.sbuf_tensor("xnb0", [128, D], BF16))
        xnb1 = _a(nc.sbuf_tensor("xnb1", [128, D], BF16))
        pw_a = _a(nc.sbuf_tensor("pw_a", [128, D], F32))
        ld128_sb = _a(nc.sbuf_tensor("ld128_sb", [1, D], F32))
        d128_sb = _a(nc.sbuf_tensor("d128_sb", [128, D], F32))
        c_a = _a(nc.sbuf_tensor("c_a", [128, D], BF16))
        d12_all = _a(nc.sbuf_tensor("d12_all", [128, NT], F32))
        cnt_all = _a(nc.sbuf_tensor("cnt_all", [128, NT * E], F32))
        srt_sb = _a(nc.sbuf_tensor("srt_sb", [128, E], F32))
        s01_sb = _a(nc.sbuf_tensor("s01_sb", [128, E], F32))
        eqrow_sb = _a(nc.sbuf_tensor("eqrow_sb", [128, E], F32))
        e10row_sb = _a(nc.sbuf_tensor("e10row_sb", [128, E], F32))
        tokid_sb = _a(nc.sbuf_tensor("tokid_sb", [128, NT], I32))
        colrow0 = _a(nc.sbuf_tensor("colrow0", [1, D], F32))
        colrow1 = _a(nc.sbuf_tensor("colrow1", [1, D], F32))
        zeroi_sb = _a(nc.sbuf_tensor("zeroi_sb", [128, E * RB], I32))

        t_bwt = dma("gpsimd", qg, bwt_sb[:, :].rearrange("p (dc s) -> p dc s", dc=DC),
                     bwt_in.rearrange("(dc p) s -> p dc s", p=128))
        t_wg = dma("gpsimd", qg, wg_sb[:, :].rearrange("p (dc e) -> p dc e", dc=DC),
                    wg_in.rearrange("(dc p) e -> p dc e", p=128))
        t_cwt = t_bwt = t_wg = qg.tok()
        t_eq = dma("sync", qs, eqrow_sb[:, :], eqrow_in[:, :])
        t_e10 = dma("sync", qs, e10row_sb[:, :], e10row_in[:, :])
        t_tok = dma("sync", qs, tokid_sb[:, :], tokid_in[:, :])
        t_rm = dma("sync", qs, rmask_sb[:, :], rmask_in[:, :])
        t_cm = dma("sync", qs, cmask_sb[:, :], cmask_in[:, :])
        t_z1 = op("vector", lambda e: e.memset(zeroi_sb[:, :], 0))
        t_tid0 = dma("sync", qs, tidd2d, zeroi_sb[:, :], waits=[t_z1])

        xts = [xt0, xt1]
        xnbs = [xnb0, xnb1]
        xr = x_in.rearrange("(i p) d -> i p d", p=128)
        xndr = xnd.rearrange("(i p) d -> i p d", p=128)

        t_xload = [None] * NT
        t_xm = [None] * NT
        t_xnb = [None] * NT
        t_xnw = [None] * NT

        t_xload[0] = dma("sync", qs, xts[0][:, :], xr[0])
        if NT > 1:
            t_xload[1] = dma("sync", qs, xts[1][:, :], xr[1])
        for i in range(NT):
            b = i % 2
            t1 = op("vector", lambda e, b=b: e.bn_stats(stat_sb[:, 0:6], xts[b][:, 0:512]),
                    waits=[t_xload[i]])
            t2 = op("vector", lambda e, b=b: e.bn_stats(stat_sb[:, 6:12], xts[b][:, 512:1024]))
            t3 = op("vector", lambda e: e.bn_aggr(stat_sb[:, 12:14], stat_sb[:, 0:12]))
            t3b = op("vector", lambda e: e.tensor_scalar(stat_sb[:, 13:14], stat_sb[:, 13:14], float(EPS), None, ALU.add))
            t3c = op("vector", lambda e: e.tensor_copy(out=scrv[:, :], in_=stat_sb[:, 13:14]))
            t6 = op("scalar", lambda e: e.activation(out=stat_sb[:, 14:15], in_=stat_sb[:, 13:14], func=AF.Sqrt),
                    waits=[t3c])
            t6b = op("scalar", lambda e: e.activation(out=scrs[:, :], in_=stat_sb[:, 14:15], func=AF.Copy))
            t7 = op("vector", lambda e: e.reciprocal(stat_sb[:, 14:15], stat_sb[:, 14:15]),
                    waits=[t6b])
            t_xnb[i] = op("vector", lambda e, b=b: e.tensor_scalar(xnbs[b][:, :], xts[b][:, :],
                                                                   stat_sb[:, 12:13], stat_sb[:, 14:15],
                                                                   ALU.subtract, ALU.mult),
                          waits=[t_xnw[i - 2]] if i >= 2 else [])
            t_xm[i] = t_xnb[i]
            t_xnw[i] = dma("scalar", qc, xndr[i], xnbs[b][:, :], waits=[t_xnb[i]])
            if i + 2 < NT:
                t_xload[i + 2] = dma("sync", qs, xts[b][:, :], xr[i + 2], waits=[t_xnb[i]])

        t_xnall = qc.tok()
        t_T = []
        for dc in range(DC):
            t_T.append(dma("sync", qt, xnT[:, dc * NTOK:(dc + 1) * NTOK],
                           xnd[:, dc * 128:(dc + 1) * 128], waits=[t_xnall], transpose=True))

        t_logev = [None] * NT
        t_buev = [None] * (NT // 4)
        for g in range(NT // 4):
            for i in range(4 * g, 4 * g + 4):
                pslot = PSC[:, 0:8] if i % 2 == 0 else PSC[:, 512:520]
                w = (list(t_T) + [t_wg]) if i == 0 else []
                if i >= 2:
                    w = w + [t_logev[i - 2]]
                tpe = mmg([(pslot, xnT_tile(dc, i), wg_sb[:, dc * E:(dc + 1) * E],
                            dc == 0, dc == DC - 1) for dc in range(DC)], waits=w)
                t_logev[i] = op("vector", lambda e, i=i, pslot=pslot:
                                e.tensor_copy(out=logits_all[:, i * E:(i + 1) * E], in_=pslot),
                                waits=[tpe])
            pslot = PSD[0:64, 0:512] if g % 2 == 0 else PSD[0:64, 512:1024]
            w = (list(t_T) + [t_bwt]) if g == 0 else []
            if g >= 2:
                w = w + [t_buev[g - 2]]
            tpe = mmg([(pslot, bwt_sb[:, dc * S:(dc + 1) * S],
                        xnT[:, dc * NTOK + g * 512: dc * NTOK + (g + 1) * 512],
                        dc == 0, dc == DC - 1) for dc in range(DC)], waits=w)
            t_buev[g] = op("vector", lambda e, g=g, pslot=pslot:
                           e.tensor_copy(out=bu_sb[:, g * 512:(g + 1) * 512], in_=pslot),
                           waits=[tpe])
            t_buev[g] = op("vector", lambda e: e.tensor_copy(out=scrv[0:64, :], in_=bu_sb[0:64, g * 512:g * 512 + 1]))

        t_l128 = op("vector", lambda e: e.tensor_scalar(ld128_sb[:, :], ldlog_sb[:, :], 128.0, None, ALU.mult),
                    waits=[t_ldlog])
        t_d128mm = mmg([(PSB[:, 0:512], o1x128f_sb[:, :], ld128_sb[:, 0:512], True, True),
                        (PSB[:, 512:1024], o1x128f_sb[:, :], ld128_sb[:, 512:1024], True, True)],
                       waits=[t_l128, t_o1f])
        t_d128 = op("scalar", lambda e: e.activation(out=d128_sb[:, :], in_=PSB[:, :], func=AF.Exp),
                    waits=[t_d128mm])
        t_pb = [None] * NT
        t_c = [None] * NT
        t_cs = [None] * NT
        t_cadd = [None] * NT
        t_pwv = [None] * NT
        for i in range(NT):
            pbt = PSA if i % 2 == 0 else PSC
            pwt = PSB if i % 2 == 0 else PSD
            w = [t_buev[i // 4], t_cwt]
            if i >= 2:
                w.append(t_c[i - 2])
            if i in (0, 1):
                w.append(t_logev[NT - 1])
            t_pb[i] = mmg(
                [(pbt[:, 0:512], bu_sb[0:64, i * 128:(i + 1) * 128], cwt_sb[:, 0:512], True, True),
                 (pbt[:, 512:1024], bu_sb[0:64, i * 128:(i + 1) * 128], cwt_sb[:, 512:1024], True, True)],
                waits=w)
            if i == 0:
                t_pw = mmg(
                    [(pwt[:, 0:512], iotag_sb[:, 0:128], ldlog_sb[:, 0:512], True, True),
                     (pwt[:, 512:1024], iotag_sb[:, 0:128], ldlog_sb[:, 512:1024], True, True)],
                    waits=[t_iog, t_ldlog, t_d128])
                t_pwv[i] = op("scalar", lambda e, p=pwt: e.activation(out=pw_a[:, :], in_=p[:, :], func=AF.Exp),
                              waits=[t_pw])
            else:
                t_pwv[i] = op("vector", lambda e: e.tensor_mul(pw_a[:, :], pw_a[:, :], d128_sb[:, :]),
                              waits=[t_c[i - 1], t_d128])
            t_c[i] = op("vector", lambda e, p=pbt: e.tensor_mul(c_a[:, :], p[:, :], pw_a[:, :]),
                        waits=[t_pwv[i], t_pb[i]] + ([t_cs[i - 1]] if i >= 1 else []))
            t_cb = op("vector", lambda e: e.tensor_copy(out=scrv[:, :], in_=c_a[:, 0:1]))
            w = [t_cb, t_o128b]
            if i == 0:
                w.append(t_pwv[0])
            if i == 1:
                w.append(t_buev[NT // 4 - 1])
            if i >= 2:
                w.append(t_cadd[i - 2])
            t_cs[i] = mmg(
                [(pwt[0:1, 0:512], o128b_sb[:, :], c_a[:, 0:512], True, True),
                 (pwt[0:1, 512:1024], o128b_sb[:, :], c_a[:, 512:1024], True, True)],
                waits=w)
            cr = [colrow0, colrow1][i % 2]
            tcp = op("scalar", lambda e, p=pwt, cr=cr: e.copy(out=cr[:, :], in_=p[0:1, 0:1024]),
                     waits=[t_cs[i]] + ([t_cadd[i - 2]] if i >= 2 else []))
            t_cadd[i] = dma("scalar", qc, carrd[i:i + 1, :], cr[:, :], waits=[tcp])
        t_c32 = dma("sync", qs, carr32_sb[:, :], carrd[:, :], waits=[t_cadd[NT - 1], t_cadd[NT - 2]])
        t_cl = mmg([(PSB[0:1, 0:512], o128f_sb[0:32, 0:1], carr32_sb[:, 0:512], True, True),
                    (PSB[0:1, 512:1024], o128f_sb[0:32, 0:1], carr32_sb[:, 512:1024], True, True)],
                   waits=[t_c32, t_o128f, t_cs[NT - 1], t_cs[NT - 2]])
        t_carry_done = op("vector", lambda e: e.tensor_copy(out=carry_local[:, :], in_=PSB[0:1, 0:1024]),
                          waits=[t_cl, t_z3])

        t_crow = [None] * NT
        t_crowev = [None] * NT
        t_cntev = [None] * NT
        t_oh1 = [None] * NT
        t_oh2 = [None] * NT
        t_d12 = [None] * NT
        for i in range(NT):
            lg = logits_all[:, i * E:(i + 1) * E]
            tsrt = op("vector", lambda e, lg=lg: e.max(srt_sb[:, :], lg), waits=[t_logev[i]])
            t_oh1[i] = op("vector", lambda e, lg=lg, i=i:
                          e.tensor_scalar(oh1_all[:, i * E:(i + 1) * E], lg, srt_sb[:, 0:1], None, ALU.is_equal))
            t_oh2[i] = op("vector", lambda e, lg=lg, i=i:
                          e.tensor_scalar(oh2_all[:, i * E:(i + 1) * E], lg, srt_sb[:, 1:2], None, ALU.is_equal))
            t_d12[i] = op("vector", lambda e, i=i:
                          e.tensor_scalar(d12_all[:, i:i + 1], srt_sb[:, 0:1], srt_sb[:, 1:2], None, ALU.subtract))
            ts01 = op("vector", lambda e, i=i:
                      e.tensor_add(s01_sb[:, :], oh1_all[:, i * E:(i + 1) * E], oh2_all[:, i * E:(i + 1) * E]))
            ts01b = op("vector", lambda e: e.tensor_copy(out=scrv[:, :], in_=s01_sb[:, 0:1]))
            w = [ts01b, t_trikf, t_o1f, t_cadd[NT - 1]]
            if i >= 1:
                w += [t_crowev[i - 1], t_cntev[i - 1]]
            if i == 0:
                w.append(t_cadd[NT - 1])
            mlist = [(PSA[:, 0:8], trikf_sb[:, :], s01_sb[:, :], True, i == 0)]
            if i >= 1:
                mlist.append((PSA[:, 0:8], o1x128f_sb[:, :], crow_sb[:, :], False, True))
            t_cnt = mmg(mlist, waits=w)
            t_cntev[i] = op("vector", lambda e, i=i: e.tensor_copy(out=cnt_all[:, i * E:(i + 1) * E], in_=PSA[:, 0:8]),
                            waits=[t_cnt])
            t_crow[i] = mmg([(PSB[0:1, 0:8], o128f_sb[:, :], s01_sb[:, :], i == 0, True)],
                            waits=[ts01b, t_o128f] + ([t_cadd[NT - 1]] if i == 0 else [t_crowev[i - 1]]))
            t_crowev[i] = op("scalar", lambda e: e.copy(out=crow_sb[:, :], in_=PSB[0:1, 0:8]),
                             waits=[t_crow[i]])

        oh1_3 = oh1_all[:, :].rearrange("p (t e) -> p t e", e=E)
        oh2_3 = oh2_all[:, :].rearrange("p (t e) -> p t e", e=E)
        cnt_3 = cnt_all[:, :].rearrange("p (t e) -> p t e", e=E)
        tmp_3 = tmp256[:, :].rearrange("p (t e) -> p t e", e=E)
        op("vector", lambda e: e.tensor_mul(tmp_3, oh1_3, cnt_3), waits=[t_cntev[NT - 1]])
        t_r0 = op("vector", lambda e: e.reduce_sum(rank0_all[:, :].unsqueeze(2), tmp_3, axis=AXX))
        op("vector", lambda e: e.tensor_mul(tmp_3, oh2_3, cnt_3))
        t_r1 = op("vector", lambda e: e.reduce_sum(rank1_all[:, :].unsqueeze(2), tmp_3, axis=AXX))

        t_d12b = op("vector", lambda e: e.tensor_copy(out=scrv[:, :], in_=d12_all[:, NT - 1:NT]),
                    waits=[t_d12[NT - 1]])
        t_g1 = op("scalar", lambda e: e.activation(out=g1_all[:, :], in_=d12_all[:, :], func=AF.Sigmoid),
                  waits=[t_d12b])
        t_g2 = op("vector", lambda e: e.tensor_scalar(g2_all[:, :], g1_all[:, :], -1.0, 1.0, ALU.mult, ALU.add),
                  waits=[t_g1])

        eqrow_b = eqrow_sb[:, :].unsqueeze(1).broadcast_to([128, NT, E])
        t_dest = {}
        for (slot, oh_3x, rank_all_x, dall, pdall) in (
            (0, oh1_3, rank0_all, d20_all, pdest0_sb),
            (1, oh2_3, rank1_all, d21_all, pdest1_sb),
        ):
            rk = rank_all_x[:, :]
            op("vector", lambda e, oh_3x=oh_3x: e.tensor_mul(tmp_3, oh_3x, eqrow_b),
               waits=[t_eq, t_r0, t_r1])
            op("vector", lambda e: e.reduce_sum(t32a[:, :].unsqueeze(2), tmp_3, axis=AXX))
            op("vector", lambda e, rk=rk: e.tensor_add(t32b[:, :], t32a[:, :], rk))
            t_pd = op("vector", lambda e, pdall=pdall: e.tensor_copy(out=pdall[:, :], in_=t32b[:, :]))
            op("vector", lambda e, pdall=pdall: e.tensor_scalar(i32a[:, :], pdall[:, :], 7, None, ALU.arith_shift_right))
            op("vector", lambda e, rk=rk: e.tensor_copy(out=i32b[:, :], in_=rk))
            op("vector", lambda e: e.tensor_scalar(i32b[:, :], i32b[:, :], 127, None, ALU.bitwise_and))
            op("vector", lambda e: e.tensor_scalar(i32c[:, :], i32b[:, :], 6, None, ALU.logical_shift_left))
            op("vector", lambda e: e.tensor_scalar(i32b[:, :], i32b[:, :], 4, None, ALU.logical_shift_left))
            op("vector", lambda e: e.tensor_add(i32c[:, :], i32c[:, :], i32b[:, :]))
            t_d2 = op("vector", lambda e, dall=dall: e.tensor_add(dall[:, :], i32c[:, :], i32a[:, :]))
            t_dest[slot] = (t_d2, t_pd)
        t_last_meta = t_dest[1][0]

        t_tscat = []
        for i in range(NT):
            for (slot, dall) in ((0, d20_all), (1, d21_all)):
                t_tscat.append(idma(tidd, IndirectOffsetOnAxis(dall[:, i:i + 1], 1),
                                    tokid_sb[:, i:i + 1], None,
                                    waits=[t_dest[slot][0], t_tid0, t_tok]))
        t_tidld = dma("sync", qs, tidt_sb[:, :], tidd2d, waits=[t_tscat[-1]])

        tp1 = op("vector", lambda e: e.tensor_copy(out=pack_sb[:, 0:1024], in_=carry_local[:, :]),
                 waits=[t_carry_done])
        t_crowb = op("scalar", lambda e: e.activation(out=scrs[0:1, :], in_=crow_sb[:, 0:1], func=AF.Copy),
                     waits=[t_crowev[NT - 1]])
        tp2 = op("vector", lambda e: e.tensor_copy(out=pack_sb[:, 1024:1032], in_=crow_sb[:, :]),
                 waits=[t_crowb])
        t_ccw = dma("gpsimd", qg, ccin[:, :], pack_sb[:, :], waits=[tp2])
        cc.v += 1
        t_cc = cc.tok()

        def cc_fn(e, w=t_ccw, h=cc.h):
            e.wait_ge(w[0], w[1])
            e.collective_compute(
                "AllGather", ALU.bypass,
                replica_groups=[list(range(NCORE))],
                ins=[ccin.opt()], outs=[ccout.opt()],
            ).then_inc(h, 1)
        emit("gpsimd", cc_fn)
        t_ccr = dma("gpsimd", qg, gath_sb[:, :], ccout[:, :], waits=[t_cc])

        t_offs = mmg([(PSA[0:1, 0:8], rmask_sb[:, :], gath_sb[0:8, 1024:1032], True, True)],
                     waits=[t_ccr, t_rm, t_cntev[NT - 1]])
        t_offsev = op("vector", lambda e: e.tensor_copy(out=offsrow_sb[:, :], in_=PSA[0:1, 0:8]),
                      waits=[t_offs])
        t_offsevb = op("vector", lambda e: e.tensor_copy(out=scrv[0:1, :], in_=offsrow_sb[:, 0:1]))
        t_offsb = mmg([(PSA[:, 0:8], o1x128f_sb[:, :], offsrow_sb[:, :], True, True)],
                      waits=[t_offsevb])
        t_offsbev = op("vector", lambda e: e.tensor_copy(out=offsb_sb[:, :], in_=PSA[:, 0:8]),
                       waits=[t_offsb])
        t_c0 = mmg([(PSB[0:1, 0:512], cmask_sb[:, :], gath_sb[0:8, 0:512], True, True),
                    (PSB[0:1, 512:1024], cmask_sb[:, :], gath_sb[0:8, 512:1024], True, True)],
                   waits=[t_ccr, t_cm, t_crowev[NT - 1]])
        t_c0ev = op("vector", lambda e: e.tensor_copy(out=carry0_sb[:, :], in_=PSB[0:1, 0:1024]),
                    waits=[t_c0])
        t_c0evb = op("vector", lambda e: e.tensor_copy(out=scrv[0:1, :], in_=carry0_sb[:, 0:1]))
        t_carr = mmg([(PSC[0:32, 0:512], trikf_sb[0:32, 0:32], carr32_sb[:, 0:512], True, False),
                      (PSC[0:32, 0:512], o1x128f_sb[:, 0:32], carry0_sb[:, 0:512], False, True),
                      (PSC[0:32, 512:1024], trikf_sb[0:32, 0:32], carr32_sb[:, 512:1024], True, False),
                      (PSC[0:32, 512:1024], o1x128f_sb[:, 0:32], carry0_sb[:, 512:1024], False, True)],
                     waits=[t_c0evb, t_c[NT - 1]])
        t_carrev = op("vector", lambda e: e.tensor_copy(out=carrb32_sb[:, :], in_=PSC[0:32, 0:1024]),
                      waits=[t_carr])
        t_carrw = dma("scalar", qc, carrbd[:, :], carrb32_sb[:, :], waits=[t_carrev])

        offsb_b = offsb_sb[:, :].unsqueeze(1).broadcast_to([128, NT, E])
        t_gate = {}
        for (slot, oh_3x, rank_all_x, g_all_x, gate_all_x) in (
            (0, oh1_3, rank0_all, g1_all, gate0_all),
            (1, oh2_3, rank1_all, g2_all, gate1_all),
        ):
            rk = rank_all_x[:, :]
            op("vector", lambda e, oh_3x=oh_3x: e.tensor_mul(tmp_3, oh_3x, offsb_b),
               waits=[t_offsbev, t_g2])
            op("vector", lambda e: e.reduce_sum(t32a[:, :].unsqueeze(2), tmp_3, axis=AXX))
            op("vector", lambda e, rk=rk: e.tensor_add(t32b[:, :], t32a[:, :], rk))
            op("vector", lambda e: e.tensor_scalar(t32c[:, :], t32b[:, :], CAP - 0.5, None, ALU.is_lt))
            t_gate[slot] = op("vector", lambda e, g_all_x=g_all_x, gate_all_x=gate_all_x:
                              e.tensor_mul(gate_all_x[:, :], g_all_x[:, :], t32c[:, :]))

        if DEBUG:
            dma("sync", qs, dbg["logits"][:, :], logits_all[:, :], waits=[t_logev[NT - 1]])
            dma("sync", qs, dbg["carry"][:, :], carry_local[:, :], waits=[t_carry_done])
            dma("sync", qs, dbg["rank0"][:, :], rank0_all[:, :], waits=[t_r0])
            dma("sync", qs, dbg["rank1"][:, :], rank1_all[:, :], waits=[t_r1])
            dma("sync", qs, dbg["gate0"][:, :], gate0_all[:, :], waits=[t_gate[0]])
            dma("sync", qs, dbg["gate1"][:, :], gate1_all[:, :], waits=[t_gate[1]])
            dma("sync", qs, dbg["pdest0"][:, :], pdest0_sb[:, :], waits=[t_last_meta])
            dma("sync", qs, dbg["pdest1"][:, :], pdest1_sb[:, :], waits=[t_last_meta])
            dma("sync", qs, dbg["tidT"][:, :], tidt_sb[:, :], waits=[t_tidld])

        t_b0_psA_free = t_offsbev
        t_b0_psB_free = t_c0ev
        t_logits_psC_free = t_c[NT - 1]
        t_bu_psD_free = t_cadd[NT - 1]

    a_ctx.close()

    b_ctx = contextlib.ExitStack()
    w1p0 = b_ctx.enter_context(nc.sbuf_tensor("w1p0", [128, DC * 512], BF16))
    w1p1 = b_ctx.enter_context(nc.sbuf_tensor("w1p1", [128, DC * 512], BF16))
    w2_sb = b_ctx.enter_context(nc.sbuf_tensor("w2_sb", [128, HC * D], BF16))
    hT = b_ctx.enter_context(nc.sbuf_tensor("hT", [128, HC * HQ], BF16))
    packT = b_ctx.enter_context(nc.sbuf_tensor("packT", [128, DC * Q], BF16))
    rows0 = b_ctx.enter_context(nc.sbuf_tensor("rows0", [128, D], BF16))
    rows1 = b_ctx.enter_context(nc.sbuf_tensor("rows1", [128, D], BF16))
    rows2 = b_ctx.enter_context(nc.sbuf_tensor("rows2", [128, D], BF16))
    rows3 = b_ctx.enter_context(nc.sbuf_tensor("rows3", [128, D], BF16))
    ysb0 = b_ctx.enter_context(nc.sbuf_tensor("ysb0", [128, D], BF16))
    ysb1 = b_ctx.enter_context(nc.sbuf_tensor("ysb1", [128, D], BF16))
    b1t_sb = b_ctx.enter_context(nc.sbuf_tensor("b1t_sb", [128, E * HC], F32))
    b2_sb = b_ctx.enter_context(nc.sbuf_tensor("b2_sb", [1, 2 * D], BF16))
    w1ps = [w1p0, w1p1]
    rowss = [rows0, rows1, rows2, rows3]
    ysbs = [ysb0, ysb1]

    t_b1t = dma("sync", qs, b1t_sb[:, :], b1t_in[:, :], waits=[t_tidld])
    t_b2_e = [None] * E

    RCH = [(0, 512), (512, 128)]

    pydr = pyd.rearrange("(r p) d -> r p d", p=128)
    packdrs = [packd0.rearrange("(r p) d -> r p d", p=128),
               packd1.rearrange("(r p) d -> r p d", p=128)]
    packds = [packd0, packd1]

    t_packw = {}
    t_gelu_seq = []
    t_yev_all = []
    t_yw_all = []
    t_mm1_last = None
    t_mm2_last = None
    t_mm2_by_e = {}
    t_transp_done = None
    t_transp_prev = None
    t_hdump = None
    w1k = 0

    for e in range(E):
        for rb in range(RB):
            k = e * RB + rb
            war = [t_packw[k - 4]] if k >= 4 else []
            tg_ = idma(rowss[k % 4][:, :], None, xnd,
                       IndirectOffsetOnAxis(tidt_sb[:, e * RB + rb:e * RB + rb + 1], 0),
                       waits=[t_tidld] + war)
            w = [tg_]
            if e >= 2:
                w.append(t_transp_prev)
            t_packw[k] = dma("sync", qs, packdrs[e % 2][rb], rowss[k % 4][:, :], waits=w)
        tts = []
        for dc in range(DC):
            w = [t_packw[e * RB + RB - 1]]
            if e >= 1:
                w.append(t_mm1_last)
            tts.append(dma("sync", qt, packT[:, dc * Q:(dc + 1) * Q],
                           packds[e % 2][:, dc * 128:(dc + 1) * 128], waits=w, transpose=True))
        t_transp_prev = t_transp_done
        t_transp_done = tts[-1]
        if DEBUG2:
            dma("sync", qs, dbg["packT2"].rearrange("(e p) c -> e p c", p=128)[e],
                packT[:, :], waits=[t_transp_done])

        t_w2_done = None
        w2_pieces = []
        for kpc in range(8):
            w = [t_mm2_last] if (e >= 1 and kpc == 0) else []
            if e == 0 and kpc == 0:
                w = [t_tidld]
            w2_pieces.append((kpc, w))

        for half in range(2):
            t_w1p = [None] * 8
            pbase = w1k

            def issue_w1_piece(pp, e=e, pbase=pbase, t_w1p=t_w1p):
                w = []
                k = pbase + pp
                if k >= 2:
                    w.append(t_gelu_seq[(k - 1) * 4 - 1])
                t_w1p[pp] = dma("gpsimd", qw1[k % 2],
                                w1ps[k % 2][:, :].rearrange("p (dc hh) -> p dc hh", dc=DC),
                                w1_in[e][:, pp * 512:(pp + 1) * 512].rearrange("(dc p) hh -> p dc hh", p=128),
                                waits=w)

            def issue_w2_pieces(n, e=e):
                nonlocal_t = None
                for _ in range(n):
                    if w2_pieces:
                        kpc, wp = w2_pieces.pop(0)
                        nonlocal_t = dma("gpsimd", qw2,
                                         w2_sb[:, kpc * 4 * D:(kpc + 1) * 4 * D].rearrange("p (hc d) -> p hc d", hc=4),
                                         w2_in[e][kpc * 512:(kpc + 1) * 512, :].rearrange("(hc p) d -> p hc d", p=128),
                                         waits=wp)
                return nonlocal_t

            for hc in range(HC):
                pp = hc // 4
                hcl = hc % 4
                if hc == 0:
                    issue_w1_piece(0)
                    if half == 0:
                        tw2 = issue_w2_pieces(2)
                        t_w2_done = tw2 or t_w2_done
                    issue_w1_piece(1)
                    if half == 0:
                        tw2 = issue_w2_pieces(1)
                        t_w2_done = tw2 or t_w2_done
                elif hc % 4 == 0 and hc // 4 + 1 < 8:
                    issue_w1_piece(hc // 4 + 1)
                    if half == 0:
                        tw2 = issue_w2_pieces(1)
                        t_w2_done = tw2 or t_w2_done
                w1buf = w1ps[(pbase + pp) % 2]
                m1k = len(t_gelu_seq)
                pm1 = PSA if m1k % 2 == 0 else PSC
                w = [t_w1p[pp], t_transp_done]
                if m1k >= 2:
                    w.append(t_gelu_seq[m1k - 2])
                elif m1k == 0:
                    w += [t_b0_psA_free]
                elif m1k == 1:
                    w += [t_logits_psC_free]
                if hc == 0 and t_mm2_last is not None:
                    w.append(t_mm2_last)
                gro = half * HQ
                mms = []
                for dc in range(DC):
                    lhs = w1buf[:, dc * 512 + hcl * 128: dc * 512 + (hcl + 1) * 128]
                    mms.append((pm1[:, 0:512], lhs, packT[:, dc * Q + gro: dc * Q + gro + 512],
                                dc == 0, dc == DC - 1))
                    mms.append((pm1[:, 512:640], lhs, packT[:, dc * Q + gro + 512: dc * Q + gro + 640],
                                dc == 0, dc == DC - 1))
                tmm = mmg(mms, waits=w)
                wg_ = [tmm, t_b1t]
                if hc == 0 and t_mm2_last is not None:
                    wg_.append(t_mm2_last)
                if hc == 0 and t_hdump is not None:
                    wg_.append(t_hdump)
                tg = op("scalar",
                        lambda ee, pm1=pm1, hc=hc, e=e:
                        ee.activation(out=hT[:, hc * HQ: hc * HQ + HQ],
                                      in_=pm1[:, 0:HQ], func=AF.Gelu_apprx_tanh,
                                      bias=b1t_sb[:, e * HC + hc: e * HC + hc + 1]),
                        waits=wg_)
                t_gelu_seq.append(tg)
            w1k += 8
            t_mm1_last = t_gelu_seq[-1]
            if half == 0:
                wb2 = [t_tidld] if e == 0 else ([t_mm2_by_e[e - 2]] if e >= 2 else [])
                t_b2_e[e] = dma("gpsimd", qb2[e % 2], b2_sb[:, (e % 2) * D:(e % 2 + 1) * D],
                                b2_in[:, e * D:(e + 1) * D], waits=wb2)
            if DEBUG2:
                t_hdump = dma("sync", qs, dbg["hT2"].rearrange("(eh p) c -> eh p c", p=128)[e * 2 + half],
                              hT[:, :], waits=[t_mm1_last])

            for rbl in range(5):
                yk = len(t_yev_all)
                pm2 = PSB if yk % 2 == 0 else PSD
                grb = half * 5 + rbl
                w = []
                if rbl == 0:
                    w = [t_mm1_last, t_w2_done, t_b2_e[e], t_o1b]
                if yk >= 2:
                    w.append(t_yev_all[yk - 2])
                elif yk == 0:
                    w += [t_b0_psB_free]
                elif yk == 1:
                    w += [t_bu_psD_free]
                mms = []
                for hc in range(HC):
                    lhs = hT[:, hc * HQ + rbl * 128: hc * HQ + (rbl + 1) * 128]
                    mms.append((pm2[:, 0:512], lhs, w2_sb[:, hc * D: hc * D + 512],
                                hc == 0, False))
                    mms.append((pm2[:, 512:1024], lhs, w2_sb[:, hc * D + 512: (hc + 1) * D],
                                hc == 0, False))
                eo = (e % 2) * D
                mms.append((pm2[:, 0:512], o1x128b_sb[:, :], b2_sb[:, eo: eo + 512],
                            False, True))
                mms.append((pm2[:, 512:1024], o1x128b_sb[:, :], b2_sb[:, eo + 512: eo + D],
                            False, True))
                t_mm2_last = mmg(mms, waits=w)
                tv = op("vector",
                        lambda e_, pm2=pm2, yk=yk:
                        e_.tensor_copy(out=ysbs[yk % 2][:, :], in_=pm2[:, 0:1024]),
                        waits=[t_mm2_last] + ([t_yw_all[yk - 2]] if yk >= 2 else []))
                t_yev_all.append(tv)
                t_yw_all.append(dma("scalar", qc, pydr[e * RB + grb], ysbs[yk % 2][:, :], waits=[tv]))
        t_mm2_by_e[e] = t_mm2_last

    t_py_all = t_yw_all[-1]
    b_ctx.close()

    c_ctx = contextlib.ExitStack()
    if True:
        _c = c_ctx.enter_context
        xnT2 = _c(nc.sbuf_tensor("xnT2", [128, DC * NTOK], BF16))
        gwt_sb = _c(nc.sbuf_tensor("gwt_sb", [128, DC * D], BF16))
        gateb_sb = _c(nc.sbuf_tensor("gateb_sb", [1, D], BF16))
        pw_c = _c(nc.sbuf_tensor("pw_c", [128, D], F32))
        ctile0 = _c(nc.sbuf_tensor("ctile0", [1, D], BF16))
        ctile1 = _c(nc.sbuf_tensor("ctile1", [1, D], BF16))
        d128c_sb = _c(nc.sbuf_tensor("d128c_sb", [128, D], F32))
        ld128c_sb = _c(nc.sbuf_tensor("ld128c_sb", [1, D], F32))
        c_c = _c(nc.sbuf_tensor("c_c", [128, D], BF16))
        gate_c0 = _c(nc.sbuf_tensor("gate_c0", [128, D], F32))
        gate_c1 = _c(nc.sbuf_tensor("gate_c1", [128, D], F32))
        sg2_c = _c(nc.sbuf_tensor("sg2_c", [128, D], F32))
        sg3_c = _c(nc.sbuf_tensor("sg3_c", [128, D], F32))
        s_c = _c(nc.sbuf_tensor("s_c", [128, D], F32))
        ya0 = _c(nc.sbuf_tensor("ya0", [128, D], BF16))
        ya1 = _c(nc.sbuf_tensor("ya1", [128, D], BF16))
        yb0 = _c(nc.sbuf_tensor("yb0", [128, D], BF16))
        yb1 = _c(nc.sbuf_tensor("yb1", [128, D], BF16))
        ysum_c = _c(nc.sbuf_tensor("ysum_c", [128, D], F32))
        xt0c = _c(nc.sbuf_tensor("xt0c", [128, D], F32))
        xt1c = _c(nc.sbuf_tensor("xt1c", [128, D], F32))
        osb0 = _c(nc.sbuf_tensor("osb0", [128, D], F32))
        osb1 = _c(nc.sbuf_tensor("osb1", [128, D], F32))
        yas = [ya0, ya1]
        ybs = [yb0, yb1]
        xtc = [xt0c, xt1c]
        osb = [osb0, osb1]
        xr = x_in.rearrange("(i p) d -> i p d", p=128)
        outr = out_io.rearrange("(i p) d -> i p d", p=128)

        t_bdrain = [t_py_all, t_mm2_last, t_yev_all[-1], t_gelu_seq[-1]]
        t_T2 = []
        for dc in range(DC):
            t_T2.append(dma("sync", qt, xnT2[:, dc * NTOK:(dc + 1) * NTOK],
                            xnd[:, dc * 128:(dc + 1) * 128],
                            waits=t_bdrain if dc == 0 else [], transpose=True))
        t_gwt = dma("gpsimd", qgw, gwt_sb[:, :].rearrange("p (dc e) -> p dc e", dc=DC),
                    gwt_in.rearrange("(dc p) e -> p dc e", p=128),
                    waits=t_bdrain)
        t_gateb = dma("gpsimd", qgw, gateb_sb[:, :], gateb_in[:, :], waits=[])

        def xnT2_tile(dc, i):
            return xnT2[:, dc * NTOK + i * 128: dc * NTOK + (i + 1) * 128]

        t_l128c = op("vector", lambda e: e.tensor_scalar(ld128c_sb[:, :], ldlog_sb[:, :], 128.0, None, ALU.mult))
        t_l128cb = op("vector", lambda e: e.tensor_copy(out=scrv[0:1, :], in_=ld128c_sb[:, 0:1]),
                      waits=[t_l128c])
        t_d128cmm = mmg([(PSB[:, 0:512], o1x128f_sb[:, :], ld128c_sb[:, 0:512], True, True),
                         (PSB[:, 512:1024], o1x128f_sb[:, :], ld128c_sb[:, 512:1024], True, True)],
                        waits=[t_l128cb] + t_bdrain + [t_yev_all[-1]])
        t_d128c = op("scalar", lambda e: e.activation(out=d128c_sb[:, :], in_=PSB[:, :], func=AF.Exp),
                     waits=[t_d128cmm])

        ctiles = [ctile0, ctile1]
        t_ct = [None] * NT
        t_h2 = [None] * NT
        t_xl = [None] * NT
        t_out = [None] * NT
        t_oa = [None] * NT
        t_sgm = [None] * NT
        t_exp2 = [None] * NT
        t_c2 = [None] * NT
        t_s2 = [None] * NT
        t_ct[0] = dma("sync", qs, ctiles[0][:, :], carrbd[0:1, :], waits=[t_carrw] + t_bdrain)
        t_ct[1] = dma("sync", qs, ctiles[1][:, :], carrbd[1:2, :], waits=[])
        t_ysum = [None] * NT
        t_sil = [None] * NT
        t_sgmul = [None] * NT

        t_xl[0] = dma("sync", qs, xtc[0][:, :], xr[0], waits=t_bdrain)
        t_xl[1] = dma("sync", qs, xtc[1][:, :], xr[1])

        gate_cs = [gate_c0, gate_c1]
        t_tyb = [None] * NT

        def emit_front(i):
            par = i % 2
            pbt = PSA if par == 0 else PSC
            pwt = PSB if par == 0 else PSD
            w = [t_cwt]
            if i >= 2:
                w += [t_c2[i - 2], t_s2[i - 2]]
            if i < 2:
                w += t_bdrain + [t_yev_all[-1]]
            t_pb2 = mmg(
                [(pbt[:, 0:512], bu_sb[0:64, i * 128:(i + 1) * 128], cwt_sb[:, 0:512], True, True),
                 (pbt[:, 512:1024], bu_sb[0:64, i * 128:(i + 1) * 128], cwt_sb[:, 512:1024], True, True)],
                waits=w)
            if i == 0:
                t_pw2 = mmg(
                    [(pwt[:, 0:512], iotag_sb[:, 0:128], ldlog_sb[:, 0:512], True, True),
                     (pwt[:, 512:1024], iotag_sb[:, 0:128], ldlog_sb[:, 512:1024], True, True)],
                    waits=[t_iog, t_ldlog, t_d128c])
                t_exp2[i] = op("scalar", lambda e, p=pwt: e.activation(out=pw_c[:, :], in_=p[:, :], func=AF.Exp),
                               waits=[t_pw2])
            else:
                t_exp2[i] = op("vector", lambda e: e.tensor_mul(pw_c[:, :], pw_c[:, :], d128c_sb[:, :]),
                               waits=[t_c2[i - 1], t_d128c])
            t_c2[i] = op("vector", lambda e, p=pbt: e.tensor_mul(c_c[:, :], p[:, :], pw_c[:, :]),
                         waits=[t_exp2[i], t_pb2] + ([t_h2[i - 1]] if i >= 1 else []))
            ctb = ctiles[i % 2]
            t_c2b = op("vector", lambda e: e.tensor_copy(out=scrv[:, :], in_=c_c[:, 0:1]))
            t_h2[i] = mmg(
                [(pbt[:, 0:512], trikb_sb[:, :], c_c[:, 0:512], True, False),
                 (pbt[:, 0:512], o1x128b_sb[:, :], ctb[:, 0:512], False, True),
                 (pbt[:, 512:1024], trikb_sb[:, :], c_c[:, 512:1024], True, False),
                 (pbt[:, 512:1024], o1x128b_sb[:, :], ctb[:, 512:1024], False, True)],
                waits=[t_c2b, t_trikb, t_o1b, t_ct[i]])
            if i + 2 < NT:
                t_ct[i + 2] = dma("sync", qs, ctb[:, :], carrbd[i + 2:i + 3, :], waits=[t_h2[i]])
            mms = []
            for dc in range(DC):
                mms.append((pwt[:, 0:512], xnT2_tile(dc, i), gwt_sb[:, dc * D: dc * D + 512],
                            dc == 0, False))
                mms.append((pwt[:, 512:1024], xnT2_tile(dc, i), gwt_sb[:, dc * D + 512: (dc + 1) * D],
                            dc == 0, False))
            mms.append((pwt[:, 0:512], o1x128b_sb[:, :], gateb_sb[:, 0:512], False, True))
            mms.append((pwt[:, 512:1024], o1x128b_sb[:, :], gateb_sb[:, 512:1024], False, True))
            w = [t_gwt, t_gateb, t_o1b]
            if i >= 2:
                w.append(t_sgm[i - 2])
            if i < 2:
                w.append(t_yev_all[-1])
            if i == 0:
                w.append(t_exp2[0])
            t_gmm = mmg(mms, waits=w + (list(t_T2) if i == 0 else []))
            t_sgm[i] = op("scalar", lambda e, p=pwt, i=i: e.activation(out=gate_cs[i % 2][:, :], in_=p[:, :], func=AF.Sigmoid),
                          waits=[t_gmm] + ([t_s2[i - 2]] if i >= 2 else []))

        def emit_tail(i):
            par = i % 2
            pbt = PSA if par == 0 else PSC
            t_s2[i] = op("vector", lambda e, p=pbt, i=i: e.tensor_mul(s_c[:, :], p[:, :], gate_cs[i % 2][:, :]),
                         waits=[t_sgm[i], t_h2[i]] + ([t_sgmul[i - 1]] if i >= 1 else []))
            if DEBUG:
                dma("scalar", qc, dbg["s"].rearrange("(i p) d -> i p d", p=128)[i], s_c[:, :],
                    waits=[t_s2[i]])
            w = [t_py_all]
            if i >= 2:
                w.append(t_ysum[i - 2])
            t_ya = idma(yas[par][:, :], None, pyd,
                        IndirectOffsetOnAxis(pdest0_sb[:, i:i + 1], 0), waits=w)
            t_yb = idma(ybs[par][:, :], None, pyd,
                        IndirectOffsetOnAxis(pdest1_sb[:, i:i + 1], 0), waits=[])
            tyb = op("vector", lambda e, par=par, i=i: e.tensor_scalar(sg2_c[:, :], ybs[par][:, :],
                                                                        gate1_all[:, i:i + 1], None, ALU.mult),
                     waits=[t_yb, t_gate[1]] + ([t_ysum[i - 1]] if i >= 1 else []))
            t_ysum[i] = op("vector", lambda e, par=par, i=i: e.scalar_tensor_tensor(
                               ysum_c[:, :], yas[par][:, :], gate0_all[:, i:i + 1], sg2_c[:, :],
                               ALU.mult, ALU.add),
                           waits=[t_ya, tyb, t_gate[0]] + ([t_sil[i - 1]] if i >= 1 else []))
            tsg2 = op("scalar", lambda e: e.activation(out=sg3_c[:, :], in_=ysum_c[:, :], func=AF.Sigmoid),
                      waits=[t_ysum[i]] + ([t_sil[i - 1]] if i >= 1 else []))
            t_sil[i] = op("vector", lambda e: e.tensor_mul(ysum_c[:, :], ysum_c[:, :], sg3_c[:, :]),
                          waits=[tsg2])
            if DEBUG:
                dma("scalar", qc, dbg["g"].rearrange("(i p) d -> i p d", p=128)[i], ysum_c[:, :],
                    waits=[t_sil[i]])
            t_sgmul[i] = op("vector", lambda e: e.tensor_mul(s_c[:, :], s_c[:, :], ysum_c[:, :]),
                            waits=[t_sil[i], t_s2[i]])
            t_oa[i] = op("vector", lambda e, par=par: e.tensor_add(osb[par][:, :], s_c[:, :], xtc[par][:, :]),
                         waits=[t_sgmul[i], t_xl[i]] + ([t_out[i - 2]] if i >= 2 else []))
            t_out[i] = dma("sync", qs, outr[i], osb[par][:, :], waits=[t_oa[i]])
            if i + 2 < NT:
                t_xl[i + 2] = dma("sync", qs, xtc[par][:, :], xr[i + 2], waits=[t_oa[i]])

        emit_front(0)
        emit_front(1)
        for i in range(1, NT):
            emit_tail(i - 1)
            if i + 1 < NT:
                emit_front(i + 1)
        emit_tail(NT - 1)

        if DEBUG2:
            dma("sync", qs, dbg["log2"][:, :], logits_all[:, :], waits=[t_py_all])
            dma("sync", qs, dbg["bu2"][:, :], bu_sb[:, :], waits=[t_py_all])
            dma("sync", qs, dbg["pyd2"][:, :], pyd[:, :], waits=[t_py_all])
            dma("sync", qs, dbg["xnd2"][:, :], xnd[:, :], waits=[t_py_all])
            dma("sync", qs, dbg["tidd2"][:, :], tidd[:, :], waits=[t_py_all])
            dma("sync", qs, dbg["carrbd2"][:, :], carrbd[:, :], waits=[t_py_all])
        final_qs = qs.tok()

        def f_final(e, tok=final_qs):
            e.wait_ge(tok[0], tok[1])
        emit("sync", f_final)
    c_ctx.close()

    with nc.Block() as block:
        @block.sync
        def _(e):
            for f in prog["sync"]:
                f(e)

        @block.scalar
        def _(e):
            for f in prog["scalar"]:
                f(e)

        @block.vector
        def _(e):
            for f in prog["vector"]:
                f(e)

        @block.tensor
        def _(e):
            for f in prog["tensor"]:
                f(e)

        @block.gpsimd
        def _(e):
            for f in prog["gpsimd"]:
                f(e)

    stack.close()
    return nc


def _consts():
    bf = ml_dtypes.bfloat16
    c = {}
    c["trikf"] = np.triu(np.ones((128, 128), np.float32), 1)
    c["trikb"] = c["trikf"].astype(bf)
    c["o128f"] = np.ones((128, 1), np.float32)
    c["o128b"] = c["o128f"].astype(bf)
    c["o1x128f"] = np.ones((1, 128), np.float32)
    c["o1x128b"] = c["o1x128f"].astype(bf)
    c["eqrow"] = np.ascontiguousarray(np.tile((np.arange(E) * Q).astype(np.float32), (128, 1)))
    c["e10row"] = np.ascontiguousarray(np.tile((np.arange(E) * RB).astype(np.float32), (128, 1)))
    c["tokid"] = np.ascontiguousarray(np.arange(NTOK, dtype=np.int32).reshape(NT, 128).T)
    return c


def _host_prep(L0_attn_x, L0_attn_B_w, L0_attn_C_w, L0_attn_gate_w, L0_attn_gate_b,
               L0_attn_log_decay, L0_ffn_Wg, L0_ffn_W1, L0_ffn_b1, L0_ffn_W2, L0_ffn_b2):
    x = np.ascontiguousarray(np.asarray(L0_attn_x, dtype=np.float32)).reshape(NCORE * NTOK, D)
    shared = {
        "w1": np.ascontiguousarray(np.asarray(L0_ffn_W1, dtype=np.float32)),
        "w2": np.ascontiguousarray(np.asarray(L0_ffn_W2, dtype=np.float32)),
        "wg": np.ascontiguousarray(np.asarray(L0_ffn_Wg, dtype=np.float32)),
        "bwt": np.ascontiguousarray(np.asarray(L0_attn_B_w, dtype=np.float32).T),
        "cwt": np.ascontiguousarray(np.asarray(L0_attn_C_w, dtype=np.float32).T),
        "gwt": np.ascontiguousarray(np.asarray(L0_attn_gate_w, dtype=np.float32).T),
        "gateb": np.asarray(L0_attn_gate_b, dtype=np.float32).reshape(1, D),
        "ldec": np.asarray(L0_attn_log_decay, dtype=np.float32).reshape(1, D),
        "b1t": np.ascontiguousarray(
            np.asarray(L0_ffn_b1, dtype=np.float32).reshape(E, HC, 128).transpose(2, 0, 1).reshape(128, E * HC)),
        "b2": np.ascontiguousarray(np.asarray(L0_ffn_b2, dtype=np.float32).reshape(1, E * D)),
    }
    shared.update(_consts())
    in_maps = []
    for c in range(NCORE):
        m = dict(shared)
        m["x"] = np.ascontiguousarray(x[c * NTOK:(c + 1) * NTOK])
        m["iotag"] = ((c % 2) * NTOK + np.arange(128, dtype=np.float32)).reshape(1, 128)
        m["rmask"] = (np.arange(8) < c).astype(np.float32).reshape(8, 1)
        m["cmask"] = (np.eye(8, dtype=np.float32)[c - 1].reshape(8, 1)
                      if c % 2 == 1 else np.zeros((8, 1), np.float32))
        in_maps.append(m)
    return in_maps


def kernel(**inputs):
    in_maps = _host_prep(**inputs)
    nc = build_nc()
    res = run_bass_kernel_spmd(nc, in_maps, core_ids=list(range(NCORE)))
    out = np.concatenate([np.asarray(r["out"]) for r in res.results], axis=0)
    return out.reshape(4, 8192, 1024)


# version 11
# speedup vs baseline: 1.1258x; 1.0041x over previous
import numpy as np
import ml_dtypes
import contextlib
import concourse.bass as bass
import concourse.mybir as mybir
from concourse.bass import IndirectOffsetOnAxis
from concourse.bass_utils import run_bass_kernel_spmd

F32 = mybir.dt.float32
BF16 = mybir.dt.bfloat16
I32 = mybir.dt.int32
AF = mybir.ActivationFunctionType
ALU = mybir.AluOpType
AXX = mybir.AxisListType.X

NCORE = 8
NTOK = 4096
NT = 32
D = 1024
DC = 8
S = 64
E = 8
H = 4096
HC = 32
Q = 1280
RB = 10
HQ = 640
CAP = 8602
EPS = 1e-5

ENGINES = ("sync", "scalar", "vector", "tensor", "gpsimd")

DEBUG = False
DEBUG2 = False
PAD_DRAM = 0


class TSem:
    __slots__ = ("h", "v", "name")

    def __init__(self, h, name):
        self.h = h
        self.v = 0
        self.name = name

    def bump(self, n):
        self.v += n
        return (self.h, self.v)

    def tok(self):
        return (self.h, self.v)


def build_nc():
    nc = bass.Bass("TRN2", target_bir_lowering=False)

    x_in = nc.dram_tensor("x", [NTOK, D], F32, kind="ExternalInput").ap()
    w1_in = nc.dram_tensor("w1", [E, D, H], F32, kind="ExternalInput").ap()
    w2_in = nc.dram_tensor("w2", [E, H, D], F32, kind="ExternalInput").ap()
    wg_in = nc.dram_tensor("wg", [D, E], F32, kind="ExternalInput").ap()
    bwt_in = nc.dram_tensor("bwt", [D, S], F32, kind="ExternalInput").ap()
    cwt_in = nc.dram_tensor("cwt", [S, D], F32, kind="ExternalInput").ap()
    gwt_in = nc.dram_tensor("gwt", [D, D], F32, kind="ExternalInput").ap()
    gateb_in = nc.dram_tensor("gateb", [1, D], F32, kind="ExternalInput").ap()
    ldec_in = nc.dram_tensor("ldec", [1, D], F32, kind="ExternalInput").ap()
    b1t_in = nc.dram_tensor("b1t", [128, E * HC], F32, kind="ExternalInput").ap()
    b2_in = nc.dram_tensor("b2", [1, E * D], F32, kind="ExternalInput").ap()
    trikf_in = nc.dram_tensor("trikf", [128, 128], F32, kind="ExternalInput").ap()
    trikb_in = nc.dram_tensor("trikb", [128, 128], BF16, kind="ExternalInput").ap()
    o128f_in = nc.dram_tensor("o128f", [128, 1], F32, kind="ExternalInput").ap()
    o128b_in = nc.dram_tensor("o128b", [128, 1], BF16, kind="ExternalInput").ap()
    o1x128f_in = nc.dram_tensor("o1x128f", [1, 128], F32, kind="ExternalInput").ap()
    o1x128b_in = nc.dram_tensor("o1x128b", [1, 128], BF16, kind="ExternalInput").ap()
    eqrow_in = nc.dram_tensor("eqrow", [128, E], F32, kind="ExternalInput").ap()
    e10row_in = nc.dram_tensor("e10row", [128, E], F32, kind="ExternalInput").ap()
    tokid_in = nc.dram_tensor("tokid", [128, NT], I32, kind="ExternalInput").ap()
    iotag_in = nc.dram_tensor("iotag", [1, 128], F32, kind="ExternalInput").ap()
    rmask_in = nc.dram_tensor("rmask", [8, 1], F32, kind="ExternalInput").ap()
    cmask_in = nc.dram_tensor("cmask", [8, 1], F32, kind="ExternalInput").ap()
    out_io = nc.dram_tensor("out", [NTOK, D], F32, kind="ExternalOutput").ap()

    dbg = {}
    if DEBUG2:
        dbg["pyd2"] = nc.dram_tensor("dbg_pyd2", [E * Q, D], BF16, kind="ExternalOutput").ap()
        dbg["xnd2"] = nc.dram_tensor("dbg_xnd2", [NTOK, D], BF16, kind="ExternalOutput").ap()
        dbg["tidd2"] = nc.dram_tensor("dbg_tidd2", [1, 128 * E * RB], I32, kind="ExternalOutput").ap()
        dbg["carrbd2"] = nc.dram_tensor("dbg_carrbd2", [NT, D], BF16, kind="ExternalOutput").ap()
        dbg["packT2"] = nc.dram_tensor("dbg_packT2", [E * 128, DC * Q], BF16, kind="ExternalOutput").ap()
        dbg["hT2"] = nc.dram_tensor("dbg_hT2", [E * 2 * 128, HC * HQ], BF16, kind="ExternalOutput").ap()
        dbg["log2"] = nc.dram_tensor("dbg_log2", [128, NT * E], F32, kind="ExternalOutput").ap()
        dbg["bu2"] = nc.dram_tensor("dbg_bu2", [64, NTOK], BF16, kind="ExternalOutput").ap()
    if DEBUG:
        dbg["logits"] = nc.dram_tensor("dbg_logits", [128, NT * E], F32, kind="ExternalOutput").ap()
        dbg["carry"] = nc.dram_tensor("dbg_carry", [1, D], F32, kind="ExternalOutput").ap()
        dbg["rank0"] = nc.dram_tensor("dbg_rank0", [128, NT], F32, kind="ExternalOutput").ap()
        dbg["rank1"] = nc.dram_tensor("dbg_rank1", [128, NT], F32, kind="ExternalOutput").ap()
        dbg["gate0"] = nc.dram_tensor("dbg_gate0", [128, NT], F32, kind="ExternalOutput").ap()
        dbg["gate1"] = nc.dram_tensor("dbg_gate1", [128, NT], F32, kind="ExternalOutput").ap()
        dbg["pdest0"] = nc.dram_tensor("dbg_pdest0", [128, NT], I32, kind="ExternalOutput").ap()
        dbg["pdest1"] = nc.dram_tensor("dbg_pdest1", [128, NT], I32, kind="ExternalOutput").ap()
        dbg["tidT"] = nc.dram_tensor("dbg_tidT", [128, E * RB], I32, kind="ExternalOutput").ap()
        dbg["gateT"] = nc.dram_tensor("dbg_gateT", [128, E * RB], F32, kind="ExternalOutput").ap()
        dbg["s"] = nc.dram_tensor("dbg_s", [NTOK, D], F32, kind="ExternalOutput").ap()
        dbg["g"] = nc.dram_tensor("dbg_g", [NTOK, D], F32, kind="ExternalOutput").ap()

    if PAD_DRAM:
        nc.dram_tensor("pad_dram", [1, PAD_DRAM], F32)
    xnd = nc.dram_tensor("xnd", [NTOK, D], BF16).ap()
    packd0 = nc.dram_tensor("packd0", [Q, D], BF16).ap()
    packd1 = nc.dram_tensor("packd1", [Q, D], BF16).ap()
    pyd = nc.dram_tensor("pyd", [E * Q, D], BF16).ap()
    tidd = nc.dram_tensor("tidd", [1, 128 * E * RB], I32).ap()
    tidd2d = tidd.rearrange("o (p c) -> (o p) c", p=128)
    carrd = nc.dram_tensor("carrd", [NT, D], F32).ap()
    carrbd = nc.dram_tensor("carrbd", [NT, D], BF16).ap()
    ccin = nc.dram_tensor("ccin", [1, 1040], F32).ap()
    ccout = nc.dram_tensor("ccout", [8, 1040], F32).ap()

    prog = {k: [] for k in ENGINES}

    def emit(eng, fn):
        prog[eng].append(fn)

    stack = contextlib.ExitStack()

    def sem(name):
        return TSem(stack.enter_context(nc.semaphore(name)), name)

    qs = sem("qs")
    qc = sem("qc")
    qt = sem("qt")
    qgi = sem("qgi")
    qw1 = [sem("qw1a"), sem("qw1b")]
    qw2 = sem("qw2")
    qb2 = [sem("qb2a"), sem("qb2b")]
    qgw = sem("qgw")
    qcc = sem("qcc")
    qg = sem("qg")
    vs = sem("vs")
    ss = sem("ss")
    ps = sem("ps")
    cc = sem("cc")
    ge = sem("ge")
    last_v = [None]
    last_s = [None]

    def dma(queue, qsem, out, in_, waits=(), **kw):
        tok = qsem.bump(16)
        wl = tuple(t for t in waits if t is not None)

        def f(e, out=out, in_=in_, wl=wl, kw=kw, h=qsem.h):
            for (sh, v) in wl:
                e.wait_ge(sh, v)
            e.dma_start(out=out, in_=in_, **kw).then_inc(h, 16)
        emit(queue, f)
        return tok

    def idma(out, out_off, in_, in_off, waits=()):
        tok = qgi.bump(16)
        wl = tuple(t for t in waits if t is not None)

        def f(e, out=out, oo=out_off, in_=in_, io=in_off, wl=wl, h=qgi.h):
            for (sh, v) in wl:
                e.wait_ge(sh, v)
            e.indirect_dma_start(out=out, out_offset=oo, in_=in_, in_offset=io).then_inc(h, 16)
        emit("gpsimd", f)
        return tok

    def op(eng, build, waits=()):
        esem = vs if eng == "vector" else ss
        last = last_v if eng == "vector" else last_s
        wl = [t for t in waits if t is not None]
        if last[0] is not None:
            wl.append(last[0])
        tok = esem.bump(1)
        wl = tuple(wl)

        def f(e, build=build, wl=wl, h=esem.h):
            for (sh, v) in wl:
                e.wait_ge(sh, v)
            build(e).then_inc(h, 1)
        emit(eng, f)
        last[0] = tok
        return tok

    def mmg(mms, waits=()):
        tok = ps.bump(1)
        wl = tuple(t for t in waits if t is not None)

        def f(e, mms=mms, wl=wl, h=ps.h):
            for (sh, v) in wl:
                e.wait_ge(sh, v)
            for (o, l, r, st, sp) in mms:
                ins = e.matmul(o, l, r, start=st, stop=sp)
            ins.then_inc(h, 1)
        emit("tensor", f)
        return tok

    logits_all = stack.enter_context(nc.sbuf_tensor("logits_all", [128, NT * E], F32))
    bu_sb = stack.enter_context(nc.sbuf_tensor("bu_sb", [64, NTOK], BF16))
    cwt_sb = stack.enter_context(nc.sbuf_tensor("cwt_sb", [64, D], BF16))
    ldlog_sb = stack.enter_context(nc.sbuf_tensor("ldlog_sb", [1, D], F32))
    trikf_sb = stack.enter_context(nc.sbuf_tensor("trikf_sb", [128, 128], F32))
    trikb_sb = stack.enter_context(nc.sbuf_tensor("trikb_sb", [128, 128], BF16))
    o128f_sb = stack.enter_context(nc.sbuf_tensor("o128f_sb", [128, 1], F32))
    o128b_sb = stack.enter_context(nc.sbuf_tensor("o128b_sb", [128, 1], BF16))
    o1x128f_sb = stack.enter_context(nc.sbuf_tensor("o1x128f_sb", [1, 128], F32))
    o1x128b_sb = stack.enter_context(nc.sbuf_tensor("o1x128b_sb", [1, 128], BF16))
    iotag_sb = stack.enter_context(nc.sbuf_tensor("iotag_sb", [1, 128], F32))
    pdest0_sb = stack.enter_context(nc.sbuf_tensor("pdest0_sb", [128, NT], I32))
    pdest1_sb = stack.enter_context(nc.sbuf_tensor("pdest1_sb", [128, NT], I32))
    crow_sb = stack.enter_context(nc.sbuf_tensor("crow_sb", [1, E], F32))
    tidt_sb = stack.enter_context(nc.sbuf_tensor("tidt_sb", [128, E * RB], I32))
    stat_sb = stack.enter_context(nc.sbuf_tensor("stat_sb", [128, 16], F32))
    scrv = stack.enter_context(nc.sbuf_tensor("scrv", [128, 1], F32))
    scrs = stack.enter_context(nc.sbuf_tensor("scrs", [128, 1], F32))
    oh1_all = stack.enter_context(nc.sbuf_tensor("oh1_all", [128, NT * E], F32))
    oh2_all = stack.enter_context(nc.sbuf_tensor("oh2_all", [128, NT * E], F32))
    rank0_all = stack.enter_context(nc.sbuf_tensor("rank0_all", [128, NT], F32))
    rank1_all = stack.enter_context(nc.sbuf_tensor("rank1_all", [128, NT], F32))
    g1_all = stack.enter_context(nc.sbuf_tensor("g1_all", [128, NT], F32))
    g2_all = stack.enter_context(nc.sbuf_tensor("g2_all", [128, NT], F32))
    gate0_all = stack.enter_context(nc.sbuf_tensor("gate0_all", [128, NT], F32))
    gate1_all = stack.enter_context(nc.sbuf_tensor("gate1_all", [128, NT], F32))
    d20_all = stack.enter_context(nc.sbuf_tensor("d20_all", [128, NT], I32))
    d21_all = stack.enter_context(nc.sbuf_tensor("d21_all", [128, NT], I32))
    tmp256 = stack.enter_context(nc.sbuf_tensor("tmp256", [128, NT * E], F32))
    t32a = stack.enter_context(nc.sbuf_tensor("t32a", [128, NT], F32))
    t32b = stack.enter_context(nc.sbuf_tensor("t32b", [128, NT], F32))
    t32c = stack.enter_context(nc.sbuf_tensor("t32c", [128, NT], F32))
    i32a = stack.enter_context(nc.sbuf_tensor("i32a", [128, NT], I32))
    i32b = stack.enter_context(nc.sbuf_tensor("i32b", [128, NT], I32))
    i32c = stack.enter_context(nc.sbuf_tensor("i32c", [128, NT], I32))
    rmask_sb = stack.enter_context(nc.sbuf_tensor("rmask_sb", [8, 1], F32))
    cmask_sb = stack.enter_context(nc.sbuf_tensor("cmask_sb", [8, 1], F32))
    offsrow_sb = stack.enter_context(nc.sbuf_tensor("offsrow_sb", [1, E], F32))
    offsb_sb = stack.enter_context(nc.sbuf_tensor("offsb_sb", [128, E], F32))
    pack_sb = stack.enter_context(nc.sbuf_tensor("pack_sb", [1, 1040], F32))
    gath_sb = stack.enter_context(nc.sbuf_tensor("gath_sb", [8, 1040], F32))
    carry_local = stack.enter_context(nc.sbuf_tensor("carry_local", [1, D], F32))
    carry0_sb = stack.enter_context(nc.sbuf_tensor("carry0_sb", [1, D], F32))
    carr32_sb = stack.enter_context(nc.sbuf_tensor("carr32_sb", [32, D], F32))
    carrb32_sb = stack.enter_context(nc.sbuf_tensor("carrb32_sb", [32, D], BF16))

    PSA = stack.enter_context(nc.psum_tensor("PSA", [128, 1024], F32))
    PSB = stack.enter_context(nc.psum_tensor("PSB", [128, 1024], F32))
    PSC = stack.enter_context(nc.psum_tensor("PSC", [128, 1024], F32))
    PSD = stack.enter_context(nc.psum_tensor("PSD", [128, 1024], F32))

    t_cwt = dma("gpsimd", qg, cwt_sb[:, :], cwt_in[:, :])
    t_ldec_sb = stack.enter_context(nc.sbuf_tensor("ldec_sb", [1, D], F32))
    ldec_sb = t_ldec_sb
    t_ldec = dma("sync", qs, ldec_sb[:, :], ldec_in[:, :])
    t_trikf = dma("sync", qs, trikf_sb[:, :], trikf_in[:, :])
    t_trikb = dma("sync", qs, trikb_sb[:, :], trikb_in[:, :])
    t_o128f = dma("sync", qs, o128f_sb[:, :], o128f_in[:, :])
    t_o128b = dma("sync", qs, o128b_sb[:, :], o128b_in[:, :])
    t_o1f = dma("sync", qs, o1x128f_sb[:, :], o1x128f_in[:, :])
    t_o1b = dma("sync", qs, o1x128b_sb[:, :], o1x128b_in[:, :])
    t_iog = dma("sync", qs, iotag_sb[:, :], iotag_in[:, :])

    t_sg0 = op("scalar", lambda e: e.activation(out=ldlog_sb[:, :], in_=ldec_sb[:, :], func=AF.Sigmoid),
               waits=[t_ldec])
    t_ldlog = op("scalar", lambda e: e.activation(out=ldlog_sb[:, :], in_=ldlog_sb[:, :], func=AF.Ln),
                 waits=[t_sg0])
    t_z3 = op("vector", lambda e: e.memset(carry_local[:, :], 0.0))

    a_ctx = contextlib.ExitStack()
    xnT = a_ctx.enter_context(nc.sbuf_tensor("xnT", [128, DC * NTOK], BF16))

    def xnT_tile(dc, i):
        return xnT[:, dc * NTOK + i * 128: dc * NTOK + (i + 1) * 128]

    if True:
        _a = a_ctx.enter_context
        bwt_sb = _a(nc.sbuf_tensor("bwt_sb", [128, DC * S], BF16))
        wg_sb = _a(nc.sbuf_tensor("wg_sb", [128, DC * E], BF16))
        xt0 = _a(nc.sbuf_tensor("xt0", [128, D], F32))
        xt1 = _a(nc.sbuf_tensor("xt1", [128, D], F32))
        xnb0 = _a(nc.sbuf_tensor("xnb0", [128, D], BF16))
        xnb1 = _a(nc.sbuf_tensor("xnb1", [128, D], BF16))
        pw_a = _a(nc.sbuf_tensor("pw_a", [128, D], F32))
        ld128_sb = _a(nc.sbuf_tensor("ld128_sb", [1, D], F32))
        d128_sb = _a(nc.sbuf_tensor("d128_sb", [128, D], F32))
        c_a = _a(nc.sbuf_tensor("c_a", [128, D], BF16))
        d12_all = _a(nc.sbuf_tensor("d12_all", [128, NT], F32))
        cnt_all = _a(nc.sbuf_tensor("cnt_all", [128, NT * E], F32))
        srt_sb = _a(nc.sbuf_tensor("srt_sb", [128, E], F32))
        s01_sb = _a(nc.sbuf_tensor("s01_sb", [128, E], F32))
        eqrow_sb = _a(nc.sbuf_tensor("eqrow_sb", [128, E], F32))
        e10row_sb = _a(nc.sbuf_tensor("e10row_sb", [128, E], F32))
        tokid_sb = _a(nc.sbuf_tensor("tokid_sb", [128, NT], I32))
        colrow0 = _a(nc.sbuf_tensor("colrow0", [1, D], F32))
        colrow1 = _a(nc.sbuf_tensor("colrow1", [1, D], F32))
        zeroi_sb = _a(nc.sbuf_tensor("zeroi_sb", [128, E * RB], I32))

        t_bwt = dma("gpsimd", qg, bwt_sb[:, :].rearrange("p (dc s) -> p dc s", dc=DC),
                     bwt_in.rearrange("(dc p) s -> p dc s", p=128))
        t_wg = dma("gpsimd", qg, wg_sb[:, :].rearrange("p (dc e) -> p dc e", dc=DC),
                    wg_in.rearrange("(dc p) e -> p dc e", p=128))
        t_cwt = t_bwt = t_wg = qg.tok()
        t_eq = dma("sync", qs, eqrow_sb[:, :], eqrow_in[:, :])
        t_e10 = dma("sync", qs, e10row_sb[:, :], e10row_in[:, :])
        t_tok = dma("sync", qs, tokid_sb[:, :], tokid_in[:, :])
        t_rm = dma("sync", qs, rmask_sb[:, :], rmask_in[:, :])
        t_cm = dma("sync", qs, cmask_sb[:, :], cmask_in[:, :])
        t_z1 = op("vector", lambda e: e.memset(zeroi_sb[:, :], 0))
        t_tid0 = dma("sync", qs, tidd2d, zeroi_sb[:, :], waits=[t_z1])

        xts = [xt0, xt1]
        xnbs = [xnb0, xnb1]
        xr = x_in.rearrange("(i p) d -> i p d", p=128)
        xndr = xnd.rearrange("(i p) d -> i p d", p=128)

        t_xload = [None] * NT
        t_xm = [None] * NT
        t_xnb = [None] * NT
        t_xnw = [None] * NT

        t_xload[0] = dma("sync", qs, xts[0][:, :], xr[0])
        if NT > 1:
            t_xload[1] = dma("sync", qs, xts[1][:, :], xr[1])
        for i in range(NT):
            b = i % 2
            t1 = op("vector", lambda e, b=b: e.bn_stats(stat_sb[:, 0:6], xts[b][:, 0:512]),
                    waits=[t_xload[i]])
            t2 = op("vector", lambda e, b=b: e.bn_stats(stat_sb[:, 6:12], xts[b][:, 512:1024]))
            t3 = op("vector", lambda e: e.bn_aggr(stat_sb[:, 12:14], stat_sb[:, 0:12]))
            t3b = op("vector", lambda e: e.tensor_scalar(stat_sb[:, 13:14], stat_sb[:, 13:14], float(EPS), None, ALU.add))
            t3c = op("vector", lambda e: e.tensor_copy(out=scrv[:, :], in_=stat_sb[:, 13:14]))
            t6 = op("scalar", lambda e: e.activation(out=stat_sb[:, 14:15], in_=stat_sb[:, 13:14], func=AF.Sqrt),
                    waits=[t3c])
            t6b = op("scalar", lambda e: e.activation(out=scrs[:, :], in_=stat_sb[:, 14:15], func=AF.Copy))
            t7 = op("vector", lambda e: e.reciprocal(stat_sb[:, 14:15], stat_sb[:, 14:15]),
                    waits=[t6b])
            t_xnb[i] = op("vector", lambda e, b=b: e.tensor_scalar(xnbs[b][:, :], xts[b][:, :],
                                                                   stat_sb[:, 12:13], stat_sb[:, 14:15],
                                                                   ALU.subtract, ALU.mult),
                          waits=[t_xnw[i - 2]] if i >= 2 else [])
            t_xm[i] = t_xnb[i]
            t_xnw[i] = dma("scalar", qc, xndr[i], xnbs[b][:, :], waits=[t_xnb[i]])
            if i + 2 < NT:
                t_xload[i + 2] = dma("sync", qs, xts[b][:, :], xr[i + 2], waits=[t_xnb[i]])

        t_xnall = qc.tok()
        t_T = []
        for dc in range(DC):
            t_T.append(dma("sync", qt, xnT[:, dc * NTOK:(dc + 1) * NTOK],
                           xnd[:, dc * 128:(dc + 1) * 128], waits=[t_xnall], transpose=True))

        t_logev = [None] * NT
        t_buev = [None] * (NT // 4)
        for g in range(NT // 4):
            for i in range(4 * g, 4 * g + 4):
                pslot = PSC[:, 0:8] if i % 2 == 0 else PSC[:, 512:520]
                w = (list(t_T) + [t_wg]) if i == 0 else []
                if i >= 2:
                    w = w + [t_logev[i - 2]]
                tpe = mmg([(pslot, xnT_tile(dc, i), wg_sb[:, dc * E:(dc + 1) * E],
                            dc == 0, dc == DC - 1) for dc in range(DC)], waits=w)
                t_logev[i] = op("vector", lambda e, i=i, pslot=pslot:
                                e.tensor_copy(out=logits_all[:, i * E:(i + 1) * E], in_=pslot),
                                waits=[tpe])
            pslot = PSD[0:64, 0:512] if g % 2 == 0 else PSD[0:64, 512:1024]
            w = (list(t_T) + [t_bwt]) if g == 0 else []
            if g >= 2:
                w = w + [t_buev[g - 2]]
            tpe = mmg([(pslot, bwt_sb[:, dc * S:(dc + 1) * S],
                        xnT[:, dc * NTOK + g * 512: dc * NTOK + (g + 1) * 512],
                        dc == 0, dc == DC - 1) for dc in range(DC)], waits=w)
            t_buev[g] = op("vector", lambda e, g=g, pslot=pslot:
                           e.tensor_copy(out=bu_sb[:, g * 512:(g + 1) * 512], in_=pslot),
                           waits=[tpe])
            t_buev[g] = op("vector", lambda e: e.tensor_copy(out=scrv[0:64, :], in_=bu_sb[0:64, g * 512:g * 512 + 1]))

        t_l128 = op("vector", lambda e: e.tensor_scalar(ld128_sb[:, :], ldlog_sb[:, :], 128.0, None, ALU.mult),
                    waits=[t_ldlog])
        t_d128mm = mmg([(PSB[:, 0:512], o1x128f_sb[:, :], ld128_sb[:, 0:512], True, True),
                        (PSB[:, 512:1024], o1x128f_sb[:, :], ld128_sb[:, 512:1024], True, True)],
                       waits=[t_l128, t_o1f])
        t_d128a = op("scalar", lambda e: e.activation(out=d128_sb[:, :], in_=PSB[:, :], func=AF.Exp),
                    waits=[t_d128mm])
        t_d128 = op("scalar", lambda e: e.activation(out=scrs[:, :], in_=d128_sb[:, 0:1], func=AF.Copy))
        t_pb = [None] * NT
        t_c = [None] * NT
        t_cs = [None] * NT
        t_cadd = [None] * NT
        t_pwv = [None] * NT
        for i in range(NT):
            pbt = PSA if i % 2 == 0 else PSC
            pwt = PSB if i % 2 == 0 else PSD
            w = [t_buev[i // 4], t_cwt]
            if i >= 2:
                w.append(t_c[i - 2])
            if i in (0, 1):
                w.append(t_logev[NT - 1])
            t_pb[i] = mmg(
                [(pbt[:, 0:512], bu_sb[0:64, i * 128:(i + 1) * 128], cwt_sb[:, 0:512], True, True),
                 (pbt[:, 512:1024], bu_sb[0:64, i * 128:(i + 1) * 128], cwt_sb[:, 512:1024], True, True)],
                waits=w)
            if i == 0:
                t_pw = mmg(
                    [(pwt[:, 0:512], iotag_sb[:, 0:128], ldlog_sb[:, 0:512], True, True),
                     (pwt[:, 512:1024], iotag_sb[:, 0:128], ldlog_sb[:, 512:1024], True, True)],
                    waits=[t_iog, t_ldlog, t_d128])
                t_pwv0 = op("scalar", lambda e, p=pwt: e.activation(out=pw_a[:, :], in_=p[:, :], func=AF.Exp),
                              waits=[t_pw])
                t_pwv[i] = op("scalar", lambda e: e.activation(out=scrs[:, :], in_=pw_a[:, 0:1], func=AF.Copy))
            else:
                t_pwv[i] = op("vector", lambda e: e.tensor_mul(pw_a[:, :], pw_a[:, :], d128_sb[:, :]),
                              waits=[t_c[i - 1], t_d128])
            t_c[i] = op("vector", lambda e, p=pbt: e.tensor_mul(c_a[:, :], p[:, :], pw_a[:, :]),
                        waits=[t_pwv[i], t_pb[i]] + ([t_cs[i - 1]] if i >= 1 else []))
            t_cb = op("vector", lambda e: e.tensor_copy(out=scrv[:, :], in_=c_a[:, 0:1]))
            w = [t_cb, t_o128b]
            if i == 0:
                w.append(t_pwv[0])
            if i == 1:
                w.append(t_buev[NT // 4 - 1])
            if i >= 2:
                w.append(t_cadd[i - 2])
            t_cs[i] = mmg(
                [(pwt[0:1, 0:512], o128b_sb[:, :], c_a[:, 0:512], True, True),
                 (pwt[0:1, 512:1024], o128b_sb[:, :], c_a[:, 512:1024], True, True)],
                waits=w)
            cr = [colrow0, colrow1][i % 2]
            tcp = op("scalar", lambda e, p=pwt, cr=cr: e.copy(out=cr[:, :], in_=p[0:1, 0:1024]),
                     waits=[t_cs[i]] + ([t_cadd[i - 2]] if i >= 2 else []))
            t_cadd[i] = dma("scalar", qc, carrd[i:i + 1, :], cr[:, :], waits=[tcp])
        t_c32 = dma("sync", qs, carr32_sb[:, :], carrd[:, :], waits=[t_cadd[NT - 1], t_cadd[NT - 2]])
        t_cl = mmg([(PSB[0:1, 0:512], o128f_sb[0:32, 0:1], carr32_sb[:, 0:512], True, True),
                    (PSB[0:1, 512:1024], o128f_sb[0:32, 0:1], carr32_sb[:, 512:1024], True, True)],
                   waits=[t_c32, t_o128f, t_cs[NT - 1], t_cs[NT - 2]])
        t_carry_done = op("vector", lambda e: e.tensor_copy(out=carry_local[:, :], in_=PSB[0:1, 0:1024]),
                          waits=[t_cl, t_z3])

        t_crow = [None] * NT
        t_crowev = [None] * NT
        t_cntev = [None] * NT
        t_oh1 = [None] * NT
        t_oh2 = [None] * NT
        t_d12 = [None] * NT
        for i in range(NT):
            lg = logits_all[:, i * E:(i + 1) * E]
            tsrt = op("vector", lambda e, lg=lg: e.max(srt_sb[:, :], lg), waits=[t_logev[i]])
            t_oh1[i] = op("vector", lambda e, lg=lg, i=i:
                          e.tensor_scalar(oh1_all[:, i * E:(i + 1) * E], lg, srt_sb[:, 0:1], None, ALU.is_equal))
            t_oh2[i] = op("vector", lambda e, lg=lg, i=i:
                          e.tensor_scalar(oh2_all[:, i * E:(i + 1) * E], lg, srt_sb[:, 1:2], None, ALU.is_equal))
            t_d12[i] = op("vector", lambda e, i=i:
                          e.tensor_scalar(d12_all[:, i:i + 1], srt_sb[:, 0:1], srt_sb[:, 1:2], None, ALU.subtract))
            ts01 = op("vector", lambda e, i=i:
                      e.tensor_add(s01_sb[:, :], oh1_all[:, i * E:(i + 1) * E], oh2_all[:, i * E:(i + 1) * E]))
            ts01b = op("vector", lambda e: e.tensor_copy(out=scrv[:, :], in_=s01_sb[:, 0:1]))
            w = [ts01b, t_trikf, t_o1f, t_cadd[NT - 1]]
            if i >= 1:
                w += [t_crowev[i - 1], t_cntev[i - 1]]
            if i == 0:
                w.append(t_cadd[NT - 1])
            mlist = [(PSA[:, 0:8], trikf_sb[:, :], s01_sb[:, :], True, i == 0)]
            if i >= 1:
                mlist.append((PSA[:, 0:8], o1x128f_sb[:, :], crow_sb[:, :], False, True))
            t_cnt = mmg(mlist, waits=w)
            t_cntev[i] = op("vector", lambda e, i=i: e.tensor_copy(out=cnt_all[:, i * E:(i + 1) * E], in_=PSA[:, 0:8]),
                            waits=[t_cnt])
            t_crow[i] = mmg([(PSB[0:1, 0:8], o128f_sb[:, :], s01_sb[:, :], i == 0, True)],
                            waits=[ts01b, t_o128f] + ([t_cadd[NT - 1]] if i == 0 else [t_crowev[i - 1]]))
            t_crowev[i] = op("scalar", lambda e: e.copy(out=crow_sb[:, :], in_=PSB[0:1, 0:8]),
                             waits=[t_crow[i]])

        oh1_3 = oh1_all[:, :].rearrange("p (t e) -> p t e", e=E)
        oh2_3 = oh2_all[:, :].rearrange("p (t e) -> p t e", e=E)
        cnt_3 = cnt_all[:, :].rearrange("p (t e) -> p t e", e=E)
        tmp_3 = tmp256[:, :].rearrange("p (t e) -> p t e", e=E)
        op("vector", lambda e: e.tensor_mul(tmp_3, oh1_3, cnt_3), waits=[t_cntev[NT - 1]])
        t_r0 = op("vector", lambda e: e.reduce_sum(rank0_all[:, :].unsqueeze(2), tmp_3, axis=AXX))
        op("vector", lambda e: e.tensor_mul(tmp_3, oh2_3, cnt_3))
        t_r1 = op("vector", lambda e: e.reduce_sum(rank1_all[:, :].unsqueeze(2), tmp_3, axis=AXX))

        t_d12b = op("vector", lambda e: e.tensor_copy(out=scrv[:, :], in_=d12_all[:, NT - 1:NT]),
                    waits=[t_d12[NT - 1]])
        t_g1a = op("scalar", lambda e: e.activation(out=g1_all[:, :], in_=d12_all[:, :], func=AF.Sigmoid),
                   waits=[t_d12b])
        t_g1 = op("scalar", lambda e: e.activation(out=scrs[:, :], in_=g1_all[:, 0:1], func=AF.Copy))
        t_g2 = op("vector", lambda e: e.tensor_scalar(g2_all[:, :], g1_all[:, :], -1.0, 1.0, ALU.mult, ALU.add),
                  waits=[t_g1])

        eqrow_b = eqrow_sb[:, :].unsqueeze(1).broadcast_to([128, NT, E])
        t_dest = {}
        for (slot, oh_3x, rank_all_x, dall, pdall) in (
            (0, oh1_3, rank0_all, d20_all, pdest0_sb),
            (1, oh2_3, rank1_all, d21_all, pdest1_sb),
        ):
            rk = rank_all_x[:, :]
            op("vector", lambda e, oh_3x=oh_3x: e.tensor_mul(tmp_3, oh_3x, eqrow_b),
               waits=[t_eq, t_r0, t_r1])
            op("vector", lambda e: e.reduce_sum(t32a[:, :].unsqueeze(2), tmp_3, axis=AXX))
            op("vector", lambda e, rk=rk: e.tensor_add(t32b[:, :], t32a[:, :], rk))
            t_pd = op("vector", lambda e, pdall=pdall: e.tensor_copy(out=pdall[:, :], in_=t32b[:, :]))
            op("vector", lambda e, pdall=pdall: e.tensor_scalar(i32a[:, :], pdall[:, :], 7, None, ALU.arith_shift_right))
            op("vector", lambda e, rk=rk: e.tensor_copy(out=i32b[:, :], in_=rk))
            op("vector", lambda e: e.tensor_scalar(i32b[:, :], i32b[:, :], 127, None, ALU.bitwise_and))
            op("vector", lambda e: e.tensor_scalar(i32c[:, :], i32b[:, :], 6, None, ALU.logical_shift_left))
            op("vector", lambda e: e.tensor_scalar(i32b[:, :], i32b[:, :], 4, None, ALU.logical_shift_left))
            op("vector", lambda e: e.tensor_add(i32c[:, :], i32c[:, :], i32b[:, :]))
            t_d2 = op("vector", lambda e, dall=dall: e.tensor_add(dall[:, :], i32c[:, :], i32a[:, :]))
            t_dest[slot] = (t_d2, t_pd)
        t_last_meta = t_dest[1][0]

        t_tscat = []
        for i in range(NT):
            for (slot, dall) in ((0, d20_all), (1, d21_all)):
                t_tscat.append(idma(tidd, IndirectOffsetOnAxis(dall[:, i:i + 1], 1),
                                    tokid_sb[:, i:i + 1], None,
                                    waits=[t_dest[slot][0], t_tid0, t_tok]))
        t_tidld = dma("sync", qs, tidt_sb[:, :], tidd2d, waits=[t_tscat[-1]])

        tp1 = op("vector", lambda e: e.tensor_copy(out=pack_sb[:, 0:1024], in_=carry_local[:, :]),
                 waits=[t_carry_done])
        t_crowb = op("scalar", lambda e: e.activation(out=scrs[0:1, :], in_=crow_sb[:, 0:1], func=AF.Copy),
                     waits=[t_crowev[NT - 1]])
        tp2 = op("vector", lambda e: e.tensor_copy(out=pack_sb[:, 1024:1032], in_=crow_sb[:, :]),
                 waits=[t_crowb])
        t_ccw = dma("gpsimd", qg, ccin[:, :], pack_sb[:, :], waits=[tp2])
        cc.v += 1
        t_cc = cc.tok()

        def cc_fn(e, w=t_ccw, h=cc.h):
            e.wait_ge(w[0], w[1])
            e.collective_compute(
                "AllGather", ALU.bypass,
                replica_groups=[list(range(NCORE))],
                ins=[ccin.opt()], outs=[ccout.opt()],
            ).then_inc(h, 1)
        emit("gpsimd", cc_fn)
        t_ccr = dma("gpsimd", qg, gath_sb[:, :], ccout[:, :], waits=[t_cc])

        t_offs = mmg([(PSA[0:1, 0:8], rmask_sb[:, :], gath_sb[0:8, 1024:1032], True, True)],
                     waits=[t_ccr, t_rm, t_cntev[NT - 1]])
        t_offsev = op("vector", lambda e: e.tensor_copy(out=offsrow_sb[:, :], in_=PSA[0:1, 0:8]),
                      waits=[t_offs])
        t_offsevb = op("vector", lambda e: e.tensor_copy(out=scrv[0:1, :], in_=offsrow_sb[:, 0:1]))
        t_offsb = mmg([(PSA[:, 0:8], o1x128f_sb[:, :], offsrow_sb[:, :], True, True)],
                      waits=[t_offsevb])
        t_offsbev = op("vector", lambda e: e.tensor_copy(out=offsb_sb[:, :], in_=PSA[:, 0:8]),
                       waits=[t_offsb])
        t_c0 = mmg([(PSB[0:1, 0:512], cmask_sb[:, :], gath_sb[0:8, 0:512], True, True),
                    (PSB[0:1, 512:1024], cmask_sb[:, :], gath_sb[0:8, 512:1024], True, True)],
                   waits=[t_ccr, t_cm, t_crowev[NT - 1]])
        t_c0ev = op("vector", lambda e: e.tensor_copy(out=carry0_sb[:, :], in_=PSB[0:1, 0:1024]),
                    waits=[t_c0])
        t_c0evb = op("vector", lambda e: e.tensor_copy(out=scrv[0:1, :], in_=carry0_sb[:, 0:1]))
        t_carr = mmg([(PSC[0:32, 0:512], trikf_sb[0:32, 0:32], carr32_sb[:, 0:512], True, False),
                      (PSC[0:32, 0:512], o1x128f_sb[:, 0:32], carry0_sb[:, 0:512], False, True),
                      (PSC[0:32, 512:1024], trikf_sb[0:32, 0:32], carr32_sb[:, 512:1024], True, False),
                      (PSC[0:32, 512:1024], o1x128f_sb[:, 0:32], carry0_sb[:, 512:1024], False, True)],
                     waits=[t_c0evb, t_c[NT - 1]])
        t_carrev = op("vector", lambda e: e.tensor_copy(out=carrb32_sb[:, :], in_=PSC[0:32, 0:1024]),
                      waits=[t_carr])
        t_carrw = dma("scalar", qc, carrbd[:, :], carrb32_sb[:, :], waits=[t_carrev])

        offsb_b = offsb_sb[:, :].unsqueeze(1).broadcast_to([128, NT, E])
        t_gate = {}
        for (slot, oh_3x, rank_all_x, g_all_x, gate_all_x) in (
            (0, oh1_3, rank0_all, g1_all, gate0_all),
            (1, oh2_3, rank1_all, g2_all, gate1_all),
        ):
            rk = rank_all_x[:, :]
            op("vector", lambda e, oh_3x=oh_3x: e.tensor_mul(tmp_3, oh_3x, offsb_b),
               waits=[t_offsbev, t_g2])
            op("vector", lambda e: e.reduce_sum(t32a[:, :].unsqueeze(2), tmp_3, axis=AXX))
            op("vector", lambda e, rk=rk: e.tensor_add(t32b[:, :], t32a[:, :], rk))
            op("vector", lambda e: e.tensor_scalar(t32c[:, :], t32b[:, :], CAP - 0.5, None, ALU.is_lt))
            t_gate[slot] = op("vector", lambda e, g_all_x=g_all_x, gate_all_x=gate_all_x:
                              e.tensor_mul(gate_all_x[:, :], g_all_x[:, :], t32c[:, :]))

        if DEBUG:
            dma("sync", qs, dbg["logits"][:, :], logits_all[:, :], waits=[t_logev[NT - 1]])
            dma("sync", qs, dbg["carry"][:, :], carry_local[:, :], waits=[t_carry_done])
            dma("sync", qs, dbg["rank0"][:, :], rank0_all[:, :], waits=[t_r0])
            dma("sync", qs, dbg["rank1"][:, :], rank1_all[:, :], waits=[t_r1])
            dma("sync", qs, dbg["gate0"][:, :], gate0_all[:, :], waits=[t_gate[0]])
            dma("sync", qs, dbg["gate1"][:, :], gate1_all[:, :], waits=[t_gate[1]])
            dma("sync", qs, dbg["pdest0"][:, :], pdest0_sb[:, :], waits=[t_last_meta])
            dma("sync", qs, dbg["pdest1"][:, :], pdest1_sb[:, :], waits=[t_last_meta])
            dma("sync", qs, dbg["tidT"][:, :], tidt_sb[:, :], waits=[t_tidld])

        t_b0_psA_free = t_offsbev
        t_b0_psB_free = t_c0ev
        t_logits_psC_free = t_c[NT - 1]
        t_bu_psD_free = t_cadd[NT - 1]

    a_ctx.close()

    b_ctx = contextlib.ExitStack()
    w1p0 = b_ctx.enter_context(nc.sbuf_tensor("w1p0", [128, DC * 512], BF16))
    w1p1 = b_ctx.enter_context(nc.sbuf_tensor("w1p1", [128, DC * 512], BF16))
    w2_sb = b_ctx.enter_context(nc.sbuf_tensor("w2_sb", [128, HC * D], BF16))
    hT = b_ctx.enter_context(nc.sbuf_tensor("hT", [128, HC * HQ], BF16))
    packT = b_ctx.enter_context(nc.sbuf_tensor("packT", [128, DC * Q], BF16))
    rows0 = b_ctx.enter_context(nc.sbuf_tensor("rows0", [128, D], BF16))
    rows1 = b_ctx.enter_context(nc.sbuf_tensor("rows1", [128, D], BF16))
    rows2 = b_ctx.enter_context(nc.sbuf_tensor("rows2", [128, D], BF16))
    rows3 = b_ctx.enter_context(nc.sbuf_tensor("rows3", [128, D], BF16))
    ysb0 = b_ctx.enter_context(nc.sbuf_tensor("ysb0", [128, D], BF16))
    ysb1 = b_ctx.enter_context(nc.sbuf_tensor("ysb1", [128, D], BF16))
    b1t_sb = b_ctx.enter_context(nc.sbuf_tensor("b1t_sb", [128, E * HC], F32))
    b2_sb = b_ctx.enter_context(nc.sbuf_tensor("b2_sb", [1, 2 * D], BF16))
    w1ps = [w1p0, w1p1]
    rowss = [rows0, rows1, rows2, rows3]
    ysbs = [ysb0, ysb1]

    t_b1t = dma("sync", qs, b1t_sb[:, :], b1t_in[:, :], waits=[t_tidld])
    t_b2_e = [None] * E

    RCH = [(0, 512), (512, 128)]

    pydr = pyd.rearrange("(r p) d -> r p d", p=128)
    packdrs = [packd0.rearrange("(r p) d -> r p d", p=128),
               packd1.rearrange("(r p) d -> r p d", p=128)]
    packds = [packd0, packd1]

    t_packw = {}
    t_gelu_seq = []
    t_yev_all = []
    t_yw_all = []
    t_mm1_last = None
    t_mm2_last = None
    t_mm2_by_e = {}
    t_transp_done = None
    t_transp_prev = None
    t_hdump = None
    w1k = 0

    for e in range(E):
        for rb in range(RB):
            k = e * RB + rb
            war = [t_packw[k - 4]] if k >= 4 else []
            tg_ = idma(rowss[k % 4][:, :], None, xnd,
                       IndirectOffsetOnAxis(tidt_sb[:, e * RB + rb:e * RB + rb + 1], 0),
                       waits=[t_tidld] + war)
            w = [tg_]
            if e >= 2:
                w.append(t_transp_prev)
            t_packw[k] = dma("sync", qs, packdrs[e % 2][rb], rowss[k % 4][:, :], waits=w)
        tts = []
        for dc in range(DC):
            w = [t_packw[e * RB + RB - 1]]
            if e >= 1:
                w.append(t_mm1_last)
            tts.append(dma("sync", qt, packT[:, dc * Q:(dc + 1) * Q],
                           packds[e % 2][:, dc * 128:(dc + 1) * 128], waits=w, transpose=True))
        t_transp_prev = t_transp_done
        t_transp_done = tts[-1]
        if DEBUG2:
            dma("sync", qs, dbg["packT2"].rearrange("(e p) c -> e p c", p=128)[e],
                packT[:, :], waits=[t_transp_done])

        t_w2_done = None
        w2_pieces = []
        for kpc in range(8):
            w = [t_mm2_last] if (e >= 1 and kpc == 0) else []
            if e == 0 and kpc == 0:
                w = [t_tidld]
            w2_pieces.append((kpc, w))

        for half in range(2):
            t_w1p = [None] * 8
            pbase = w1k

            def issue_w1_piece(pp, e=e, pbase=pbase, t_w1p=t_w1p):
                w = []
                k = pbase + pp
                if k >= 2:
                    w.append(t_gelu_seq[(k - 1) * 4 - 1])
                t_w1p[pp] = dma("gpsimd", qw1[k % 2],
                                w1ps[k % 2][:, :].rearrange("p (dc hh) -> p dc hh", dc=DC),
                                w1_in[e][:, pp * 512:(pp + 1) * 512].rearrange("(dc p) hh -> p dc hh", p=128),
                                waits=w)

            def issue_w2_pieces(n, e=e):
                nonlocal_t = None
                for _ in range(n):
                    if w2_pieces:
                        kpc, wp = w2_pieces.pop(0)
                        nonlocal_t = dma("gpsimd", qw2,
                                         w2_sb[:, kpc * 4 * D:(kpc + 1) * 4 * D].rearrange("p (hc d) -> p hc d", hc=4),
                                         w2_in[e][kpc * 512:(kpc + 1) * 512, :].rearrange("(hc p) d -> p hc d", p=128),
                                         waits=wp)
                return nonlocal_t

            for hc in range(HC):
                pp = hc // 4
                hcl = hc % 4
                if hc == 0:
                    issue_w1_piece(0)
                    if half == 0:
                        tw2 = issue_w2_pieces(2)
                        t_w2_done = tw2 or t_w2_done
                    issue_w1_piece(1)
                    if half == 0:
                        tw2 = issue_w2_pieces(1)
                        t_w2_done = tw2 or t_w2_done
                elif hc % 4 == 0 and hc // 4 + 1 < 8:
                    issue_w1_piece(hc // 4 + 1)
                    if half == 0:
                        tw2 = issue_w2_pieces(1)
                        t_w2_done = tw2 or t_w2_done
                w1buf = w1ps[(pbase + pp) % 2]
                m1k = len(t_gelu_seq)
                pm1 = PSA if m1k % 2 == 0 else PSC
                w = [t_w1p[pp], t_transp_done]
                if m1k >= 2:
                    w.append(t_gelu_seq[m1k - 2])
                elif m1k == 0:
                    w += [t_b0_psA_free]
                elif m1k == 1:
                    w += [t_logits_psC_free]
                if hc == 0 and t_mm2_last is not None:
                    w.append(t_mm2_last)
                gro = half * HQ
                mms = []
                for dc in range(DC):
                    lhs = w1buf[:, dc * 512 + hcl * 128: dc * 512 + (hcl + 1) * 128]
                    mms.append((pm1[:, 0:512], lhs, packT[:, dc * Q + gro: dc * Q + gro + 512],
                                dc == 0, dc == DC - 1))
                    mms.append((pm1[:, 512:640], lhs, packT[:, dc * Q + gro + 512: dc * Q + gro + 640],
                                dc == 0, dc == DC - 1))
                tmm = mmg(mms, waits=w)
                wg_ = [tmm, t_b1t]
                if hc == 0 and t_mm2_last is not None:
                    wg_.append(t_mm2_last)
                if hc == 0 and t_hdump is not None:
                    wg_.append(t_hdump)
                tg = op("scalar",
                        lambda ee, pm1=pm1, hc=hc, e=e:
                        ee.activation(out=hT[:, hc * HQ: hc * HQ + HQ],
                                      in_=pm1[:, 0:HQ], func=AF.Gelu_apprx_tanh,
                                      bias=b1t_sb[:, e * HC + hc: e * HC + hc + 1]),
                        waits=wg_)
                t_gelu_seq.append(tg)
            w1k += 8
            t_mm1_last = t_gelu_seq[-1]
            if half == 0:
                wb2 = [t_tidld] if e == 0 else ([t_mm2_by_e[e - 2]] if e >= 2 else [])
                t_b2_e[e] = dma("gpsimd", qb2[e % 2], b2_sb[:, (e % 2) * D:(e % 2 + 1) * D],
                                b2_in[:, e * D:(e + 1) * D], waits=wb2)
            if DEBUG2:
                t_hdump = dma("sync", qs, dbg["hT2"].rearrange("(eh p) c -> eh p c", p=128)[e * 2 + half],
                              hT[:, :], waits=[t_mm1_last])

            for rbl in range(5):
                yk = len(t_yev_all)
                pm2 = PSB if yk % 2 == 0 else PSD
                grb = half * 5 + rbl
                w = []
                if rbl == 0:
                    w = [t_mm1_last, t_w2_done, t_b2_e[e], t_o1b]
                if yk >= 2:
                    w.append(t_yev_all[yk - 2])
                elif yk == 0:
                    w += [t_b0_psB_free]
                elif yk == 1:
                    w += [t_bu_psD_free]
                mms = []
                for hc in range(HC):
                    lhs = hT[:, hc * HQ + rbl * 128: hc * HQ + (rbl + 1) * 128]
                    mms.append((pm2[:, 0:512], lhs, w2_sb[:, hc * D: hc * D + 512],
                                hc == 0, False))
                    mms.append((pm2[:, 512:1024], lhs, w2_sb[:, hc * D + 512: (hc + 1) * D],
                                hc == 0, False))
                eo = (e % 2) * D
                mms.append((pm2[:, 0:512], o1x128b_sb[:, :], b2_sb[:, eo: eo + 512],
                            False, True))
                mms.append((pm2[:, 512:1024], o1x128b_sb[:, :], b2_sb[:, eo + 512: eo + D],
                            False, True))
                t_mm2_last = mmg(mms, waits=w)
                tv = op("vector",
                        lambda e_, pm2=pm2, yk=yk:
                        e_.tensor_copy(out=ysbs[yk % 2][:, :], in_=pm2[:, 0:1024]),
                        waits=[t_mm2_last] + ([t_yw_all[yk - 2]] if yk >= 2 else []))
                t_yev_all.append(tv)
                t_yw_all.append(dma("scalar", qc, pydr[e * RB + grb], ysbs[yk % 2][:, :], waits=[tv]))
        t_mm2_by_e[e] = t_mm2_last

    t_py_all = t_yw_all[-1]
    b_ctx.close()

    c_ctx = contextlib.ExitStack()
    if True:
        _c = c_ctx.enter_context
        xnT2 = _c(nc.sbuf_tensor("xnT2", [128, DC * NTOK], BF16))
        gwt_sb = _c(nc.sbuf_tensor("gwt_sb", [128, DC * D], BF16))
        gateb_sb = _c(nc.sbuf_tensor("gateb_sb", [1, D], BF16))
        pw_c = _c(nc.sbuf_tensor("pw_c", [128, D], F32))
        ctile0 = _c(nc.sbuf_tensor("ctile0", [1, D], BF16))
        ctile1 = _c(nc.sbuf_tensor("ctile1", [1, D], BF16))
        d128c_sb = _c(nc.sbuf_tensor("d128c_sb", [128, D], F32))
        ld128c_sb = _c(nc.sbuf_tensor("ld128c_sb", [1, D], F32))
        c_c = _c(nc.sbuf_tensor("c_c", [128, D], BF16))
        gate_c0 = _c(nc.sbuf_tensor("gate_c0", [128, D], F32))
        gate_c1 = _c(nc.sbuf_tensor("gate_c1", [128, D], F32))
        sg2_c = _c(nc.sbuf_tensor("sg2_c", [128, D], F32))
        sg3_c = _c(nc.sbuf_tensor("sg3_c", [128, D], F32))
        s_c = _c(nc.sbuf_tensor("s_c", [128, D], F32))
        ya0 = _c(nc.sbuf_tensor("ya0", [128, D], BF16))
        ya1 = _c(nc.sbuf_tensor("ya1", [128, D], BF16))
        yb0 = _c(nc.sbuf_tensor("yb0", [128, D], BF16))
        yb1 = _c(nc.sbuf_tensor("yb1", [128, D], BF16))
        ysum_c = _c(nc.sbuf_tensor("ysum_c", [128, D], F32))
        xt0c = _c(nc.sbuf_tensor("xt0c", [128, D], F32))
        xt1c = _c(nc.sbuf_tensor("xt1c", [128, D], F32))
        osb0 = _c(nc.sbuf_tensor("osb0", [128, D], F32))
        osb1 = _c(nc.sbuf_tensor("osb1", [128, D], F32))
        yas = [ya0, ya1]
        ybs = [yb0, yb1]
        xtc = [xt0c, xt1c]
        osb = [osb0, osb1]
        xr = x_in.rearrange("(i p) d -> i p d", p=128)
        outr = out_io.rearrange("(i p) d -> i p d", p=128)

        t_bdrain = [t_py_all, t_mm2_last, t_yev_all[-1], t_gelu_seq[-1]]
        t_T2 = []
        for dc in range(DC):
            t_T2.append(dma("sync", qt, xnT2[:, dc * NTOK:(dc + 1) * NTOK],
                            xnd[:, dc * 128:(dc + 1) * 128],
                            waits=t_bdrain if dc == 0 else [], transpose=True))
        t_gwt = dma("gpsimd", qgw, gwt_sb[:, :].rearrange("p (dc e) -> p dc e", dc=DC),
                    gwt_in.rearrange("(dc p) e -> p dc e", p=128),
                    waits=t_bdrain)
        t_gateb = dma("gpsimd", qgw, gateb_sb[:, :], gateb_in[:, :], waits=[])

        def xnT2_tile(dc, i):
            return xnT2[:, dc * NTOK + i * 128: dc * NTOK + (i + 1) * 128]

        t_l128c = op("vector", lambda e: e.tensor_scalar(ld128c_sb[:, :], ldlog_sb[:, :], 128.0, None, ALU.mult))
        t_l128cb = op("vector", lambda e: e.tensor_copy(out=scrv[0:1, :], in_=ld128c_sb[:, 0:1]),
                      waits=[t_l128c])
        t_d128cmm = mmg([(PSB[:, 0:512], o1x128f_sb[:, :], ld128c_sb[:, 0:512], True, True),
                         (PSB[:, 512:1024], o1x128f_sb[:, :], ld128c_sb[:, 512:1024], True, True)],
                        waits=[t_l128cb] + t_bdrain + [t_yev_all[-1]])
        t_d128c0 = op("scalar", lambda e: e.activation(out=d128c_sb[:, :], in_=PSB[:, :], func=AF.Exp),
                      waits=[t_d128cmm])
        t_d128c = op("scalar", lambda e: e.activation(out=scrs[:, :], in_=d128c_sb[:, 0:1], func=AF.Copy))

        ctiles = [ctile0, ctile1]
        t_ct = [None] * NT
        t_h2 = [None] * NT
        t_xl = [None] * NT
        t_out = [None] * NT
        t_oa = [None] * NT
        t_sgm = [None] * NT
        t_exp2 = [None] * NT
        t_c2 = [None] * NT
        t_s2 = [None] * NT
        t_ct[0] = dma("sync", qs, ctiles[0][:, :], carrbd[0:1, :], waits=[t_carrw] + t_bdrain)
        t_ct[1] = dma("sync", qs, ctiles[1][:, :], carrbd[1:2, :], waits=[])
        t_ysum = [None] * NT
        t_sil = [None] * NT
        t_sgmul = [None] * NT

        t_xl[0] = dma("sync", qs, xtc[0][:, :], xr[0], waits=t_bdrain)
        t_xl[1] = dma("sync", qs, xtc[1][:, :], xr[1])

        gate_cs = [gate_c0, gate_c1]
        t_tyb = [None] * NT

        def emit_front(i):
            par = i % 2
            pbt = PSA if par == 0 else PSC
            pwt = PSB if par == 0 else PSD
            w = [t_cwt]
            if i >= 2:
                w += [t_c2[i - 2], t_s2[i - 2]]
            if i < 2:
                w += t_bdrain + [t_yev_all[-1]]
            t_pb2 = mmg(
                [(pbt[:, 0:512], bu_sb[0:64, i * 128:(i + 1) * 128], cwt_sb[:, 0:512], True, True),
                 (pbt[:, 512:1024], bu_sb[0:64, i * 128:(i + 1) * 128], cwt_sb[:, 512:1024], True, True)],
                waits=w)
            if i == 0:
                t_pw2 = mmg(
                    [(pwt[:, 0:512], iotag_sb[:, 0:128], ldlog_sb[:, 0:512], True, True),
                     (pwt[:, 512:1024], iotag_sb[:, 0:128], ldlog_sb[:, 512:1024], True, True)],
                    waits=[t_iog, t_ldlog, t_d128c])
                t_exp0 = op("scalar", lambda e, p=pwt: e.activation(out=pw_c[:, :], in_=p[:, :], func=AF.Exp),
                               waits=[t_pw2])
                t_exp2[i] = op("scalar", lambda e: e.activation(out=scrs[:, :], in_=pw_c[:, 0:1], func=AF.Copy))
            else:
                t_exp2[i] = op("vector", lambda e: e.tensor_mul(pw_c[:, :], pw_c[:, :], d128c_sb[:, :]),
                               waits=[t_c2[i - 1], t_d128c])
            t_c2[i] = op("vector", lambda e, p=pbt: e.tensor_mul(c_c[:, :], p[:, :], pw_c[:, :]),
                         waits=[t_exp2[i], t_pb2] + ([t_h2[i - 1]] if i >= 1 else []))
            ctb = ctiles[i % 2]
            t_c2b = op("vector", lambda e: e.tensor_copy(out=scrv[:, :], in_=c_c[:, 0:1]))
            t_h2[i] = mmg(
                [(pbt[:, 0:512], trikb_sb[:, :], c_c[:, 0:512], True, False),
                 (pbt[:, 0:512], o1x128b_sb[:, :], ctb[:, 0:512], False, True),
                 (pbt[:, 512:1024], trikb_sb[:, :], c_c[:, 512:1024], True, False),
                 (pbt[:, 512:1024], o1x128b_sb[:, :], ctb[:, 512:1024], False, True)],
                waits=[t_c2b, t_trikb, t_o1b, t_ct[i]])
            if i + 2 < NT:
                t_ct[i + 2] = dma("sync", qs, ctb[:, :], carrbd[i + 2:i + 3, :], waits=[t_h2[i]])
            mms = []
            for dc in range(DC):
                mms.append((pwt[:, 0:512], xnT2_tile(dc, i), gwt_sb[:, dc * D: dc * D + 512],
                            dc == 0, False))
                mms.append((pwt[:, 512:1024], xnT2_tile(dc, i), gwt_sb[:, dc * D + 512: (dc + 1) * D],
                            dc == 0, False))
            mms.append((pwt[:, 0:512], o1x128b_sb[:, :], gateb_sb[:, 0:512], False, True))
            mms.append((pwt[:, 512:1024], o1x128b_sb[:, :], gateb_sb[:, 512:1024], False, True))
            w = [t_gwt, t_gateb, t_o1b]
            if i >= 2:
                w.append(t_sgm[i - 2])
            if i < 2:
                w.append(t_yev_all[-1])
            if i == 0:
                w.append(t_exp2[0])
            t_gmm = mmg(mms, waits=w + (list(t_T2) if i == 0 else []))
            t_sgm0 = op("scalar", lambda e, p=pwt, i=i: e.activation(out=gate_cs[i % 2][:, :], in_=p[:, :], func=AF.Sigmoid),
                        waits=[t_gmm] + ([t_s2[i - 2]] if i >= 2 else []))
            t_sgm[i] = op("scalar", lambda e, i=i: e.activation(out=scrs[:, :], in_=gate_cs[i % 2][:, 0:1], func=AF.Copy))

        def emit_tail(i):
            par = i % 2
            pbt = PSA if par == 0 else PSC
            t_s2[i] = op("vector", lambda e, p=pbt, i=i: e.tensor_mul(s_c[:, :], p[:, :], gate_cs[i % 2][:, :]),
                         waits=[t_sgm[i], t_h2[i]] + ([t_sgmul[i - 1]] if i >= 1 else []))
            if DEBUG:
                dma("scalar", qc, dbg["s"].rearrange("(i p) d -> i p d", p=128)[i], s_c[:, :],
                    waits=[t_s2[i]])
            w = [t_py_all]
            if i >= 2:
                w.append(t_ysum[i - 2])
            t_ya = idma(yas[par][:, :], None, pyd,
                        IndirectOffsetOnAxis(pdest0_sb[:, i:i + 1], 0), waits=w)
            t_yb = idma(ybs[par][:, :], None, pyd,
                        IndirectOffsetOnAxis(pdest1_sb[:, i:i + 1], 0), waits=[])
            tyb = op("vector", lambda e, par=par, i=i: e.tensor_scalar(sg2_c[:, :], ybs[par][:, :],
                                                                        gate1_all[:, i:i + 1], None, ALU.mult),
                     waits=[t_yb, t_gate[1]] + ([t_ysum[i - 1]] if i >= 1 else []))
            t_ysum[i] = op("vector", lambda e, par=par, i=i: e.scalar_tensor_tensor(
                               ysum_c[:, :], yas[par][:, :], gate0_all[:, i:i + 1], sg2_c[:, :],
                               ALU.mult, ALU.add),
                           waits=[t_ya, tyb, t_gate[0]] + ([t_sil[i - 1]] if i >= 1 else []))
            tsg2a = op("scalar", lambda e: e.activation(out=sg3_c[:, :], in_=ysum_c[:, :], func=AF.Sigmoid),
                       waits=[t_ysum[i]] + ([t_sil[i - 1]] if i >= 1 else []))
            tsg2 = op("scalar", lambda e: e.activation(out=scrs[:, :], in_=sg3_c[:, 0:1], func=AF.Copy))
            t_sil[i] = op("vector", lambda e: e.tensor_mul(ysum_c[:, :], ysum_c[:, :], sg3_c[:, :]),
                          waits=[tsg2])
            if DEBUG:
                dma("scalar", qc, dbg["g"].rearrange("(i p) d -> i p d", p=128)[i], ysum_c[:, :],
                    waits=[t_sil[i]])
            t_sgmul[i] = op("vector", lambda e: e.tensor_mul(s_c[:, :], s_c[:, :], ysum_c[:, :]),
                            waits=[t_sil[i], t_s2[i]])
            t_oa[i] = op("vector", lambda e, par=par: e.tensor_add(osb[par][:, :], s_c[:, :], xtc[par][:, :]),
                         waits=[t_sgmul[i], t_xl[i]] + ([t_out[i - 2]] if i >= 2 else []))
            t_out[i] = dma("sync", qs, outr[i], osb[par][:, :], waits=[t_oa[i]])
            if i + 2 < NT:
                t_xl[i + 2] = dma("sync", qs, xtc[par][:, :], xr[i + 2], waits=[t_oa[i]])

        emit_front(0)
        emit_front(1)
        for i in range(1, NT):
            emit_tail(i - 1)
            if i + 1 < NT:
                emit_front(i + 1)
        emit_tail(NT - 1)

        if DEBUG2:
            dma("sync", qs, dbg["log2"][:, :], logits_all[:, :], waits=[t_py_all])
            dma("sync", qs, dbg["bu2"][:, :], bu_sb[:, :], waits=[t_py_all])
            dma("sync", qs, dbg["pyd2"][:, :], pyd[:, :], waits=[t_py_all])
            dma("sync", qs, dbg["xnd2"][:, :], xnd[:, :], waits=[t_py_all])
            dma("sync", qs, dbg["tidd2"][:, :], tidd[:, :], waits=[t_py_all])
            dma("sync", qs, dbg["carrbd2"][:, :], carrbd[:, :], waits=[t_py_all])
        final_qs = qs.tok()

        def f_final(e, tok=final_qs):
            e.wait_ge(tok[0], tok[1])
        emit("sync", f_final)
    c_ctx.close()

    with nc.Block() as block:
        @block.sync
        def _(e):
            for f in prog["sync"]:
                f(e)

        @block.scalar
        def _(e):
            for f in prog["scalar"]:
                f(e)

        @block.vector
        def _(e):
            for f in prog["vector"]:
                f(e)

        @block.tensor
        def _(e):
            for f in prog["tensor"]:
                f(e)

        @block.gpsimd
        def _(e):
            for f in prog["gpsimd"]:
                f(e)

    stack.close()
    return nc


def _consts():
    bf = ml_dtypes.bfloat16
    c = {}
    c["trikf"] = np.triu(np.ones((128, 128), np.float32), 1)
    c["trikb"] = c["trikf"].astype(bf)
    c["o128f"] = np.ones((128, 1), np.float32)
    c["o128b"] = c["o128f"].astype(bf)
    c["o1x128f"] = np.ones((1, 128), np.float32)
    c["o1x128b"] = c["o1x128f"].astype(bf)
    c["eqrow"] = np.ascontiguousarray(np.tile((np.arange(E) * Q).astype(np.float32), (128, 1)))
    c["e10row"] = np.ascontiguousarray(np.tile((np.arange(E) * RB).astype(np.float32), (128, 1)))
    c["tokid"] = np.ascontiguousarray(np.arange(NTOK, dtype=np.int32).reshape(NT, 128).T)
    return c


def _host_prep(L0_attn_x, L0_attn_B_w, L0_attn_C_w, L0_attn_gate_w, L0_attn_gate_b,
               L0_attn_log_decay, L0_ffn_Wg, L0_ffn_W1, L0_ffn_b1, L0_ffn_W2, L0_ffn_b2):
    x = np.ascontiguousarray(np.asarray(L0_attn_x, dtype=np.float32)).reshape(NCORE * NTOK, D)
    shared = {
        "w1": np.ascontiguousarray(np.asarray(L0_ffn_W1, dtype=np.float32)),
        "w2": np.ascontiguousarray(np.asarray(L0_ffn_W2, dtype=np.float32)),
        "wg": np.ascontiguousarray(np.asarray(L0_ffn_Wg, dtype=np.float32)),
        "bwt": np.ascontiguousarray(np.asarray(L0_attn_B_w, dtype=np.float32).T),
        "cwt": np.ascontiguousarray(np.asarray(L0_attn_C_w, dtype=np.float32).T),
        "gwt": np.ascontiguousarray(np.asarray(L0_attn_gate_w, dtype=np.float32).T),
        "gateb": np.asarray(L0_attn_gate_b, dtype=np.float32).reshape(1, D),
        "ldec": np.asarray(L0_attn_log_decay, dtype=np.float32).reshape(1, D),
        "b1t": np.ascontiguousarray(
            np.asarray(L0_ffn_b1, dtype=np.float32).reshape(E, HC, 128).transpose(2, 0, 1).reshape(128, E * HC)),
        "b2": np.ascontiguousarray(np.asarray(L0_ffn_b2, dtype=np.float32).reshape(1, E * D)),
    }
    shared.update(_consts())
    in_maps = []
    for c in range(NCORE):
        m = dict(shared)
        m["x"] = np.ascontiguousarray(x[c * NTOK:(c + 1) * NTOK])
        m["iotag"] = ((c % 2) * NTOK + np.arange(128, dtype=np.float32)).reshape(1, 128)
        m["rmask"] = (np.arange(8) < c).astype(np.float32).reshape(8, 1)
        m["cmask"] = (np.eye(8, dtype=np.float32)[c - 1].reshape(8, 1)
                      if c % 2 == 1 else np.zeros((8, 1), np.float32))
        in_maps.append(m)
    return in_maps


def kernel(**inputs):
    in_maps = _host_prep(**inputs)
    nc = build_nc()
    res = run_bass_kernel_spmd(nc, in_maps, core_ids=list(range(NCORE)))
    out = np.concatenate([np.asarray(r["out"]) for r in res.results], axis=0)
    return out.reshape(4, 8192, 1024)


# version 12
# speedup vs baseline: 1.1716x; 1.0406x over previous
import numpy as np
import ml_dtypes
import contextlib
import concourse.bass as bass
import concourse.mybir as mybir
from concourse.bass import IndirectOffsetOnAxis
from concourse.bass_utils import run_bass_kernel_spmd

F32 = mybir.dt.float32
BF16 = mybir.dt.bfloat16
I32 = mybir.dt.int32
AF = mybir.ActivationFunctionType
ALU = mybir.AluOpType
AXX = mybir.AxisListType.X

NCORE = 8
NTOK = 4096
NT = 32
D = 1024
DC = 8
S = 64
E = 8
H = 4096
HC = 32
Q = 1280
RB = 10
HQ = 640
CAP = 8602
EPS = 1e-5

ENGINES = ("sync", "scalar", "vector", "tensor", "gpsimd")

DEBUG = False
DEBUG2 = False
PAD_DRAM = 0


class TSem:
    __slots__ = ("h", "v", "name")

    def __init__(self, h, name):
        self.h = h
        self.v = 0
        self.name = name

    def bump(self, n):
        self.v += n
        return (self.h, self.v)

    def tok(self):
        return (self.h, self.v)


def build_nc():
    nc = bass.Bass("TRN2", target_bir_lowering=False)

    x_in = nc.dram_tensor("x", [NTOK, D], F32, kind="ExternalInput").ap()
    w1_in = nc.dram_tensor("w1", [E, D, H], F32, kind="ExternalInput").ap()
    w2_in = nc.dram_tensor("w2", [E, H, D], F32, kind="ExternalInput").ap()
    wg_in = nc.dram_tensor("wg", [D, E], F32, kind="ExternalInput").ap()
    bwt_in = nc.dram_tensor("bwt", [D, S], F32, kind="ExternalInput").ap()
    cwt_in = nc.dram_tensor("cwt", [S, D], F32, kind="ExternalInput").ap()
    gwt_in = nc.dram_tensor("gwt", [D, D], F32, kind="ExternalInput").ap()
    gateb_in = nc.dram_tensor("gateb", [1, D], F32, kind="ExternalInput").ap()
    ldec_in = nc.dram_tensor("ldec", [1, D], F32, kind="ExternalInput").ap()
    b1t_in = nc.dram_tensor("b1t", [128, E * HC], F32, kind="ExternalInput").ap()
    b2_in = nc.dram_tensor("b2", [1, E * D], F32, kind="ExternalInput").ap()
    trikf_in = nc.dram_tensor("trikf", [128, 128], F32, kind="ExternalInput").ap()
    trikb_in = nc.dram_tensor("trikb", [128, 128], BF16, kind="ExternalInput").ap()
    o128f_in = nc.dram_tensor("o128f", [128, 1], F32, kind="ExternalInput").ap()
    o128b_in = nc.dram_tensor("o128b", [128, 1], BF16, kind="ExternalInput").ap()
    o1x128f_in = nc.dram_tensor("o1x128f", [1, 128], F32, kind="ExternalInput").ap()
    o1x128b_in = nc.dram_tensor("o1x128b", [1, 128], BF16, kind="ExternalInput").ap()
    eqrow_in = nc.dram_tensor("eqrow", [128, E], F32, kind="ExternalInput").ap()
    e10row_in = nc.dram_tensor("e10row", [128, E], F32, kind="ExternalInput").ap()
    tokid_in = nc.dram_tensor("tokid", [128, NT], I32, kind="ExternalInput").ap()
    iotag_in = nc.dram_tensor("iotag", [1, 128], F32, kind="ExternalInput").ap()
    rmask_in = nc.dram_tensor("rmask", [8, 1], F32, kind="ExternalInput").ap()
    cmask_in = nc.dram_tensor("cmask", [8, 1], F32, kind="ExternalInput").ap()
    out_io = nc.dram_tensor("out", [NTOK, D], F32, kind="ExternalOutput").ap()

    dbg = {}
    if DEBUG2:
        dbg["pyd2"] = nc.dram_tensor("dbg_pyd2", [E * Q, D], BF16, kind="ExternalOutput").ap()
        dbg["xnd2"] = nc.dram_tensor("dbg_xnd2", [NTOK, D], BF16, kind="ExternalOutput").ap()
        dbg["tidd2"] = nc.dram_tensor("dbg_tidd2", [1, 128 * E * RB], I32, kind="ExternalOutput").ap()
        dbg["carrbd2"] = nc.dram_tensor("dbg_carrbd2", [NT, D], BF16, kind="ExternalOutput").ap()
        dbg["packT2"] = nc.dram_tensor("dbg_packT2", [E * 128, DC * Q], BF16, kind="ExternalOutput").ap()
        dbg["hT2"] = nc.dram_tensor("dbg_hT2", [E * 2 * 128, HC * HQ], BF16, kind="ExternalOutput").ap()
        dbg["log2"] = nc.dram_tensor("dbg_log2", [128, NT * E], F32, kind="ExternalOutput").ap()
        dbg["bu2"] = nc.dram_tensor("dbg_bu2", [64, NTOK], BF16, kind="ExternalOutput").ap()
    if DEBUG:
        dbg["logits"] = nc.dram_tensor("dbg_logits", [128, NT * E], F32, kind="ExternalOutput").ap()
        dbg["carry"] = nc.dram_tensor("dbg_carry", [1, D], F32, kind="ExternalOutput").ap()
        dbg["rank0"] = nc.dram_tensor("dbg_rank0", [128, NT], F32, kind="ExternalOutput").ap()
        dbg["rank1"] = nc.dram_tensor("dbg_rank1", [128, NT], F32, kind="ExternalOutput").ap()
        dbg["gate0"] = nc.dram_tensor("dbg_gate0", [128, NT], F32, kind="ExternalOutput").ap()
        dbg["gate1"] = nc.dram_tensor("dbg_gate1", [128, NT], F32, kind="ExternalOutput").ap()
        dbg["pdest0"] = nc.dram_tensor("dbg_pdest0", [128, NT], I32, kind="ExternalOutput").ap()
        dbg["pdest1"] = nc.dram_tensor("dbg_pdest1", [128, NT], I32, kind="ExternalOutput").ap()
        dbg["tidT"] = nc.dram_tensor("dbg_tidT", [128, E * RB], I32, kind="ExternalOutput").ap()
        dbg["gateT"] = nc.dram_tensor("dbg_gateT", [128, E * RB], F32, kind="ExternalOutput").ap()
        dbg["s"] = nc.dram_tensor("dbg_s", [NTOK, D], F32, kind="ExternalOutput").ap()
        dbg["g"] = nc.dram_tensor("dbg_g", [NTOK, D], F32, kind="ExternalOutput").ap()

    if PAD_DRAM:
        nc.dram_tensor("pad_dram", [1, PAD_DRAM], F32)
    xnd = nc.dram_tensor("xnd", [NTOK, D], BF16).ap()
    packd0 = nc.dram_tensor("packd0", [Q, D], BF16).ap()
    packd1 = nc.dram_tensor("packd1", [Q, D], BF16).ap()
    pyd = nc.dram_tensor("pyd", [E * Q, D], BF16).ap()
    tidd = nc.dram_tensor("tidd", [1, 128 * E * RB], I32).ap()
    tidd2d = tidd.rearrange("o (p c) -> (o p) c", p=128)
    carrd = nc.dram_tensor("carrd", [NT, D], F32).ap()
    carrbd = nc.dram_tensor("carrbd", [NT, D], BF16).ap()
    ccin = nc.dram_tensor("ccin", [1, 1040], F32).ap()
    ccout = nc.dram_tensor("ccout", [8, 1040], F32).ap()

    prog = {k: [] for k in ENGINES}

    def emit(eng, fn):
        prog[eng].append(fn)

    stack = contextlib.ExitStack()

    def sem(name):
        return TSem(stack.enter_context(nc.semaphore(name)), name)

    qs = sem("qs")
    qc = sem("qc")
    qt = sem("qt")
    qgi = sem("qgi")
    qw1 = [sem("qw1a"), sem("qw1b")]
    qw2 = sem("qw2")
    qb2 = [sem("qb2a"), sem("qb2b")]
    qgw = sem("qgw")
    qcc = sem("qcc")
    qg = sem("qg")
    vs = sem("vs")
    ss = sem("ss")
    ps = sem("ps")
    cc = sem("cc")
    ge = sem("ge")
    last_v = [None]
    last_s = [None]

    def dma(queue, qsem, out, in_, waits=(), **kw):
        tok = qsem.bump(16)
        wl = tuple(t for t in waits if t is not None)

        def f(e, out=out, in_=in_, wl=wl, kw=kw, h=qsem.h):
            for (sh, v) in wl:
                e.wait_ge(sh, v)
            e.dma_start(out=out, in_=in_, **kw).then_inc(h, 16)
        emit(queue, f)
        return tok

    def idma(out, out_off, in_, in_off, waits=()):
        tok = qgi.bump(16)
        wl = tuple(t for t in waits if t is not None)

        def f(e, out=out, oo=out_off, in_=in_, io=in_off, wl=wl, h=qgi.h):
            for (sh, v) in wl:
                e.wait_ge(sh, v)
            e.indirect_dma_start(out=out, out_offset=oo, in_=in_, in_offset=io).then_inc(h, 16)
        emit("gpsimd", f)
        return tok

    def op(eng, build, waits=()):
        esem = vs if eng == "vector" else ss
        last = last_v if eng == "vector" else last_s
        wl = [t for t in waits if t is not None]
        if last[0] is not None:
            wl.append(last[0])
        tok = esem.bump(1)
        wl = tuple(wl)

        def f(e, build=build, wl=wl, h=esem.h):
            for (sh, v) in wl:
                e.wait_ge(sh, v)
            build(e).then_inc(h, 1)
        emit(eng, f)
        last[0] = tok
        return tok

    def mmg(mms, waits=()):
        tok = ps.bump(1)
        wl = tuple(t for t in waits if t is not None)

        def f(e, mms=mms, wl=wl, h=ps.h):
            for (sh, v) in wl:
                e.wait_ge(sh, v)
            for (o, l, r, st, sp) in mms:
                ins = e.matmul(o, l, r, start=st, stop=sp)
            ins.then_inc(h, 1)
        emit("tensor", f)
        return tok

    logits_all = stack.enter_context(nc.sbuf_tensor("logits_all", [128, NT * E], F32))
    bu_sb = stack.enter_context(nc.sbuf_tensor("bu_sb", [64, NTOK], BF16))
    cwt_sb = stack.enter_context(nc.sbuf_tensor("cwt_sb", [64, D], BF16))
    ldlog_sb = stack.enter_context(nc.sbuf_tensor("ldlog_sb", [1, D], F32))
    trikf_sb = stack.enter_context(nc.sbuf_tensor("trikf_sb", [128, 128], F32))
    trikb_sb = stack.enter_context(nc.sbuf_tensor("trikb_sb", [128, 128], BF16))
    o128f_sb = stack.enter_context(nc.sbuf_tensor("o128f_sb", [128, 1], F32))
    o128b_sb = stack.enter_context(nc.sbuf_tensor("o128b_sb", [128, 1], BF16))
    o1x128f_sb = stack.enter_context(nc.sbuf_tensor("o1x128f_sb", [1, 128], F32))
    o1x128b_sb = stack.enter_context(nc.sbuf_tensor("o1x128b_sb", [1, 128], BF16))
    iotag_sb = stack.enter_context(nc.sbuf_tensor("iotag_sb", [1, 128], F32))
    pdest0_sb = stack.enter_context(nc.sbuf_tensor("pdest0_sb", [128, NT], I32))
    pdest1_sb = stack.enter_context(nc.sbuf_tensor("pdest1_sb", [128, NT], I32))
    crow_sb = stack.enter_context(nc.sbuf_tensor("crow_sb", [1, E], F32))
    tidt_sb = stack.enter_context(nc.sbuf_tensor("tidt_sb", [128, E * RB], I32))
    stat_sb = stack.enter_context(nc.sbuf_tensor("stat_sb", [128, 16], F32))
    scrv = stack.enter_context(nc.sbuf_tensor("scrv", [128, 1], F32))
    scrs = stack.enter_context(nc.sbuf_tensor("scrs", [128, 1], F32))
    oh1_all = stack.enter_context(nc.sbuf_tensor("oh1_all", [128, NT * E], F32))
    oh2_all = stack.enter_context(nc.sbuf_tensor("oh2_all", [128, NT * E], F32))
    rank0_all = stack.enter_context(nc.sbuf_tensor("rank0_all", [128, NT], F32))
    rank1_all = stack.enter_context(nc.sbuf_tensor("rank1_all", [128, NT], F32))
    g1_all = stack.enter_context(nc.sbuf_tensor("g1_all", [128, NT], F32))
    g2_all = stack.enter_context(nc.sbuf_tensor("g2_all", [128, NT], F32))
    gate0_all = stack.enter_context(nc.sbuf_tensor("gate0_all", [128, NT], F32))
    gate1_all = stack.enter_context(nc.sbuf_tensor("gate1_all", [128, NT], F32))
    d20_all = stack.enter_context(nc.sbuf_tensor("d20_all", [128, NT], I32))
    d21_all = stack.enter_context(nc.sbuf_tensor("d21_all", [128, NT], I32))
    tmp256 = stack.enter_context(nc.sbuf_tensor("tmp256", [128, NT * E], F32))
    t32a = stack.enter_context(nc.sbuf_tensor("t32a", [128, NT], F32))
    t32b = stack.enter_context(nc.sbuf_tensor("t32b", [128, NT], F32))
    t32c = stack.enter_context(nc.sbuf_tensor("t32c", [128, NT], F32))
    i32a = stack.enter_context(nc.sbuf_tensor("i32a", [128, NT], I32))
    i32b = stack.enter_context(nc.sbuf_tensor("i32b", [128, NT], I32))
    i32c = stack.enter_context(nc.sbuf_tensor("i32c", [128, NT], I32))
    rmask_sb = stack.enter_context(nc.sbuf_tensor("rmask_sb", [8, 1], F32))
    cmask_sb = stack.enter_context(nc.sbuf_tensor("cmask_sb", [8, 1], F32))
    offsrow_sb = stack.enter_context(nc.sbuf_tensor("offsrow_sb", [1, E], F32))
    offsb_sb = stack.enter_context(nc.sbuf_tensor("offsb_sb", [128, E], F32))
    pack_sb = stack.enter_context(nc.sbuf_tensor("pack_sb", [1, 1040], F32))
    gath_sb = stack.enter_context(nc.sbuf_tensor("gath_sb", [8, 1040], F32))
    carry_local = stack.enter_context(nc.sbuf_tensor("carry_local", [1, D], F32))
    carry0_sb = stack.enter_context(nc.sbuf_tensor("carry0_sb", [1, D], F32))
    carr32_sb = stack.enter_context(nc.sbuf_tensor("carr32_sb", [32, D], F32))
    carrb32_sb = stack.enter_context(nc.sbuf_tensor("carrb32_sb", [32, D], BF16))

    PSA = stack.enter_context(nc.psum_tensor("PSA", [128, 1024], F32))
    PSB = stack.enter_context(nc.psum_tensor("PSB", [128, 1024], F32))
    PSC = stack.enter_context(nc.psum_tensor("PSC", [128, 1024], F32))
    PSD = stack.enter_context(nc.psum_tensor("PSD", [128, 1024], F32))

    t_cwt = dma("gpsimd", qg, cwt_sb[:, :], cwt_in[:, :])
    t_ldec_sb = stack.enter_context(nc.sbuf_tensor("ldec_sb", [1, D], F32))
    ldec_sb = t_ldec_sb
    t_ldec = dma("sync", qs, ldec_sb[:, :], ldec_in[:, :])
    t_trikf = dma("sync", qs, trikf_sb[:, :], trikf_in[:, :])
    t_trikb = dma("sync", qs, trikb_sb[:, :], trikb_in[:, :])
    t_o128f = dma("sync", qs, o128f_sb[:, :], o128f_in[:, :])
    t_o128b = dma("sync", qs, o128b_sb[:, :], o128b_in[:, :])
    t_o1f = dma("sync", qs, o1x128f_sb[:, :], o1x128f_in[:, :])
    t_o1b = dma("sync", qs, o1x128b_sb[:, :], o1x128b_in[:, :])
    t_iog = dma("sync", qs, iotag_sb[:, :], iotag_in[:, :])

    t_sg0 = op("scalar", lambda e: e.activation(out=ldlog_sb[:, :], in_=ldec_sb[:, :], func=AF.Sigmoid),
               waits=[t_ldec])
    t_ldlog = op("scalar", lambda e: e.activation(out=ldlog_sb[:, :], in_=ldlog_sb[:, :], func=AF.Ln),
                 waits=[t_sg0])
    t_z3 = op("vector", lambda e: e.memset(carry_local[:, :], 0.0))

    a_ctx = contextlib.ExitStack()
    xnT = a_ctx.enter_context(nc.sbuf_tensor("xnT", [128, DC * NTOK], BF16))

    def xnT_tile(dc, i):
        return xnT[:, dc * NTOK + i * 128: dc * NTOK + (i + 1) * 128]

    if True:
        _a = a_ctx.enter_context
        bwt_sb = _a(nc.sbuf_tensor("bwt_sb", [128, DC * S], BF16))
        wg_sb = _a(nc.sbuf_tensor("wg_sb", [128, DC * E], BF16))
        xt0 = _a(nc.sbuf_tensor("xt0", [128, D], F32))
        xt1 = _a(nc.sbuf_tensor("xt1", [128, D], F32))
        xnb0 = _a(nc.sbuf_tensor("xnb0", [128, D], BF16))
        xnb1 = _a(nc.sbuf_tensor("xnb1", [128, D], BF16))
        pw_a = _a(nc.sbuf_tensor("pw_a", [128, D], F32))
        pws_a = _a(nc.sbuf_tensor("pws_a", [128, D], F32))
        ld128_sb = _a(nc.sbuf_tensor("ld128_sb", [1, D], F32))
        d128_sb = _a(nc.sbuf_tensor("d128_sb", [128, D], F32))
        c_a = _a(nc.sbuf_tensor("c_a", [128, D], BF16))
        d12_all = _a(nc.sbuf_tensor("d12_all", [128, NT], F32))
        cnt_all = _a(nc.sbuf_tensor("cnt_all", [128, NT * E], F32))
        srt_sb = _a(nc.sbuf_tensor("srt_sb", [128, E], F32))
        s01_sb = _a(nc.sbuf_tensor("s01_sb", [128, E], F32))
        eqrow_sb = _a(nc.sbuf_tensor("eqrow_sb", [128, E], F32))
        e10row_sb = _a(nc.sbuf_tensor("e10row_sb", [128, E], F32))
        tokid_sb = _a(nc.sbuf_tensor("tokid_sb", [128, NT], I32))
        colrow0 = _a(nc.sbuf_tensor("colrow0", [1, D], F32))
        colrow1 = _a(nc.sbuf_tensor("colrow1", [1, D], F32))
        zeroi_sb = _a(nc.sbuf_tensor("zeroi_sb", [128, E * RB], I32))

        t_bwt = dma("gpsimd", qg, bwt_sb[:, :].rearrange("p (dc s) -> p dc s", dc=DC),
                     bwt_in.rearrange("(dc p) s -> p dc s", p=128))
        t_wg = dma("gpsimd", qg, wg_sb[:, :].rearrange("p (dc e) -> p dc e", dc=DC),
                    wg_in.rearrange("(dc p) e -> p dc e", p=128))
        t_cwt = t_bwt = t_wg = qg.tok()
        t_eq = dma("sync", qs, eqrow_sb[:, :], eqrow_in[:, :])
        t_e10 = dma("sync", qs, e10row_sb[:, :], e10row_in[:, :])
        t_tok = dma("sync", qs, tokid_sb[:, :], tokid_in[:, :])
        t_rm = dma("sync", qs, rmask_sb[:, :], rmask_in[:, :])
        t_cm = dma("sync", qs, cmask_sb[:, :], cmask_in[:, :])
        t_z1 = op("vector", lambda e: e.memset(zeroi_sb[:, :], 0))
        t_tid0 = dma("sync", qs, tidd2d, zeroi_sb[:, :], waits=[t_z1])

        xts = [xt0, xt1]
        xnbs = [xnb0, xnb1]
        xr = x_in.rearrange("(i p) d -> i p d", p=128)
        xndr = xnd.rearrange("(i p) d -> i p d", p=128)

        t_xload = [None] * NT
        t_xm = [None] * NT
        t_xnb = [None] * NT
        t_xnw = [None] * NT

        t_xload[0] = dma("sync", qs, xts[0][:, :], xr[0])
        if NT > 1:
            t_xload[1] = dma("sync", qs, xts[1][:, :], xr[1])
        for i in range(NT):
            b = i % 2
            t1 = op("vector", lambda e, b=b: e.bn_stats(stat_sb[:, 0:6], xts[b][:, 0:512]),
                    waits=[t_xload[i]])
            t2 = op("vector", lambda e, b=b: e.bn_stats(stat_sb[:, 6:12], xts[b][:, 512:1024]))
            t3 = op("vector", lambda e: e.bn_aggr(stat_sb[:, 12:14], stat_sb[:, 0:12]))
            t3b = op("vector", lambda e: e.tensor_scalar(stat_sb[:, 13:14], stat_sb[:, 13:14], float(EPS), None, ALU.add))
            t3c = op("vector", lambda e: e.tensor_copy(out=scrv[:, :], in_=stat_sb[:, 13:14]))
            t6 = op("scalar", lambda e: e.activation(out=stat_sb[:, 14:15], in_=stat_sb[:, 13:14], func=AF.Sqrt),
                    waits=[t3c])
            t6b = op("scalar", lambda e: e.activation(out=scrs[:, :], in_=stat_sb[:, 14:15], func=AF.Copy))
            t7 = op("vector", lambda e: e.reciprocal(stat_sb[:, 14:15], stat_sb[:, 14:15]),
                    waits=[t6b])
            t_xnb[i] = op("vector", lambda e, b=b: e.tensor_scalar(xnbs[b][:, :], xts[b][:, :],
                                                                   stat_sb[:, 12:13], stat_sb[:, 14:15],
                                                                   ALU.subtract, ALU.mult),
                          waits=[t_xnw[i - 2]] if i >= 2 else [])
            t_xm[i] = t_xnb[i]
            t_xnw[i] = dma("scalar", qc, xndr[i], xnbs[b][:, :], waits=[t_xnb[i]])
            if i + 2 < NT:
                t_xload[i + 2] = dma("sync", qs, xts[b][:, :], xr[i + 2], waits=[t_xnb[i]])

        t_xnall = qc.tok()
        t_T = []
        for dc in range(DC):
            t_T.append(dma("sync", qt, xnT[:, dc * NTOK:(dc + 1) * NTOK],
                           xnd[:, dc * 128:(dc + 1) * 128], waits=[t_xnall], transpose=True))

        t_logev = [None] * NT
        t_buev = [None] * (NT // 4)
        for g in range(NT // 4):
            for i in range(4 * g, 4 * g + 4):
                pslot = PSC[:, 0:8] if i % 2 == 0 else PSC[:, 512:520]
                w = (list(t_T) + [t_wg]) if i == 0 else []
                if i >= 2:
                    w = w + [t_logev[i - 2]]
                tpe = mmg([(pslot, xnT_tile(dc, i), wg_sb[:, dc * E:(dc + 1) * E],
                            dc == 0, dc == DC - 1) for dc in range(DC)], waits=w)
                t_logev[i] = op("vector", lambda e, i=i, pslot=pslot:
                                e.tensor_copy(out=logits_all[:, i * E:(i + 1) * E], in_=pslot),
                                waits=[tpe])
            pslot = PSD[0:64, 0:512] if g % 2 == 0 else PSD[0:64, 512:1024]
            w = (list(t_T) + [t_bwt]) if g == 0 else []
            if g >= 2:
                w = w + [t_buev[g - 2]]
            tpe = mmg([(pslot, bwt_sb[:, dc * S:(dc + 1) * S],
                        xnT[:, dc * NTOK + g * 512: dc * NTOK + (g + 1) * 512],
                        dc == 0, dc == DC - 1) for dc in range(DC)], waits=w)
            t_buev[g] = op("vector", lambda e, g=g, pslot=pslot:
                           e.tensor_copy(out=bu_sb[:, g * 512:(g + 1) * 512], in_=pslot),
                           waits=[tpe])
            t_buev[g] = op("vector", lambda e: e.tensor_copy(out=scrv[0:64, :], in_=bu_sb[0:64, g * 512:g * 512 + 1]))

        t_l128 = op("vector", lambda e: e.tensor_scalar(ld128_sb[:, :], ldlog_sb[:, :], 128.0, None, ALU.mult),
                    waits=[t_ldlog])
        t_d128mm = mmg([(PSB[:, 0:512], o1x128f_sb[:, :], ld128_sb[:, 0:512], True, True),
                        (PSB[:, 512:1024], o1x128f_sb[:, :], ld128_sb[:, 512:1024], True, True)],
                       waits=[t_l128, t_o1f])
        t_d128a = op("scalar", lambda e: e.activation(out=d128_sb[:, :], in_=PSB[:, :], func=AF.Exp),
                    waits=[t_d128mm])
        t_d128 = op("scalar", lambda e: e.activation(out=scrs[:, :], in_=d128_sb[:, 0:1], func=AF.Copy))
        t_pb = [None] * NT
        t_c = [None] * NT
        t_cs = [None] * NT
        t_cadd = [None] * NT
        t_pwv = [None] * NT
        for i in range(NT):
            pbt = PSA if i % 2 == 0 else PSC
            pwt = PSB if i % 2 == 0 else PSD
            w = [t_buev[i // 4], t_cwt]
            if i >= 2:
                w.append(t_c[i - 2])
            if i in (0, 1):
                w.append(t_logev[NT - 1])
            t_pb[i] = mmg(
                [(pbt[:, 0:512], bu_sb[0:64, i * 128:(i + 1) * 128], cwt_sb[:, 0:512], True, True),
                 (pbt[:, 512:1024], bu_sb[0:64, i * 128:(i + 1) * 128], cwt_sb[:, 512:1024], True, True)],
                waits=w)
            if i == 0:
                t_pw = mmg(
                    [(pwt[:, 0:512], iotag_sb[:, 0:128], ldlog_sb[:, 0:512], True, True),
                     (pwt[:, 512:1024], iotag_sb[:, 0:128], ldlog_sb[:, 512:1024], True, True)],
                    waits=[t_iog, t_ldlog, t_d128])
                t_pwv0 = op("scalar", lambda e, p=pwt: e.activation(out=pws_a[:, :], in_=p[:, :], func=AF.Exp),
                              waits=[t_pw])
                t_pwv[i] = dma("sync", qs, pw_a[:, :], pws_a[:, :], waits=[t_pwv0])
            else:
                t_pwv[i] = op("vector", lambda e: e.tensor_mul(pw_a[:, :], pw_a[:, :], d128_sb[:, :]),
                              waits=[t_c[i - 1], t_d128])
            t_c[i] = op("vector", lambda e, p=pbt: e.tensor_mul(c_a[:, :], p[:, :], pw_a[:, :]),
                        waits=[t_pwv[i], t_pb[i]] + ([t_cs[i - 1]] if i >= 1 else []))
            t_cb = op("vector", lambda e: e.tensor_copy(out=scrv[:, :], in_=c_a[:, 0:1]))
            w = [t_cb, t_o128b]
            if i == 0:
                w.append(t_pwv[0])
            if i == 1:
                w.append(t_buev[NT // 4 - 1])
            if i >= 2:
                w.append(t_cadd[i - 2])
            t_cs[i] = mmg(
                [(pwt[0:1, 0:512], o128b_sb[:, :], c_a[:, 0:512], True, True),
                 (pwt[0:1, 512:1024], o128b_sb[:, :], c_a[:, 512:1024], True, True)],
                waits=w)
            cr = [colrow0, colrow1][i % 2]
            tcp = op("scalar", lambda e, p=pwt, cr=cr: e.copy(out=cr[:, :], in_=p[0:1, 0:1024]),
                     waits=[t_cs[i]] + ([t_cadd[i - 2]] if i >= 2 else []))
            t_cadd[i] = dma("scalar", qc, carrd[i:i + 1, :], cr[:, :], waits=[tcp])
        t_c32 = dma("sync", qs, carr32_sb[:, :], carrd[:, :], waits=[t_cadd[NT - 1], t_cadd[NT - 2]])
        t_cl = mmg([(PSB[0:1, 0:512], o128f_sb[0:32, 0:1], carr32_sb[:, 0:512], True, True),
                    (PSB[0:1, 512:1024], o128f_sb[0:32, 0:1], carr32_sb[:, 512:1024], True, True)],
                   waits=[t_c32, t_o128f, t_cs[NT - 1], t_cs[NT - 2]])
        t_carry_done = op("vector", lambda e: e.tensor_copy(out=carry_local[:, :], in_=PSB[0:1, 0:1024]),
                          waits=[t_cl, t_z3])

        t_crow = [None] * NT
        t_crowev = [None] * NT
        t_cntev = [None] * NT
        t_oh1 = [None] * NT
        t_oh2 = [None] * NT
        t_d12 = [None] * NT
        for i in range(NT):
            lg = logits_all[:, i * E:(i + 1) * E]
            tsrt = op("vector", lambda e, lg=lg: e.max(srt_sb[:, :], lg), waits=[t_logev[i]])
            t_oh1[i] = op("vector", lambda e, lg=lg, i=i:
                          e.tensor_scalar(oh1_all[:, i * E:(i + 1) * E], lg, srt_sb[:, 0:1], None, ALU.is_equal))
            t_oh2[i] = op("vector", lambda e, lg=lg, i=i:
                          e.tensor_scalar(oh2_all[:, i * E:(i + 1) * E], lg, srt_sb[:, 1:2], None, ALU.is_equal))
            t_d12[i] = op("vector", lambda e, i=i:
                          e.tensor_scalar(d12_all[:, i:i + 1], srt_sb[:, 0:1], srt_sb[:, 1:2], None, ALU.subtract))
            ts01 = op("vector", lambda e, i=i:
                      e.tensor_add(s01_sb[:, :], oh1_all[:, i * E:(i + 1) * E], oh2_all[:, i * E:(i + 1) * E]))
            ts01b = op("vector", lambda e: e.tensor_copy(out=scrv[:, :], in_=s01_sb[:, 0:1]))
            w = [ts01b, t_trikf, t_o1f, t_cadd[NT - 1]]
            if i >= 1:
                w += [t_crowev[i - 1], t_cntev[i - 1]]
            if i == 0:
                w.append(t_cadd[NT - 1])
            mlist = [(PSA[:, 0:8], trikf_sb[:, :], s01_sb[:, :], True, i == 0)]
            if i >= 1:
                mlist.append((PSA[:, 0:8], o1x128f_sb[:, :], crow_sb[:, :], False, True))
            t_cnt = mmg(mlist, waits=w)
            t_cntev[i] = op("vector", lambda e, i=i: e.tensor_copy(out=cnt_all[:, i * E:(i + 1) * E], in_=PSA[:, 0:8]),
                            waits=[t_cnt])
            t_crow[i] = mmg([(PSB[0:1, 0:8], o128f_sb[:, :], s01_sb[:, :], i == 0, True)],
                            waits=[ts01b, t_o128f] + ([t_cadd[NT - 1]] if i == 0 else [t_crowev[i - 1]]))
            t_crowev[i] = op("scalar", lambda e: e.copy(out=crow_sb[:, :], in_=PSB[0:1, 0:8]),
                             waits=[t_crow[i]])

        oh1_3 = oh1_all[:, :].rearrange("p (t e) -> p t e", e=E)
        oh2_3 = oh2_all[:, :].rearrange("p (t e) -> p t e", e=E)
        cnt_3 = cnt_all[:, :].rearrange("p (t e) -> p t e", e=E)
        tmp_3 = tmp256[:, :].rearrange("p (t e) -> p t e", e=E)
        op("vector", lambda e: e.tensor_mul(tmp_3, oh1_3, cnt_3), waits=[t_cntev[NT - 1]])
        t_r0 = op("vector", lambda e: e.reduce_sum(rank0_all[:, :].unsqueeze(2), tmp_3, axis=AXX))
        op("vector", lambda e: e.tensor_mul(tmp_3, oh2_3, cnt_3))
        t_r1 = op("vector", lambda e: e.reduce_sum(rank1_all[:, :].unsqueeze(2), tmp_3, axis=AXX))

        t_d12b = op("vector", lambda e: e.tensor_copy(out=scrv[:, :], in_=d12_all[:, NT - 1:NT]),
                    waits=[t_d12[NT - 1]])
        t_g1a = op("scalar", lambda e: e.activation(out=g1_all[:, :], in_=d12_all[:, :], func=AF.Sigmoid),
                   waits=[t_d12b])
        t_g1 = op("scalar", lambda e: e.activation(out=scrs[:, :], in_=g1_all[:, 0:1], func=AF.Copy))
        t_g2 = op("vector", lambda e: e.tensor_scalar(g2_all[:, :], g1_all[:, :], -1.0, 1.0, ALU.mult, ALU.add),
                  waits=[t_g1])

        eqrow_b = eqrow_sb[:, :].unsqueeze(1).broadcast_to([128, NT, E])
        t_dest = {}
        for (slot, oh_3x, rank_all_x, dall, pdall) in (
            (0, oh1_3, rank0_all, d20_all, pdest0_sb),
            (1, oh2_3, rank1_all, d21_all, pdest1_sb),
        ):
            rk = rank_all_x[:, :]
            op("vector", lambda e, oh_3x=oh_3x: e.tensor_mul(tmp_3, oh_3x, eqrow_b),
               waits=[t_eq, t_r0, t_r1])
            op("vector", lambda e: e.reduce_sum(t32a[:, :].unsqueeze(2), tmp_3, axis=AXX))
            op("vector", lambda e, rk=rk: e.tensor_add(t32b[:, :], t32a[:, :], rk))
            t_pd = op("vector", lambda e, pdall=pdall: e.tensor_copy(out=pdall[:, :], in_=t32b[:, :]))
            op("vector", lambda e, pdall=pdall: e.tensor_scalar(i32a[:, :], pdall[:, :], 7, None, ALU.arith_shift_right))
            op("vector", lambda e, rk=rk: e.tensor_copy(out=i32b[:, :], in_=rk))
            op("vector", lambda e: e.tensor_scalar(i32b[:, :], i32b[:, :], 127, None, ALU.bitwise_and))
            op("vector", lambda e: e.tensor_scalar(i32c[:, :], i32b[:, :], 6, None, ALU.logical_shift_left))
            op("vector", lambda e: e.tensor_scalar(i32b[:, :], i32b[:, :], 4, None, ALU.logical_shift_left))
            op("vector", lambda e: e.tensor_add(i32c[:, :], i32c[:, :], i32b[:, :]))
            t_d2 = op("vector", lambda e, dall=dall: e.tensor_add(dall[:, :], i32c[:, :], i32a[:, :]))
            t_dest[slot] = (t_d2, t_pd)
        t_last_meta = t_dest[1][0]

        t_tscat = []
        for i in range(NT):
            for (slot, dall) in ((0, d20_all), (1, d21_all)):
                t_tscat.append(idma(tidd, IndirectOffsetOnAxis(dall[:, i:i + 1], 1),
                                    tokid_sb[:, i:i + 1], None,
                                    waits=[t_dest[slot][0], t_tid0, t_tok]))
        t_tidld = dma("sync", qs, tidt_sb[:, :], tidd2d, waits=[t_tscat[-1]])

        tp1 = op("vector", lambda e: e.tensor_copy(out=pack_sb[:, 0:1024], in_=carry_local[:, :]),
                 waits=[t_carry_done])
        t_crowb = op("scalar", lambda e: e.activation(out=scrs[0:1, :], in_=crow_sb[:, 0:1], func=AF.Copy),
                     waits=[t_crowev[NT - 1]])
        tp2 = op("vector", lambda e: e.tensor_copy(out=pack_sb[:, 1024:1032], in_=crow_sb[:, :]),
                 waits=[t_crowb])
        t_ccw = dma("gpsimd", qg, ccin[:, :], pack_sb[:, :], waits=[tp2])
        cc.v += 1
        t_cc = cc.tok()

        def cc_fn(e, w=t_ccw, h=cc.h):
            e.wait_ge(w[0], w[1])
            e.collective_compute(
                "AllGather", ALU.bypass,
                replica_groups=[list(range(NCORE))],
                ins=[ccin.opt()], outs=[ccout.opt()],
            ).then_inc(h, 1)
        emit("gpsimd", cc_fn)
        t_ccr = dma("gpsimd", qg, gath_sb[:, :], ccout[:, :], waits=[t_cc])

        t_offs = mmg([(PSA[0:1, 0:8], rmask_sb[:, :], gath_sb[0:8, 1024:1032], True, True)],
                     waits=[t_ccr, t_rm, t_cntev[NT - 1]])
        t_offsev = op("vector", lambda e: e.tensor_copy(out=offsrow_sb[:, :], in_=PSA[0:1, 0:8]),
                      waits=[t_offs])
        t_offsevb = op("vector", lambda e: e.tensor_copy(out=scrv[0:1, :], in_=offsrow_sb[:, 0:1]))
        t_offsb = mmg([(PSA[:, 0:8], o1x128f_sb[:, :], offsrow_sb[:, :], True, True)],
                      waits=[t_offsevb])
        t_offsbev = op("vector", lambda e: e.tensor_copy(out=offsb_sb[:, :], in_=PSA[:, 0:8]),
                       waits=[t_offsb])
        t_c0 = mmg([(PSB[0:1, 0:512], cmask_sb[:, :], gath_sb[0:8, 0:512], True, True),
                    (PSB[0:1, 512:1024], cmask_sb[:, :], gath_sb[0:8, 512:1024], True, True)],
                   waits=[t_ccr, t_cm, t_crowev[NT - 1]])
        t_c0ev = op("vector", lambda e: e.tensor_copy(out=carry0_sb[:, :], in_=PSB[0:1, 0:1024]),
                    waits=[t_c0])
        t_c0evb = op("vector", lambda e: e.tensor_copy(out=scrv[0:1, :], in_=carry0_sb[:, 0:1]))
        t_carr = mmg([(PSC[0:32, 0:512], trikf_sb[0:32, 0:32], carr32_sb[:, 0:512], True, False),
                      (PSC[0:32, 0:512], o1x128f_sb[:, 0:32], carry0_sb[:, 0:512], False, True),
                      (PSC[0:32, 512:1024], trikf_sb[0:32, 0:32], carr32_sb[:, 512:1024], True, False),
                      (PSC[0:32, 512:1024], o1x128f_sb[:, 0:32], carry0_sb[:, 512:1024], False, True)],
                     waits=[t_c0evb, t_c[NT - 1]])
        t_carrev = op("vector", lambda e: e.tensor_copy(out=carrb32_sb[:, :], in_=PSC[0:32, 0:1024]),
                      waits=[t_carr])
        t_carrw = dma("scalar", qc, carrbd[:, :], carrb32_sb[:, :], waits=[t_carrev])

        offsb_b = offsb_sb[:, :].unsqueeze(1).broadcast_to([128, NT, E])
        t_gate = {}
        for (slot, oh_3x, rank_all_x, g_all_x, gate_all_x) in (
            (0, oh1_3, rank0_all, g1_all, gate0_all),
            (1, oh2_3, rank1_all, g2_all, gate1_all),
        ):
            rk = rank_all_x[:, :]
            op("vector", lambda e, oh_3x=oh_3x: e.tensor_mul(tmp_3, oh_3x, offsb_b),
               waits=[t_offsbev, t_g2])
            op("vector", lambda e: e.reduce_sum(t32a[:, :].unsqueeze(2), tmp_3, axis=AXX))
            op("vector", lambda e, rk=rk: e.tensor_add(t32b[:, :], t32a[:, :], rk))
            op("vector", lambda e: e.tensor_scalar(t32c[:, :], t32b[:, :], CAP - 0.5, None, ALU.is_lt))
            t_gate[slot] = op("vector", lambda e, g_all_x=g_all_x, gate_all_x=gate_all_x:
                              e.tensor_mul(gate_all_x[:, :], g_all_x[:, :], t32c[:, :]))

        if DEBUG:
            dma("sync", qs, dbg["logits"][:, :], logits_all[:, :], waits=[t_logev[NT - 1]])
            dma("sync", qs, dbg["carry"][:, :], carry_local[:, :], waits=[t_carry_done])
            dma("sync", qs, dbg["rank0"][:, :], rank0_all[:, :], waits=[t_r0])
            dma("sync", qs, dbg["rank1"][:, :], rank1_all[:, :], waits=[t_r1])
            dma("sync", qs, dbg["gate0"][:, :], gate0_all[:, :], waits=[t_gate[0]])
            dma("sync", qs, dbg["gate1"][:, :], gate1_all[:, :], waits=[t_gate[1]])
            dma("sync", qs, dbg["pdest0"][:, :], pdest0_sb[:, :], waits=[t_last_meta])
            dma("sync", qs, dbg["pdest1"][:, :], pdest1_sb[:, :], waits=[t_last_meta])
            dma("sync", qs, dbg["tidT"][:, :], tidt_sb[:, :], waits=[t_tidld])

        t_b0_psA_free = t_offsbev
        t_b0_psB_free = t_c0ev
        t_logits_psC_free = t_c[NT - 1]
        t_bu_psD_free = t_cadd[NT - 1]

    a_ctx.close()

    b_ctx = contextlib.ExitStack()
    w1p0 = b_ctx.enter_context(nc.sbuf_tensor("w1p0", [128, DC * 512], BF16))
    w1p1 = b_ctx.enter_context(nc.sbuf_tensor("w1p1", [128, DC * 512], BF16))
    w2_sb = b_ctx.enter_context(nc.sbuf_tensor("w2_sb", [128, HC * D], BF16))
    hT = b_ctx.enter_context(nc.sbuf_tensor("hT", [128, HC * HQ], BF16))
    packT = b_ctx.enter_context(nc.sbuf_tensor("packT", [128, DC * Q], BF16))
    rows0 = b_ctx.enter_context(nc.sbuf_tensor("rows0", [128, D], BF16))
    rows1 = b_ctx.enter_context(nc.sbuf_tensor("rows1", [128, D], BF16))
    rows2 = b_ctx.enter_context(nc.sbuf_tensor("rows2", [128, D], BF16))
    rows3 = b_ctx.enter_context(nc.sbuf_tensor("rows3", [128, D], BF16))
    ysb0 = b_ctx.enter_context(nc.sbuf_tensor("ysb0", [128, D], BF16))
    ysb1 = b_ctx.enter_context(nc.sbuf_tensor("ysb1", [128, D], BF16))
    b1t_sb = b_ctx.enter_context(nc.sbuf_tensor("b1t_sb", [128, E * HC], F32))
    b2_sb = b_ctx.enter_context(nc.sbuf_tensor("b2_sb", [1, 2 * D], BF16))
    w1ps = [w1p0, w1p1]
    rowss = [rows0, rows1, rows2, rows3]
    ysbs = [ysb0, ysb1]

    t_b1t = dma("sync", qs, b1t_sb[:, :], b1t_in[:, :], waits=[t_tidld])
    t_b2_e = [None] * E

    RCH = [(0, 512), (512, 128)]

    pydr = pyd.rearrange("(r p) d -> r p d", p=128)
    packdrs = [packd0.rearrange("(r p) d -> r p d", p=128),
               packd1.rearrange("(r p) d -> r p d", p=128)]
    packds = [packd0, packd1]

    t_packw = {}
    t_gelu_seq = []
    t_yev_all = []
    t_yw_all = []
    t_mm1_last = None
    t_mm2_last = None
    t_mm2_by_e = {}
    t_transp_done = None
    t_transp_prev = None
    t_hdump = None
    w1k = 0

    for e in range(E):
        for rb in range(RB):
            k = e * RB + rb
            war = [t_packw[k - 4]] if k >= 4 else []
            tg_ = idma(rowss[k % 4][:, :], None, xnd,
                       IndirectOffsetOnAxis(tidt_sb[:, e * RB + rb:e * RB + rb + 1], 0),
                       waits=[t_tidld] + war)
            w = [tg_]
            if e >= 2:
                w.append(t_transp_prev)
            t_packw[k] = dma("sync", qs, packdrs[e % 2][rb], rowss[k % 4][:, :], waits=w)
        tts = []
        for dc in range(DC):
            w = [t_packw[e * RB + RB - 1]]
            if e >= 1:
                w.append(t_mm1_last)
            tts.append(dma("sync", qt, packT[:, dc * Q:(dc + 1) * Q],
                           packds[e % 2][:, dc * 128:(dc + 1) * 128], waits=w, transpose=True))
        t_transp_prev = t_transp_done
        t_transp_done = tts[-1]
        if DEBUG2:
            dma("sync", qs, dbg["packT2"].rearrange("(e p) c -> e p c", p=128)[e],
                packT[:, :], waits=[t_transp_done])

        t_w2_done = None
        w2_pieces = []
        for kpc in range(8):
            w = [t_mm2_last] if (e >= 1 and kpc == 0) else []
            if e == 0 and kpc == 0:
                w = [t_tidld]
            w2_pieces.append((kpc, w))

        for half in range(2):
            t_w1p = [None] * 8
            pbase = w1k

            def issue_w1_piece(pp, e=e, pbase=pbase, t_w1p=t_w1p):
                w = []
                k = pbase + pp
                if k >= 2:
                    w.append(t_gelu_seq[(k - 1) * 4 - 1])
                t_w1p[pp] = dma("gpsimd", qw1[k % 2],
                                w1ps[k % 2][:, :].rearrange("p (dc hh) -> p dc hh", dc=DC),
                                w1_in[e][:, pp * 512:(pp + 1) * 512].rearrange("(dc p) hh -> p dc hh", p=128),
                                waits=w)

            def issue_w2_pieces(n, e=e):
                nonlocal_t = None
                for _ in range(n):
                    if w2_pieces:
                        kpc, wp = w2_pieces.pop(0)
                        nonlocal_t = dma("gpsimd", qw2,
                                         w2_sb[:, kpc * 4 * D:(kpc + 1) * 4 * D].rearrange("p (hc d) -> p hc d", hc=4),
                                         w2_in[e][kpc * 512:(kpc + 1) * 512, :].rearrange("(hc p) d -> p hc d", p=128),
                                         waits=wp)
                return nonlocal_t

            for hc in range(HC):
                pp = hc // 4
                hcl = hc % 4
                if hc == 0:
                    issue_w1_piece(0)
                    if half == 0:
                        tw2 = issue_w2_pieces(2)
                        t_w2_done = tw2 or t_w2_done
                    issue_w1_piece(1)
                    if half == 0:
                        tw2 = issue_w2_pieces(1)
                        t_w2_done = tw2 or t_w2_done
                elif hc % 4 == 0 and hc // 4 + 1 < 8:
                    issue_w1_piece(hc // 4 + 1)
                    if half == 0:
                        tw2 = issue_w2_pieces(1)
                        t_w2_done = tw2 or t_w2_done
                w1buf = w1ps[(pbase + pp) % 2]
                m1k = len(t_gelu_seq)
                pm1 = PSA if m1k % 2 == 0 else PSC
                w = [t_w1p[pp], t_transp_done]
                if m1k >= 2:
                    w.append(t_gelu_seq[m1k - 2])
                elif m1k == 0:
                    w += [t_b0_psA_free]
                elif m1k == 1:
                    w += [t_logits_psC_free]
                if hc == 0 and t_mm2_last is not None:
                    w.append(t_mm2_last)
                gro = half * HQ
                mms = []
                for dc in range(DC):
                    lhs = w1buf[:, dc * 512 + hcl * 128: dc * 512 + (hcl + 1) * 128]
                    mms.append((pm1[:, 0:512], lhs, packT[:, dc * Q + gro: dc * Q + gro + 512],
                                dc == 0, dc == DC - 1))
                    mms.append((pm1[:, 512:640], lhs, packT[:, dc * Q + gro + 512: dc * Q + gro + 640],
                                dc == 0, dc == DC - 1))
                tmm = mmg(mms, waits=w)
                wg_ = [tmm, t_b1t]
                if hc == 0 and t_mm2_last is not None:
                    wg_.append(t_mm2_last)
                if hc == 0 and t_hdump is not None:
                    wg_.append(t_hdump)
                tg = op("scalar",
                        lambda ee, pm1=pm1, hc=hc, e=e:
                        ee.activation(out=hT[:, hc * HQ: hc * HQ + HQ],
                                      in_=pm1[:, 0:HQ], func=AF.Gelu_apprx_tanh,
                                      bias=b1t_sb[:, e * HC + hc: e * HC + hc + 1]),
                        waits=wg_)
                t_gelu_seq.append(tg)
            w1k += 8
            t_mm1_last = t_gelu_seq[-1]
            if half == 0:
                wb2 = [t_tidld] if e == 0 else ([t_mm2_by_e[e - 2]] if e >= 2 else [])
                t_b2_e[e] = dma("gpsimd", qb2[e % 2], b2_sb[:, (e % 2) * D:(e % 2 + 1) * D],
                                b2_in[:, e * D:(e + 1) * D], waits=wb2)
            if DEBUG2:
                t_hdump = dma("sync", qs, dbg["hT2"].rearrange("(eh p) c -> eh p c", p=128)[e * 2 + half],
                              hT[:, :], waits=[t_mm1_last])

            for rbl in range(5):
                yk = len(t_yev_all)
                pm2 = PSB if yk % 2 == 0 else PSD
                grb = half * 5 + rbl
                w = []
                if rbl == 0:
                    w = [t_mm1_last, t_w2_done, t_b2_e[e], t_o1b]
                if yk >= 2:
                    w.append(t_yev_all[yk - 2])
                elif yk == 0:
                    w += [t_b0_psB_free]
                elif yk == 1:
                    w += [t_bu_psD_free]
                mms = []
                eo = (e % 2) * D
                for hc in range(HC):
                    lhs = hT[:, hc * HQ + rbl * 128: hc * HQ + (rbl + 1) * 128]
                    mms.append((pm2[:, 0:512], lhs, w2_sb[:, hc * D: hc * D + 512],
                                hc == 0, hc == HC - 1))
                    mms.append((pm2[:, 512:1024], lhs, w2_sb[:, hc * D + 512: (hc + 1) * D],
                                hc == 0, hc == HC - 1))
                    if hc == 0:
                        mms.append((pm2[:, 0:512], o1x128b_sb[:, :], b2_sb[:, eo: eo + 512],
                                    False, False))
                        mms.append((pm2[:, 512:1024], o1x128b_sb[:, :], b2_sb[:, eo + 512: eo + D],
                                    False, False))
                t_mm2_last = mmg(mms, waits=w)
                tv = op("vector",
                        lambda e_, pm2=pm2, yk=yk:
                        e_.tensor_copy(out=ysbs[yk % 2][:, :], in_=pm2[:, 0:1024]),
                        waits=[t_mm2_last] + ([t_yw_all[yk - 2]] if yk >= 2 else []))
                t_yev_all.append(tv)
                t_yw_all.append(dma("scalar", qc, pydr[e * RB + grb], ysbs[yk % 2][:, :], waits=[tv]))
        t_mm2_by_e[e] = t_mm2_last

    t_py_all = t_yw_all[-1]
    b_ctx.close()

    c_ctx = contextlib.ExitStack()
    if True:
        _c = c_ctx.enter_context
        xnT2 = _c(nc.sbuf_tensor("xnT2", [128, DC * NTOK], BF16))
        gwt_sb = _c(nc.sbuf_tensor("gwt_sb", [128, DC * D], BF16))
        gateb_sb = _c(nc.sbuf_tensor("gateb_sb", [1, D], BF16))
        pw_c = _c(nc.sbuf_tensor("pw_c", [128, D], F32))
        pws_c = _c(nc.sbuf_tensor("pws_c", [128, D], F32))
        ctile0 = _c(nc.sbuf_tensor("ctile0", [1, D], BF16))
        ctile1 = _c(nc.sbuf_tensor("ctile1", [1, D], BF16))
        d128c_sb = _c(nc.sbuf_tensor("d128c_sb", [128, D], F32))
        ld128c_sb = _c(nc.sbuf_tensor("ld128c_sb", [1, D], F32))
        c_c = _c(nc.sbuf_tensor("c_c", [128, D], BF16))
        gate_c0 = _c(nc.sbuf_tensor("gate_c0", [128, D], F32))
        gate_c1 = _c(nc.sbuf_tensor("gate_c1", [128, D], F32))
        sg2_c = _c(nc.sbuf_tensor("sg2_c", [128, D], F32))
        sg3_c = _c(nc.sbuf_tensor("sg3_c", [128, D], F32))
        s_c = _c(nc.sbuf_tensor("s_c", [128, D], F32))
        ya0 = _c(nc.sbuf_tensor("ya0", [128, D], BF16))
        ya1 = _c(nc.sbuf_tensor("ya1", [128, D], BF16))
        yb0 = _c(nc.sbuf_tensor("yb0", [128, D], BF16))
        yb1 = _c(nc.sbuf_tensor("yb1", [128, D], BF16))
        ysum_c = _c(nc.sbuf_tensor("ysum_c", [128, D], F32))
        xt0c = _c(nc.sbuf_tensor("xt0c", [128, D], F32))
        xt1c = _c(nc.sbuf_tensor("xt1c", [128, D], F32))
        osb0 = _c(nc.sbuf_tensor("osb0", [128, D], F32))
        osb1 = _c(nc.sbuf_tensor("osb1", [128, D], F32))
        yas = [ya0, ya1]
        ybs = [yb0, yb1]
        xtc = [xt0c, xt1c]
        osb = [osb0, osb1]
        xr = x_in.rearrange("(i p) d -> i p d", p=128)
        outr = out_io.rearrange("(i p) d -> i p d", p=128)

        t_bdrain = [t_py_all, t_mm2_last, t_yev_all[-1], t_gelu_seq[-1]]
        t_T2 = []
        for dc in range(DC):
            t_T2.append(dma("sync", qt, xnT2[:, dc * NTOK:(dc + 1) * NTOK],
                            xnd[:, dc * 128:(dc + 1) * 128],
                            waits=t_bdrain if dc == 0 else [], transpose=True))
        t_gwt = dma("gpsimd", qgw, gwt_sb[:, :].rearrange("p (dc e) -> p dc e", dc=DC),
                    gwt_in.rearrange("(dc p) e -> p dc e", p=128),
                    waits=t_bdrain)
        t_gateb = dma("gpsimd", qgw, gateb_sb[:, :], gateb_in[:, :], waits=[])

        def xnT2_tile(dc, i):
            return xnT2[:, dc * NTOK + i * 128: dc * NTOK + (i + 1) * 128]

        t_l128c = op("vector", lambda e: e.tensor_scalar(ld128c_sb[:, :], ldlog_sb[:, :], 128.0, None, ALU.mult))
        t_l128cb = op("vector", lambda e: e.tensor_copy(out=scrv[0:1, :], in_=ld128c_sb[:, 0:1]),
                      waits=[t_l128c])
        t_d128cmm = mmg([(PSB[:, 0:512], o1x128f_sb[:, :], ld128c_sb[:, 0:512], True, True),
                         (PSB[:, 512:1024], o1x128f_sb[:, :], ld128c_sb[:, 512:1024], True, True)],
                        waits=[t_l128cb] + t_bdrain + [t_yev_all[-1]])
        t_d128c0 = op("scalar", lambda e: e.activation(out=d128c_sb[:, :], in_=PSB[:, :], func=AF.Exp),
                      waits=[t_d128cmm])
        t_d128c = op("scalar", lambda e: e.activation(out=scrs[:, :], in_=d128c_sb[:, 0:1], func=AF.Copy))

        ctiles = [ctile0, ctile1]
        t_ct = [None] * NT
        t_h2 = [None] * NT
        t_xl = [None] * NT
        t_out = [None] * NT
        t_oa = [None] * NT
        t_sgm = [None] * NT
        t_exp2 = [None] * NT
        t_c2 = [None] * NT
        t_s2 = [None] * NT
        t_ct[0] = dma("sync", qs, ctiles[0][:, :], carrbd[0:1, :], waits=[t_carrw] + t_bdrain)
        t_ct[1] = dma("sync", qs, ctiles[1][:, :], carrbd[1:2, :], waits=[])
        t_ysum = [None] * NT
        t_sil = [None] * NT
        t_sgmul = [None] * NT

        t_xl[0] = dma("sync", qs, xtc[0][:, :], xr[0], waits=t_bdrain)
        t_xl[1] = dma("sync", qs, xtc[1][:, :], xr[1])

        gate_cs = [gate_c0, gate_c1]
        t_gmm_c = [None] * NT
        t_tyb = [None] * NT

        def emit_front(i):
            par = i % 2
            pbt = PSA if par == 0 else PSC
            pwt = PSB if par == 0 else PSD
            w = [t_cwt]
            if i >= 2:
                w += [t_c2[i - 2], t_s2[i - 2]]
            if i < 2:
                w += t_bdrain + [t_yev_all[-1]]
            t_pb2 = mmg(
                [(pbt[:, 0:512], bu_sb[0:64, i * 128:(i + 1) * 128], cwt_sb[:, 0:512], True, True),
                 (pbt[:, 512:1024], bu_sb[0:64, i * 128:(i + 1) * 128], cwt_sb[:, 512:1024], True, True)],
                waits=w)
            if i == 0:
                t_pw2 = mmg(
                    [(pwt[:, 0:512], iotag_sb[:, 0:128], ldlog_sb[:, 0:512], True, True),
                     (pwt[:, 512:1024], iotag_sb[:, 0:128], ldlog_sb[:, 512:1024], True, True)],
                    waits=[t_iog, t_ldlog, t_d128c])
                t_exp0 = op("scalar", lambda e, p=pwt: e.activation(out=pws_c[:, :], in_=p[:, :], func=AF.Exp),
                               waits=[t_pw2])
                t_exp2[i] = dma("sync", qs, pw_c[:, :], pws_c[:, :], waits=[t_exp0])
            else:
                t_exp2[i] = op("vector", lambda e: e.tensor_mul(pw_c[:, :], pw_c[:, :], d128c_sb[:, :]),
                               waits=[t_c2[i - 1], t_d128c])
            t_c2[i] = op("vector", lambda e, p=pbt: e.tensor_mul(c_c[:, :], p[:, :], pw_c[:, :]),
                         waits=[t_exp2[i], t_pb2] + ([t_h2[i - 1]] if i >= 1 else []))
            ctb = ctiles[i % 2]
            t_c2b = op("vector", lambda e: e.tensor_copy(out=scrv[:, :], in_=c_c[:, 0:1]))
            t_h2[i] = mmg(
                [(pbt[:, 0:512], o1x128b_sb[:, :], ctb[:, 0:512], True, False),
                 (pbt[:, 0:512], trikb_sb[:, :], c_c[:, 0:512], False, True),
                 (pbt[:, 512:1024], o1x128b_sb[:, :], ctb[:, 512:1024], True, False),
                 (pbt[:, 512:1024], trikb_sb[:, :], c_c[:, 512:1024], False, True)],
                waits=[t_c2b, t_trikb, t_o1b, t_ct[i]])
            if i + 2 < NT:
                t_ct[i + 2] = dma("sync", qs, ctb[:, :], carrbd[i + 2:i + 3, :], waits=[t_h2[i]])
            mms = []
            for dc in range(DC):
                mms.append((pwt[:, 0:512], xnT2_tile(dc, i), gwt_sb[:, dc * D: dc * D + 512],
                            dc == 0, dc == DC - 1))
                mms.append((pwt[:, 512:1024], xnT2_tile(dc, i), gwt_sb[:, dc * D + 512: (dc + 1) * D],
                            dc == 0, dc == DC - 1))
                if dc == 0:
                    mms.append((pwt[:, 0:512], o1x128b_sb[:, :], gateb_sb[:, 0:512], False, False))
                    mms.append((pwt[:, 512:1024], o1x128b_sb[:, :], gateb_sb[:, 512:1024], False, False))
            w = [t_gwt, t_gateb, t_o1b]
            if i >= 2:
                w.append(t_sgm[i - 2])
            if i < 2:
                w.append(t_yev_all[-1])
            if i == 0:
                w.append(t_exp2[0])
            t_gmm = mmg(mms, waits=w + (list(t_T2) if i == 0 else []))
            t_gmm_c[i] = t_gmm

        def emit_tail(i):
            par = i % 2
            pbt = PSA if par == 0 else PSC
            pwt = PSB if par == 0 else PSD
            t_sgm0 = op("scalar", lambda e, p=pwt, i=i: e.activation(out=gate_cs[i % 2][:, :], in_=p[:, :], func=AF.Sigmoid),
                        waits=[t_gmm_c[i]] + ([t_s2[i - 2]] if i >= 2 else []))
            t_sgm[i] = op("scalar", lambda e, i=i: e.activation(out=scrs[:, :], in_=gate_cs[i % 2][:, 0:1], func=AF.Copy))
            t_s2[i] = op("vector", lambda e, p=pbt, i=i: e.tensor_mul(s_c[:, :], p[:, :], gate_cs[i % 2][:, :]),
                         waits=[t_sgm[i], t_h2[i]] + ([t_sgmul[i - 1]] if i >= 1 else []))
            if DEBUG:
                dma("scalar", qc, dbg["s"].rearrange("(i p) d -> i p d", p=128)[i], s_c[:, :],
                    waits=[t_s2[i]])
            w = [t_py_all]
            if i >= 2:
                w.append(t_ysum[i - 2])
            t_ya = idma(yas[par][:, :], None, pyd,
                        IndirectOffsetOnAxis(pdest0_sb[:, i:i + 1], 0), waits=w)
            t_yb = idma(ybs[par][:, :], None, pyd,
                        IndirectOffsetOnAxis(pdest1_sb[:, i:i + 1], 0), waits=[])
            tyb = op("vector", lambda e, par=par, i=i: e.tensor_scalar(sg2_c[:, :], ybs[par][:, :],
                                                                        gate1_all[:, i:i + 1], None, ALU.mult),
                     waits=[t_yb, t_gate[1]] + ([t_ysum[i - 1]] if i >= 1 else []))
            t_ysum[i] = op("vector", lambda e, par=par, i=i: e.scalar_tensor_tensor(
                               ysum_c[:, :], yas[par][:, :], gate0_all[:, i:i + 1], sg2_c[:, :],
                               ALU.mult, ALU.add),
                           waits=[t_ya, tyb, t_gate[0]] + ([t_sil[i - 1]] if i >= 1 else []))
            tsg2a = op("scalar", lambda e: e.activation(out=sg3_c[:, :], in_=ysum_c[:, :], func=AF.Sigmoid),
                       waits=[t_ysum[i]] + ([t_sil[i - 1]] if i >= 1 else []))
            tsg2 = op("scalar", lambda e: e.activation(out=scrs[:, :], in_=sg3_c[:, 0:1], func=AF.Copy))
            t_sil[i] = op("vector", lambda e: e.tensor_mul(ysum_c[:, :], ysum_c[:, :], sg3_c[:, :]),
                          waits=[tsg2])
            if DEBUG:
                dma("scalar", qc, dbg["g"].rearrange("(i p) d -> i p d", p=128)[i], ysum_c[:, :],
                    waits=[t_sil[i]])
            t_sgmul[i] = op("vector", lambda e: e.tensor_mul(s_c[:, :], s_c[:, :], ysum_c[:, :]),
                            waits=[t_sil[i], t_s2[i]])
            t_oa[i] = op("vector", lambda e, par=par: e.tensor_add(osb[par][:, :], s_c[:, :], xtc[par][:, :]),
                         waits=[t_sgmul[i], t_xl[i]] + ([t_out[i - 2]] if i >= 2 else []))
            t_out[i] = dma("sync", qs, outr[i], osb[par][:, :], waits=[t_oa[i]])
            if i + 2 < NT:
                t_xl[i + 2] = dma("sync", qs, xtc[par][:, :], xr[i + 2], waits=[t_oa[i]])

        emit_front(0)
        emit_front(1)
        for i in range(1, NT):
            emit_tail(i - 1)
            if i + 1 < NT:
                emit_front(i + 1)
        emit_tail(NT - 1)

        if DEBUG2:
            dma("sync", qs, dbg["log2"][:, :], logits_all[:, :], waits=[t_py_all])
            dma("sync", qs, dbg["bu2"][:, :], bu_sb[:, :], waits=[t_py_all])
            dma("sync", qs, dbg["pyd2"][:, :], pyd[:, :], waits=[t_py_all])
            dma("sync", qs, dbg["xnd2"][:, :], xnd[:, :], waits=[t_py_all])
            dma("sync", qs, dbg["tidd2"][:, :], tidd[:, :], waits=[t_py_all])
            dma("sync", qs, dbg["carrbd2"][:, :], carrbd[:, :], waits=[t_py_all])
        final_qs = qs.tok()

        def f_final(e, tok=final_qs):
            e.wait_ge(tok[0], tok[1])
        emit("sync", f_final)
    c_ctx.close()

    with nc.Block() as block:
        @block.sync
        def _(e):
            for f in prog["sync"]:
                f(e)

        @block.scalar
        def _(e):
            for f in prog["scalar"]:
                f(e)

        @block.vector
        def _(e):
            for f in prog["vector"]:
                f(e)

        @block.tensor
        def _(e):
            for f in prog["tensor"]:
                f(e)

        @block.gpsimd
        def _(e):
            for f in prog["gpsimd"]:
                f(e)

    stack.close()
    return nc


def _consts():
    bf = ml_dtypes.bfloat16
    c = {}
    c["trikf"] = np.triu(np.ones((128, 128), np.float32), 1)
    c["trikb"] = c["trikf"].astype(bf)
    c["o128f"] = np.ones((128, 1), np.float32)
    c["o128b"] = c["o128f"].astype(bf)
    c["o1x128f"] = np.ones((1, 128), np.float32)
    c["o1x128b"] = c["o1x128f"].astype(bf)
    c["eqrow"] = np.ascontiguousarray(np.tile((np.arange(E) * Q).astype(np.float32), (128, 1)))
    c["e10row"] = np.ascontiguousarray(np.tile((np.arange(E) * RB).astype(np.float32), (128, 1)))
    c["tokid"] = np.ascontiguousarray(np.arange(NTOK, dtype=np.int32).reshape(NT, 128).T)
    return c


def _host_prep(L0_attn_x, L0_attn_B_w, L0_attn_C_w, L0_attn_gate_w, L0_attn_gate_b,
               L0_attn_log_decay, L0_ffn_Wg, L0_ffn_W1, L0_ffn_b1, L0_ffn_W2, L0_ffn_b2):
    x = np.ascontiguousarray(np.asarray(L0_attn_x, dtype=np.float32)).reshape(NCORE * NTOK, D)
    shared = {
        "w1": np.ascontiguousarray(np.asarray(L0_ffn_W1, dtype=np.float32)),
        "w2": np.ascontiguousarray(np.asarray(L0_ffn_W2, dtype=np.float32)),
        "wg": np.ascontiguousarray(np.asarray(L0_ffn_Wg, dtype=np.float32)),
        "bwt": np.ascontiguousarray(np.asarray(L0_attn_B_w, dtype=np.float32).T),
        "cwt": np.ascontiguousarray(np.asarray(L0_attn_C_w, dtype=np.float32).T),
        "gwt": np.ascontiguousarray(np.asarray(L0_attn_gate_w, dtype=np.float32).T),
        "gateb": np.asarray(L0_attn_gate_b, dtype=np.float32).reshape(1, D),
        "ldec": np.asarray(L0_attn_log_decay, dtype=np.float32).reshape(1, D),
        "b1t": np.ascontiguousarray(
            np.asarray(L0_ffn_b1, dtype=np.float32).reshape(E, HC, 128).transpose(2, 0, 1).reshape(128, E * HC)),
        "b2": np.ascontiguousarray(np.asarray(L0_ffn_b2, dtype=np.float32).reshape(1, E * D)),
    }
    shared.update(_consts())
    in_maps = []
    for c in range(NCORE):
        m = dict(shared)
        m["x"] = np.ascontiguousarray(x[c * NTOK:(c + 1) * NTOK])
        m["iotag"] = ((c % 2) * NTOK + np.arange(128, dtype=np.float32)).reshape(1, 128)
        m["rmask"] = (np.arange(8) < c).astype(np.float32).reshape(8, 1)
        m["cmask"] = (np.eye(8, dtype=np.float32)[c - 1].reshape(8, 1)
                      if c % 2 == 1 else np.zeros((8, 1), np.float32))
        in_maps.append(m)
    return in_maps


def kernel(**inputs):
    in_maps = _host_prep(**inputs)
    nc = build_nc()
    res = run_bass_kernel_spmd(nc, in_maps, core_ids=list(range(NCORE)))
    out = np.concatenate([np.asarray(r["out"]) for r in res.results], axis=0)
    return out.reshape(4, 8192, 1024)
